# version 01
# baseline (speedup 1.0000x reference)
import numpy as np
import concourse.bass as bass
import concourse.mybir as mybir
from concourse.bass_utils import run_bass_kernel_spmd
from contextlib import ExitStack

F32 = mybir.dt.float32
BF16 = mybir.dt.bfloat16
AF = mybir.ActivationFunctionType
ALU = mybir.AluOpType
AX = mybir.AxisListType

B, T, D = 8, 2048, 1024
H = 16
DI, DS, DTR, DCONV = 2048, 16, 64, 4
NT, ND, NDI = T // 128, D // 128, DI // 128
EPS = 1e-5
N_SUB = 4
DEBUG_TAP = None


class Plan:
    ENG = ("sp", "pe", "dve", "act", "pool", "gq", "aq")
    SERIAL = False
    SERIAL_FROM = 10**9
    SERIAL_UNTIL = -1

    def __init__(self):
        self.steps = []
        self.waited = set()

    def last_per_engine(self):
        out = {}
        for sid, st in enumerate(self.steps):
            out[st[0]] = sid
        return out

    def add(self, engine, fn, waits=(), n_dma=0, hard=()):
        sid = len(self.steps)
        for w in tuple(waits) + tuple(hard):
            assert w is not None and 0 <= w < sid, (w, sid)
            self.waited.add(w)
        self.steps.append((engine, fn, tuple(waits), n_dma, tuple(hard)))
        return sid

    def dma(self, engine, pairs, waits=()):
        def fn(eng):
            return [eng.dma_start(out=d, in_=s) for d, s in pairs]
        return self.add(engine, fn, waits=waits, n_dma=len(pairs))

    def resolve(self):
        if self.SERIAL or self.SERIAL_UNTIL >= 0 or self.SERIAL_FROM < len(self.steps):
            self.waited = set(range(len(self.steps)))
        cnt = {e: 0 for e in self.ENG}
        self.sig = {}
        for sid, (e, fn, waits, n_dma, hard) in enumerate(self.steps):
            if sid in self.waited or n_dma:
                cnt[e] += 16 * n_dma if n_dma else 1
                self.sig[sid] = (e, cnt[e])

    def emit_for(self, eng, enames, sems):
        for sid, (e, fn, waits, n_dma, hard) in enumerate(self.steps):
            if e not in enames:
                continue
            need = {}
            if self.SERIAL or (self.SERIAL_FROM <= sid <= self.SERIAL_UNTIL):
                for w in range(sid):
                    we, wv = self.sig[w]
                    if self.steps[w][0] in enames and not self.steps[w][3]:
                        continue
                    if need.get(we, 0) < wv:
                        need[we] = wv
            for w in waits:
                we, wv = self.sig[w]
                if we in enames and not self.steps[w][3]:
                    continue
                if need.get(we, 0) < wv:
                    need[we] = wv
            for w in hard:
                we, wv = self.sig[w]
                if need.get(we, 0) < wv:
                    need[we] = wv
            for we in sorted(need):
                eng.wait_ge(sems[we], need[we])
            insts = fn(eng)
            if sid in self.waited or n_dma:
                if n_dma:
                    for di in insts:
                        di.then_inc(sems[e], 16)
                else:
                    last = insts[-1] if isinstance(insts, (list, tuple)) else insts
                    last.then_inc(sems[e], 1)


class Buf:
    __slots__ = ("t", "off", "words")

    def __init__(self, t, off, words):
        self.t, self.off, self.words = t, off, words

    def f32(self, c0=0, n=None):
        if n is None:
            n = self.words - c0
        return self.t[:, self.off + c0:self.off + c0 + n]

    def bf16(self, c0=0, n=None):
        if n is None:
            n = 2 * self.words - c0
        return self.t[:, self.off:self.off + self.words].bitcast(BF16)[:, c0:c0 + n]


class Arena:
    def __init__(self, tensor, words):
        self.t, self.words = tensor, words
        self.free = [(0, words)]
        self.peak = 0

    def alloc(self, words):
        words = (words + 1) & ~1
        for i, (off, sz) in enumerate(self.free):
            if sz >= words:
                if sz == words:
                    self.free.pop(i)
                else:
                    self.free[i] = (off + words, sz - words)
                self.peak = max(self.peak, self.words - sum(s for _, s in self.free))
                b = Buf(self.t, off, words)
                import traceback
                caller = traceback.extract_stack()[-2]
                self.live_dbg = getattr(self, "live_dbg", {})
                self.live_dbg[(off, words)] = caller.lineno
                return b
        raise MemoryError(f"arena OOM: want {words}, free {self.free}, live={sorted(getattr(self, 'live_dbg', {}).items())}")

    def release(self, *bufs):
        for b in bufs:
            self.live_dbg.pop((b.off, b.words), None)
            self.free.append((b.off, b.words))
        self.free.sort()
        out = []
        for o, s in self.free:
            if out and out[-1][0] + out[-1][1] == o:
                out[-1][1] += s
            else:
                out.append([o, s])
        self.free = [(o, s) for o, s in out]


def build_program():
    nc = bass.Bass("TRN2", target_bir_lowering=False)
    din = lambda name, shape, dt=F32: nc.dram_tensor(name, list(shape), dt, kind="ExternalInput").ap()

    x_h = din("x", (T, D))
    Wattn_h = din("W_attn", (D, 3 * D))
    battnqk_h = din("battn_qk", (128, 16))
    bvrow_h = din("bv_row", (1, D))
    Wproj_h = din("W_proj", (D, D))
    bprow_h = din("bproj_row", (1, D))
    negmask_h = din("negmask", (128, 128))
    trimask_h = din("trimask", (128, 128))
    eye_h = din("eye", (128, 128))
    WrT_h = din("WrT", (D, D))
    WkT_h = din("WkT", (D, D))
    WvT_h = din("WvT", (D, D))
    murcol_h = din("mur_col", (128, ND))
    mukcol_h = din("muk_col", (128, ND))
    inprojT_h = din("inprojT", (D, 2 * DI))
    convw_h = din("convw", (NDI, 128, DCONV))
    convb_h = din("convb", (NDI, 128, 1))
    Alog1_h = din("Alog1", (NDI, 128, DS))
    Alog2_h = din("Alog2", (NDI, 128, DS))
    D1_h = din("D1", (NDI, 128, 1))
    D2_h = din("D2", (NDI, 128, 1))
    xdtT1_h = din("xdtT1", (DI, DTR))
    xbcT1_h = din("xbcT1", (DI, 2 * DS))
    xdtT2_h = din("xdtT2", (DI, DTR))
    xbcT2_h = din("xbcT2", (DI, 2 * DS))
    dtT1_h = din("dtT1", (DTR, DI))
    dtT2_h = din("dtT2", (DTR, DI))
    dtb1_h = din("dtb1", (NDI, 128, 1))
    dtb2_h = din("dtb2", (NDI, 128, 1))
    sel_h = din("sel", (64, 2 * DS * 128))
    outprojT_h = din("outprojT", (DI, D))
    out_h = nc.dram_tensor("out", [T, D], F32, kind="ExternalOutput").ap()
    dbg_h = nc.dram_tensor("dbg", [128, 2048], F32, kind="ExternalOutput").ap()

    spill_x = nc.dram_tensor("spill_x", [NT, 128, D], F32, kind="Internal").ap()
    spill_ulo = nc.dram_tensor("spill_ulo", [NDI, 128, T], BF16, kind="Internal").ap()
    spill_uhi = nc.dram_tensor("spill_uhi", [NDI, 128, T], BF16, kind="Internal").ap()
    spill_sz = nc.dram_tensor("spill_sz", [NDI, 128, T], BF16, kind="Internal").ap()
    spill_yfin = nc.dram_tensor("spill_yfin", [NDI, 128, T], BF16, kind="Internal").ap()
    stash_ffnw = nc.dram_tensor("stash_ffnw", [3, D, D], BF16, kind="Internal").ap()
    mxfer = nc.dram_tensor("mxfer", [16, 128], BF16, kind="Internal").ap()
    rxfer = nc.dram_tensor("rxfer", [1, 1024], BF16, kind="Internal").ap()
    bxfer = nc.dram_tensor("bxfer", [32, 2048], BF16, kind="Internal").ap()

    P = Plan()
    with ExitStack() as ctx:
        part_words = nc.SBUF_PARTITION_SIZE_BYTES // 4 - 4608
        big = ctx.enter_context(nc.sbuf_tensor("arena", [128, part_words], F32))
        A = Arena(big, part_words)
        psA = ctx.enter_context(nc.psum_tensor("psA", [128, 2048], F32))
        psB = ctx.enter_context(nc.psum_tensor("psB", [128, 2048], F32))
        sems = {e: ctx.enter_context(nc.semaphore(f"s_{e}")) for e in Plan.ENG}
        block = ctx.enter_context(nc.Block())

        b_eye = A.alloc(64)
        b_eyef = A.alloc(128)
        b_negm = A.alloc(128)
        b_trim = A.alloc(64)
        b_ones2 = A.alloc(64)
        b_xres = A.alloc(NT * 1024)
        b_xn = A.alloc(NT * 512)
        b_col = A.alloc(64)
        b_cst = A.alloc(128)

        xres = lambda tt: b_xres.f32(tt * 1024, 1024)
        xn_t = lambda tt: b_xn.bf16(tt * 1024, 1024)
        eye_bf = b_eye.bf16(0, 128)
        ones2 = b_ones2.bf16(0, 128)[0:2, :]

        s_x = P.dma("sp", [(xres(tt), x_h[tt * 128:(tt + 1) * 128, :]) for tt in range(NT)])
        s_e32 = P.dma("sp", [(b_cst.f32(0, 128), eye_h[:, :]), (b_eyef.f32(0, 128), eye_h[:, :])])
        s_eye = P.add("dve", lambda v: v.tensor_copy(out=eye_bf, in_=b_cst.f32(0, 128)), waits=(s_e32,))
        s_n32 = P.dma("sp", [(b_negm.f32(0, 128), negmask_h[:, :])], waits=(s_eye,))
        s_t32 = P.dma("sp", [(b_cst.f32(0, 128), trimask_h[:, :])], waits=(s_eye,))
        s_tri = P.add("dve", lambda v: v.tensor_copy(out=b_trim.bf16(0, 128), in_=b_cst.f32(0, 128)), waits=(s_t32,))
        s_on2 = P.add("dve", lambda v: [v.memset(ones2, 1.0),
                                        v.memset(b_col.f32(30, 1), EPS),
                                        v.memset(b_col.f32(31, 1), 1.0)])
        s_base = P.add("dve", lambda v: v.engine_nop(), waits=(s_x, s_n32, s_tri, s_on2))

        def tap(name, ap, after_step):
            if DEBUG_TAP == name:
                n = ap.shape[-1]
                P.dma("gq", [(dbg_h[0:ap.shape[0], 0:n], ap)], waits=(after_step,))

        def emit_ln(waits):
            outs = []
            st6 = b_col.f32(0, 12)
            mv = b_col.f32(12, 2)
            rstd = b_col.f32(14, 1)
            bneg = b_col.f32(15, 1)
            lnt = b_col.f32(16, 1)
            prev = None
            for tt in range(NT):
                w = tuple(waits) if prev is None else (prev,)
                s1 = P.add("dve", lambda v, tt=tt: [
                    v.bn_stats(out=b_col.f32(0, 6), in_=xres(tt)[:, 0:512]),
                    v.bn_stats(out=b_col.f32(6, 6), in_=xres(tt)[:, 512:1024]),
                    v.bn_aggr(out=b_col.f32(18, 2), in_=b_col.f32(0, 6)),
                    v.bn_aggr(out=b_col.f32(24, 2), in_=b_col.f32(6, 6))], waits=w)
                s1b = P.add("pool", lambda g: [
                    g.tensor_add(out=b_col.f32(26, 2), in0=b_col.f32(18, 2), in1=b_col.f32(24, 2)),
                    g.tensor_sub(out=b_col.f32(28, 1), in0=b_col.f32(18, 1), in1=b_col.f32(24, 1)),
                    g.tensor_scalar(out=mv[:, 0:1], in0=b_col.f32(26, 1), scalar1=0.5, scalar2=None,
                                    op0=ALU.mult),
                    g.tensor_tensor(out=b_col.f32(29, 1), in0=b_col.f32(28, 1), in1=b_col.f32(28, 1),
                                    op=ALU.mult),
                    g.tensor_scalar(out=b_col.f32(29, 1), in0=b_col.f32(29, 1), scalar1=0.25, scalar2=None,
                                    op0=ALU.mult),
                    g.tensor_scalar(out=b_col.f32(17, 1), in0=b_col.f32(27, 1), scalar1=0.5, scalar2=None,
                                    op0=ALU.mult),
                    g.tensor_add(out=mv[:, 1:2], in0=b_col.f32(17, 1), in1=b_col.f32(29, 1))], waits=(s1,))
                s2 = P.add("act", lambda s: [s.activation(out=lnt, in_=mv[:, 1:2], func=AF.Ln, bias=b_col.f32(30, 1)[:, 0:1]),
                                             s.activation(out=rstd, in_=lnt, func=AF.Exp, scale=-0.5)],
                           waits=(s1b,))
                s2b = P.add("pool", lambda g: [
                    g.tensor_tensor(out=bneg, in0=mv[:, 0:1], in1=rstd, op=ALU.mult),
                    g.tensor_scalar(out=bneg, in0=bneg, scalar1=-1.0, scalar2=None, op0=ALU.mult)],
                    waits=(s2,))
                s3 = P.add("dve", lambda v, tt=tt: v.tensor_scalar(
                    out=xn_t(tt), in0=xres(tt), scalar1=rstd[:, 0:1],
                    scalar2=bneg[:, 0:1], op0=ALU.mult, op1=ALU.add), waits=(s2b,))
                prev = s3
                outs.append(s3)
            return outs

        dummy_tp = lambda t: t.transpose(out=psB[:, 1920:1984].bitcast(BF16)[:, 0:128],
                                         in_=eye_bf, identity=eye_bf)

        def barrier():
            lasts = [s for s in P.last_per_engine().values()]
            g = P.add("dve", lambda v: v.engine_nop(), waits=tuple(lasts))
            bsteps = {"dve": g}
            bsteps["pool"] = P.add("pool", lambda p: p.engine_nop(), waits=(g,))
            bsteps["act"] = P.add("act", lambda s: s.copy(out=b_col.f32(33, 1), in_=b_col.f32(31, 1)),
                                  waits=(g,))
            bsteps["pe"] = P.add("pe", lambda t: dummy_tp(t), waits=(g,))
            return [g, bsteps["pool"], bsteps["act"], bsteps["pe"]]

        def tok2fm(src_tile, dst_buf, waits):
            ps_bf = psB[:, 0:256].bitcast(BF16)
            last = None
            prev_ev = None
            for f in range(ND):
                for g in range(0, NT, 4):
                    w = list(waits) + ([prev_ev] if prev_ev else [])
                    s_tp = P.add("pe", lambda t, f=f, g=g: [
                        t.transpose(out=ps_bf[:, u * 128:(u + 1) * 128],
                                    in_=src_tile(g + u)[:, f * 128:(f + 1) * 128],
                                    identity=eye_bf) for u in range(4)] + [dummy_tp(t)], waits=tuple(w))
                    s_ev = P.add("dve", lambda v, f=f, g=g: v.tensor_copy(
                        out=dst_buf.bf16(f * 2048 + g * 128, 512), in_=ps_bf[:, 0:512]), waits=(s_tp,))
                    prev_ev = s_ev
                    last = s_ev
            return last

        def rowpair(dst_buf, src_row_h, row_stage, lo_stage, waits=()):
            sl = P.dma("sp", [(row_stage.f32(0, 1024)[0:1, :], src_row_h[0:1, :])], waits=waits)
            s_mk = P.add("dve", lambda v: [
                v.tensor_copy(out=dst_buf.bf16(0, 1024)[0:1, :], in_=row_stage.f32(0, 1024)[0:1, :]),
                v.tensor_sub(out=lo_stage.bf16(0, 1024)[0:1, :], in0=row_stage.f32(0, 1024)[0:1, :],
                             in1=dst_buf.bf16(0, 1024)[0:1, :])], waits=(sl,))
            s_lo0 = P.dma("gq", [(rxfer[0:1, :], lo_stage.bf16(0, 1024)[0:1, :])], waits=(s_mk,))
            s_lo = P.dma("gq", [(dst_buf.bf16(0, 1024)[1:2, :], rxfer[0:1, :])], waits=(s_lo0,))
            return s_lo

        def load_cast(dst_buf, dst_c0, src_ap, stage, nwords, waits, eng_c="dve"):
            sl = P.dma("sp", [(stage.f32(0, nwords), src_ap)], waits=waits)
            sc = P.add(eng_c, lambda v: v.tensor_copy(out=dst_buf.bf16(dst_c0, nwords), in_=stage.f32(0, nwords)),
                       waits=(sl,))
            return sc

        def emit_attn_a(ln_steps):
            b_xnfm = A.alloc(ND * 1024)
            xn_fm = lambda f, c0, n: b_xnfm.bf16(f * 2048 + c0, n)
            s_fm = tok2fm(xn_t, b_xnfm, tuple(ln_steps))
            tap("xn0", b_xn.f32(0, 512), ln_steps[-1])
            tap("xnfm0", b_xnfm.f32(0, 1024), s_fm)
            tap("xnfm7", b_xnfm.f32(7 * 1024, 1024), s_fm)

            mark("attn_a:vext")
            b_vext = A.alloc(NT * 520)
            vext = lambda tt: b_vext.bf16(tt * 1040, 1040)
            s_vi = P.add("dve", lambda v: v.memset(b_vext.bf16(0, NT * 1040), 0.0))
            s_vo = P.add("dve", lambda v: [
                v.memset(vext(tt).rearrange("p (h c) -> p h c", c=65)[:, :, 64:65], 1.0)
                for tt in range(NT)], waits=(s_vi,))

            b_wstage = A.alloc(1024)
            b_row = A.alloc(1024)
            b_lo = A.alloc(512)
            b_bvp = A.alloc(512)
            b_bpp = A.alloc(512)
            b_bqk = A.alloc(16)
            s_bv = rowpair(b_bvp, bvrow_h, b_row, b_lo)
            s_bp = rowpair(b_bpp, bprow_h, b_row, b_lo, waits=(s_bv,))
            s_bqk = P.dma("sp", [(b_bqk.f32(0, 16), battnqk_h[:, :])])

            b_wv = A.alloc(ND * 512)
            wv = lambda k: b_wv.bf16(k * 1024, 1024)
            prev = (s_fm,)
            for k in range(ND):
                prev = (load_cast(b_wv, k * 1024, Wattn_h[k * 128:(k + 1) * 128, 2048:3072],
                                  b_wstage, 1024, prev),)
            s_wv = prev[0]
            tap("wv01", b_wv.f32(0, 1024), s_wv)

            v_ev = []
            for tt in range(NT):
                pso = psA[:, (tt % 2) * 1024:(tt % 2) * 1024 + 1024]
                w = [s_wv, s_bv, s_vo, s_fm]
                if tt >= 2:
                    w.append(v_ev[tt - 2])
                s_mm = P.add("pe", lambda t, tt=tt, pso=pso: [
                    t.matmul(out=pso[:, hf * 512:(hf + 1) * 512],
                             lhsT=(xn_fm(k, tt * 128, 128) if k < ND else ones2),
                             rhs=(wv(k)[:, hf * 512:(hf + 1) * 512] if k < ND
                                  else b_bvp.bf16(0, 1024)[0:2, hf * 512:(hf + 1) * 512]),
                             start=(k == 0), stop=(k == ND))
                    for hf in range(2) for k in range(ND + 1)], waits=tuple(w))
                s_ev = P.add("dve", lambda v, tt=tt, pso=pso: v.tensor_copy(
                    out=vext(tt).rearrange("p (h c) -> p h c", c=65)[:, :, 0:64],
                    in_=pso.rearrange("p (h c) -> p h c", c=64)), waits=(s_mm,))
                if tt == 0:
                    tap("vext0", b_vext.f32(0, 520), s_ev)
                v_ev.append(s_ev)

            mark("attn_a:heads")
            bar = barrier()
            A.release(b_wv, b_wstage, b_row, b_lo)
            b_qk = A.alloc(4 * 1024)
            b_wqk32 = A.alloc(2048)
            b_wqkbf = A.alloc(1024)
            b_ytok = b_xn
            b_attT = A.alloc(512)
            b_mm = A.alloc(16)
            b_mmbf = A.alloc(16)
            b_mst = A.alloc(64)
            b_mrow = A.alloc(1024)
            qk_q = lambda pb, c0, n: b_qk.bf16(pb * 4096 + c0, n)
            qk_k = lambda pb, c0, n: b_qk.bf16(pb * 4096 + 2048 + c0, n)
            y_tok = lambda tt, c0, n: b_ytok.bf16(tt * 1024 + c0, n)
            attT = lambda ab, c0, n: b_attT.bf16(ab * 512 + c0, n)
            negm = b_negm.f32(0, 128)
            trim = b_trim.bf16(0, 128)
            mrow = b_mrow.bf16(0, 2048)[0:1, :]

            psA_free = [None]
            last_stm_prev = [None]
            m_guard = [None]
            m_guard2 = [None]
            qk_lastuse = {}
            wqk_lastcast = [None]
            last_head_done = [v_ev[-1]]
            psA_free[0] = v_ev[-1]

            for hp in range(H // 2):
                pb = hp % 2
                pq = [(b_wqk32.f32(c * 128, 128), Wattn_h[c * 128:(c + 1) * 128, hp * 128:(hp + 1) * 128])
                      for c in range(ND)]
                pk = [(b_wqk32.f32(1024 + c * 128, 128), Wattn_h[c * 128:(c + 1) * 128, D + hp * 128:D + (hp + 1) * 128])
                      for c in range(ND)]
                w = [s_bqk, v_ev[-1], s_bp, bar[0]]
                if wqk_lastcast[0] is not None:
                    w.append(wqk_lastcast[0])
                s_wl = P.dma("sp", pq + pk, waits=tuple(w))
                s_wc = P.add("dve", lambda v: v.tensor_copy(out=b_wqkbf.bf16(0, 2048), in_=b_wqk32.f32(0, 2048)),
                             waits=(s_wl,))
                wqk_lastcast[0] = s_wc
                wq = lambda c: b_wqkbf.bf16(c * 128, 128)
                wk = lambda c: b_wqkbf.bf16(1024 + c * 128, 128)

                w = [s_wc, s_fm]
                if pb in qk_lastuse:
                    w.append(qk_lastuse[pb])
                if psA_free[0] is not None:
                    w.append(psA_free[0])
                s_qmm = P.add("pe", lambda t: [
                    t.matmul(out=psA[:, q4 * 512:(q4 + 1) * 512], lhsT=wq(c), rhs=xn_fm(c, q4 * 512, 512),
                             start=(c == 0), stop=(c == ND - 1))
                    for q4 in range(4) for c in range(ND)], waits=tuple(w))
                s_qev = P.add("dve", lambda v, pb=pb, hp=hp: [
                    v.tensor_scalar(out=qk_q(pb, q4 * 512, 512), in0=psA[:, q4 * 512:(q4 + 1) * 512],
                                    scalar1=b_bqk.f32(hp, 1)[:, 0:1], scalar2=None, op0=ALU.add)
                    for q4 in range(4)], waits=(s_qmm,))
                s_kmm = P.add("pe", lambda t: [
                    t.matmul(out=psA[:, q4 * 512:(q4 + 1) * 512], lhsT=wk(c), rhs=xn_fm(c, q4 * 512, 512),
                             start=(c == 0), stop=(c == ND - 1))
                    for q4 in range(4) for c in range(ND)], waits=(s_qev,))
                s_kev = P.add("dve", lambda v, pb=pb, hp=hp: [
                    v.tensor_scalar(out=qk_k(pb, q4 * 512, 512), in0=psA[:, q4 * 512:(q4 + 1) * 512],
                                    scalar1=b_bqk.f32(8 + hp, 1)[:, 0:1], scalar2=None, op0=ALU.add)
                    for q4 in range(4)], waits=(s_kmm,))
                if hp == 0:
                    tap("q0", b_qk.f32(0, 1024), s_qev)
                    tap("k0", b_qk.f32(2048, 1024), s_kev)
                psA_free[0] = s_kev

                for half in range(2):
                    h = 2 * hp + half
                    e0 = 64 * half
                    prev_mx = None
                    for i in range(NT):
                        nk = (i + 1) * 128
                        w = [s_kev]
                        if prev_mx is not None:
                            w.append(prev_mx)
                        elif psA_free[0] is not None:
                            w.append(psA_free[0])
                        s_smm = P.add("pe", lambda t, i=i, nk=nk, pb=pb, e0=e0: [
                            t.matmul(out=psA[:, n0:min(nk, n0 + 512)],
                                     lhsT=qk_q(pb, i * 128, 128)[e0:e0 + 64, :],
                                     rhs=qk_k(pb, n0, min(nk, n0 + 512) - n0)[e0:e0 + 64, :],
                                     start=True, stop=True)
                            for n0 in range(0, nk, 512)], waits=tuple(w))
                        if i == 7:
                            tap(f"sc_h{h}", psA[:, 0:1024], s_smm)
                        w_mx = [s_smm]
                        if i == 0 and m_guard2[0] is not None:
                            w_mx.append(m_guard2[0])
                        s_mx = P.add("dve", lambda v, i=i, nk=nk: [
                            v.tensor_tensor(out=psA[:, i * 128:(i + 1) * 128],
                                            in0=psA[:, i * 128:(i + 1) * 128], in1=negm, op=ALU.add),
                            v.tensor_reduce(out=b_mm.f32(i, 1), in_=psA[:, 0:nk], axis=AX.X,
                                            op=ALU.max)], waits=tuple(w_mx))
                        prev_mx = s_mx
                    psA_free[0] = prev_mx
                    w_mn = [prev_mx]
                    if m_guard[0] is not None:
                        w_mn.append(m_guard[0])
                    s_mneg = P.add("pool", lambda g: g.tensor_scalar(
                        out=b_mmbf.f32(0, 16), in0=b_mm.f32(0, 16), scalar1=-1.0, scalar2=None,
                        op0=ALU.mult), waits=tuple(w_mn))
                    m_guard2[0] = s_mneg
                    ps_m = psB[:, 1536:1664]
                    s_mt = P.add("pe", lambda t: t.transpose(out=ps_m[0:16, 0:128], in_=b_mmbf.f32(0, 16),
                                                             identity=b_eyef.f32(0, 128)), waits=(s_mneg,))
                    s_ms = P.add("dve", lambda v: v.tensor_copy(out=b_mst.bf16(0, 128)[0:16, :],
                                                                in_=ps_m[0:16, 0:128]), waits=(s_mt,))
                    w_mr = [s_ms]
                    if last_stm_prev[0] is not None:
                        w_mr.append(last_stm_prev[0])
                    s_mr0 = P.dma("gq", [(mxfer[:, :], b_mst.bf16(0, 128)[0:16, :])], waits=tuple(w_mr))
                    m_guard[0] = s_mr0
                    s_mr = P.dma("gq", [(mrow, mxfer.rearrange("p f -> () (p f)"))], waits=(s_mr0,))

                    y_evs = []
                    exp_lastuse = {}
                    for i in range(NT):
                        psY = psB[:, 1040 + (i % 2) * 80:1040 + (i % 2) * 80 + 65]
                        n_grp = (i + 4) // 4
                        last_av = None
                        for gi, jg in enumerate(range(0, i + 1, 4)):
                            je = min(jg + 4, i + 1)
                            ab = gi % 2
                            pstile = psB[:, ab * 512:(ab + 1) * 512]
                            w = [s_mr]
                            if (ab in exp_lastuse):
                                w.append(exp_lastuse[ab])
                            s_stm = P.add("pe", lambda t, i=i, jg=jg, je=je, pstile=pstile, pb=pb, e0=e0: [
                                mm for j in range(jg, je) for mm in (
                                    t.matmul(out=pstile[:, (j - jg) * 128:(j - jg + 1) * 128],
                                             lhsT=qk_k(pb, j * 128, 128)[e0:e0 + 64, :],
                                             rhs=qk_q(pb, i * 128, 128)[e0:e0 + 64, :],
                                             start=True, stop=False),
                                    t.matmul(out=pstile[:, (j - jg) * 128:(j - jg + 1) * 128],
                                             lhsT=ones2[0:1, :],
                                             rhs=mrow[:, i * 128:(i + 1) * 128],
                                             start=False, stop=True))], waits=tuple(w))
                            s_ex = P.add("act", lambda s, jg=jg, je=je, ab=ab, pstile=pstile: s.activation(
                                out=attT(ab, 0, (je - jg) * 128), in_=pstile[:, 0:(je - jg) * 128],
                                func=AF.Exp, scale=0.125), waits=(s_stm,))
                            dep = s_ex
                            if je - 1 == i:
                                dep = P.add("dve", lambda v, i=i, jg=jg, ab=ab: v.tensor_tensor(
                                    out=attT(ab, (i - jg) * 128, 128), in0=attT(ab, (i - jg) * 128, 128),
                                    in1=trim, op=ALU.mult), waits=(s_ex,))
                            w2 = [dep]
                            if len(y_evs) >= 2:
                                w2.append(y_evs[i - 2])
                            last_stm_prev[0] = s_stm
                            s_av = P.add("pe", lambda t, i=i, jg=jg, je=je, ab=ab, psY=psY, h=h: [
                                t.matmul(out=psY, lhsT=attT(ab, (j - jg) * 128, 128),
                                         rhs=vext(j)[:, h * 65:h * 65 + 65],
                                         start=(j == 0), stop=(j == i)) for j in range(jg, je)],
                                waits=tuple(w2))
                            exp_lastuse[ab] = s_av
                            last_av = s_av
                        w3 = [last_av]
                        if len(y_evs) >= 1:
                            w3.append(y_evs[-1])
                        s_rc = P.add("dve", lambda v, psY=psY: v.reciprocal(
                            out=b_col.f32(22, 1), in_=psY[:, 64:65]), waits=tuple(w3))
                        s_ye = P.add("dve", lambda v, i=i, psY=psY, h=h: v.tensor_scalar(
                            out=y_tok(i, h * 64, 64), in0=psY[:, 0:64],
                            scalar1=b_col.f32(22, 1)[:, 0:1], scalar2=None, op0=ALU.mult),
                            waits=(), hard=(s_rc,))
                        y_evs.append(s_ye)
                    if h == 0:
                        tap("mrow0", b_mrow.f32(0, 1024), s_mr)
                        tap("mmat0", b_mm.f32(0, 16), s_mt)
                        tap("ytok0", b_xn.f32(0, 512), y_evs[-1])
                    qk_lastuse[pb] = y_evs[-1]
                    last_head_done[0] = y_evs[-1]
                    bar = barrier()

            mark("attn_a:proj")
            barrier()
            s_att_done = last_head_done[0]
            A.release(b_xnfm, b_vext, b_qk, b_wqk32, b_wqkbf, b_attT, b_mm, b_mmbf, b_mst, b_mrow)
            b_yfm = A.alloc(ND * 1024)
            y_fm = lambda f, c0, n: b_yfm.bf16(f * 2048 + c0, n)
            s_yfm = tok2fm(lambda tt: y_tok(tt, 0, 1024), b_yfm, (s_att_done,))

            b_wp = A.alloc(ND * 512)
            b_wps = A.alloc(1024)
            wp = lambda k: b_wp.bf16(k * 1024, 1024)
            prev = (s_att_done,)
            for k in range(ND):
                prev = (load_cast(b_wp, k * 1024, Wproj_h[k * 128:(k + 1) * 128, :], b_wps, 1024, prev),)
            s_wp = prev[0]
            res = []
            for tt in range(NT):
                pso = psA[:, (tt % 2) * 1024:(tt % 2) * 1024 + 1024]
                w = [s_wp, s_yfm, s_bp]
                if tt >= 2:
                    w.append(res[tt - 2])
                s_pm = P.add("pe", lambda t, tt=tt, pso=pso: [
                    t.matmul(out=pso[:, hf * 512:(hf + 1) * 512], lhsT=y_fm(k, tt * 128, 128),
                             rhs=wp(k)[:, hf * 512:(hf + 1) * 512], start=(k == 0), stop=False)
                    for k in range(ND) for hf in range(2)] + [
                    t.matmul(out=pso[:, hf * 512:(hf + 1) * 512], lhsT=ones2,
                             rhs=b_bpp.bf16(0, 1024)[0:2, hf * 512:(hf + 1) * 512],
                             start=False, stop=True) for hf in range(2)], waits=tuple(w))
                s_rs = P.add("dve", lambda v, tt=tt, pso=pso: v.tensor_tensor(
                    out=xres(tt), in0=pso[:, 0:1024], in1=xres(tt), op=ALU.add), waits=(s_pm,))
                if tt == 0:
                    tap("res0", b_xres.f32(0, 1024), s_rs)
                    tap("yfm0", b_yfm.f32(0, 1024), s_yfm)
                res.append(s_rs)
            A.release(b_bvp, b_bpp, b_bqk, b_yfm, b_wp, b_wps)
            return res

        def emit_ffn(ln_steps, which):
            mark(f"ffn{which}:start")
            tap(f"ffn{which}_xn0", b_xn.f32(0, 512), ln_steps[-1])
            b_mu = A.alloc(16)
            s_mul = P.dma("sp", [(b_mu.f32(0, ND), murcol_h[:, :]),
                                 (b_mu.f32(8, ND), mukcol_h[:, :])], waits=(ln_steps[-1],))
            mur = lambda f: b_mu.f32(0 + f, 1)
            muk = lambda f: b_mu.f32(8 + f, 1)

            b_xnfm2 = A.alloc(ND * 1024)
            s_fm2 = tok2fm(xn_t, b_xnfm2, tuple(ln_steps))
            xnf = lambda f, c0, n: b_xnfm2.bf16(f * 2048 + c0, n)
            b_xrfm = A.alloc(ND * 1024)
            b_xkfm = A.alloc(ND * 1024)
            b_mt = A.alloc(1024)
            xr_fm = lambda f, c0, n: b_xrfm.bf16(f * 2048 + c0, n)
            xk_fm = lambda f, c0, n: b_xkfm.bf16(f * 2048 + c0, n)
            mix_done = []
            for f in range(ND):
                s_mx1 = P.add("dve", lambda v, f=f: [
                    v.tensor_sub(out=b_mt.bf16(0, 2047), in0=xnf(f, 1, 2047), in1=xnf(f, 0, 2047)),
                    v.scalar_tensor_tensor(out=xr_fm(f, 1, 2047), in0=b_mt.bf16(0, 2047),
                                           scalar=mur(f)[:, 0:1], in1=xnf(f, 0, 2047),
                                           op0=ALU.mult, op1=ALU.add),
                    v.tensor_scalar(out=xr_fm(f, 0, 1), in0=xnf(f, 0, 1), scalar1=mur(f)[:, 0:1],
                                    scalar2=None, op0=ALU.mult),
                    v.scalar_tensor_tensor(out=xk_fm(f, 1, 2047), in0=b_mt.bf16(0, 2047),
                                           scalar=muk(f)[:, 0:1], in1=xnf(f, 0, 2047),
                                           op0=ALU.mult, op1=ALU.add),
                    v.tensor_scalar(out=xk_fm(f, 0, 1), in0=xnf(f, 0, 1), scalar1=muk(f)[:, 0:1],
                                    scalar2=None, op0=ALU.mult)],
                    waits=(s_fm2, s_mul) + ((mix_done[-1],) if mix_done else ()))
                mix_done.append(s_mx1)

            mark(f"ffn{which}:weights")
            barrier()
            A.release(b_xnfm2, b_mt)
            b_wf = A.alloc(ND * 512)
            b_wfs = A.alloc(1024)
            wf = lambda k: b_wf.bf16(k * 1024, 1024)

            def load_w(idx, src_h, waits):
                prev = tuple(waits)
                for k in range(ND):
                    if which == 0:
                        sc = load_cast(b_wf, k * 1024, src_h[k * 128:(k + 1) * 128, :], b_wfs, 1024, prev)
                        st = P.dma("gq", [(stash_ffnw[idx, k * 128:(k + 1) * 128, :], wf(k))], waits=(sc,))
                        prev = (sc, st)
                        last = st
                    else:
                        sl = P.dma("sp", [(wf(k), stash_ffnw[idx, k * 128:(k + 1) * 128, :])], waits=prev)
                        prev = (sl,)
                        last = sl
                return last

            r_tok = lambda tt, c0, n: b_xn.bf16(tt * 1024 + c0, n)
            lw = load_w(0, WrT_h, (mix_done[-1],))
            r_steps = []
            for tt in range(NT):
                pso = psA[:, (tt % 2) * 1024:(tt % 2) * 1024 + 1024]
                w = [lw, mix_done[-1]]
                if tt >= 2:
                    w.append(r_steps[tt - 2])
                s_rm = P.add("pe", lambda t, tt=tt, pso=pso: [
                    t.matmul(out=pso[:, hf * 512:(hf + 1) * 512], lhsT=xr_fm(k, tt * 128, 128),
                             rhs=wf(k)[:, hf * 512:(hf + 1) * 512], start=(k == 0), stop=(k == ND - 1))
                    for k in range(ND) for hf in range(2)], waits=tuple(w))
                s_rs = P.add("act", lambda s, tt=tt, pso=pso: s.activation(
                    out=r_tok(tt, 0, 1024), in_=pso[:, 0:1024], func=AF.Sigmoid), waits=(s_rm,))
                r_steps.append(s_rs)

            mark(f"ffn{which}:k")
            barrier()
            k2_fm = lambda f, c0, n: b_xrfm.bf16(f * 2048 + c0, n)
            tap(f"ffn{which}_r0", b_xn.f32(0, 512), r_steps[-1])
            tap(f"ffn{which}_xk0", b_xkfm.f32(0, 1024), r_steps[-1])
            tap(f"ffn{which}_xr0", b_xrfm.f32(0, 1024), mix_done[-1])
            b_krel = A.alloc(2048)
            lw2 = load_w(1, WkT_h, (r_steps[-1],))
            k_steps = []
            for m in range(ND):
                for toff in (0, 1024):
                    pso = psA[:, (len(k_steps) % 2) * 1024:(len(k_steps) % 2) * 1024 + 1024]
                    w = [lw2, r_steps[-1]]
                    if len(k_steps) >= 2:
                        w.append(k_steps[-2])
                    s_km = P.add("pe", lambda t, m=m, toff=toff, pso=pso: [
                        t.matmul(out=pso[:, q4 * 512:(q4 + 1) * 512], lhsT=wf(k)[:, m * 128:(m + 1) * 128],
                                 rhs=xk_fm(k, toff + q4 * 512, 512), start=(k == 0), stop=(k == ND - 1))
                        for k in range(ND) for q4 in range(2)], waits=tuple(w))
                    kp = len(k_steps) % 2
                    s_kr = P.add("dve", lambda v, pso=pso, kp=kp: v.tensor_scalar(
                        out=b_krel.f32(kp * 1024, 1024), in0=pso[:, 0:1024], scalar1=0.0, scalar2=None,
                        op0=ALU.max), waits=(s_km,))
                    s_sq = P.add("pool", lambda g, m=m, toff=toff, kp=kp: g.tensor_mul(
                        out=k2_fm(m, toff, 1024), in0=b_krel.f32(kp * 1024, 1024),
                        in1=b_krel.f32(kp * 1024, 1024)), waits=(s_kr,))
                    k_steps.append(s_sq)

            mark(f"ffn{which}:v")
            barrier()
            tap(f"ffn{which}_k20", b_xrfm.f32(0, 1024), k_steps[-1])
            b_g = A.alloc(1024)
            lw3 = load_w(2, WvT_h, (k_steps[-1],))
            res = []
            for tt in range(NT):
                pso = psA[:, (tt % 2) * 1024:(tt % 2) * 1024 + 1024]
                w = [lw3, k_steps[-1]]
                if tt >= 2:
                    w.append(res[tt - 2])
                s_vm = P.add("pe", lambda t, tt=tt, pso=pso: [
                    t.matmul(out=pso[:, hf * 512:(hf + 1) * 512], lhsT=k2_fm(k, tt * 128, 128),
                             rhs=wf(k)[:, hf * 512:(hf + 1) * 512], start=(k == 0), stop=(k == ND - 1))
                    for k in range(ND) for hf in range(2)], waits=tuple(w))
                s_g = P.add("dve", lambda v, tt=tt, pso=pso: [
                    v.tensor_tensor(out=b_g.f32(0, 1024), in0=pso[:, 0:1024], in1=r_tok(tt, 0, 1024),
                                    op=ALU.mult),
                    v.tensor_tensor(out=xres(tt), in0=b_g.f32(0, 1024), in1=xres(tt), op=ALU.add)],
                    waits=(s_vm,))
                res.append(s_g)
            A.release(b_mu, b_xrfm, b_xkfm, b_wf, b_wfs, b_krel, b_g)
            return res

        def emit_attn_b(ln_steps):
            b_xnfm = A.alloc(ND * 1024)
            xn_fm = lambda f, c0, n: b_xnfm.bf16(f * 2048 + c0, n)
            s_fm = tok2fm(xn_t, b_xnfm, tuple(ln_steps))
            s_spx = P.dma("gq", [(spill_x[tt, :, :], xres(tt)) for tt in range(NT)], waits=tuple(ln_steps))

            b_cs = A.alloc(NDI * 40)
            b_bs = A.alloc(64)
            cw = lambda dt, wi: b_cs.f32(dt * 40 + wi, 1)
            cb = lambda dt: b_cs.f32(dt * 40 + 4, 1)
            dtb = lambda dt, br: b_cs.f32(dt * 40 + 5 + br, 1)
            dD = lambda dt: b_cs.f32(dt * 40 + 7, 1)
            Aneg = lambda dt, br, n: b_cs.f32(dt * 40 + 8 + br * 16 + n, 1)
            prev = None
            for dt in range(NDI):
                pairs = [(b_cs.f32(dt * 40, 4), convw_h[dt, :, :]),
                         (b_cs.f32(dt * 40 + 4, 1), convb_h[dt, :, :]),
                         (b_bs.f32(0, 16), Alog1_h[dt, :, :]),
                         (b_bs.f32(16, 16), Alog2_h[dt, :, :]),
                         (b_cs.f32(dt * 40 + 5, 1), dtb1_h[dt, :, :]),
                         (b_cs.f32(dt * 40 + 6, 1), dtb2_h[dt, :, :]),
                         (b_bs.f32(32, 1), D1_h[dt, :, :]),
                         (b_bs.f32(33, 1), D2_h[dt, :, :])]
                sl = P.dma("sp", pairs, waits=(prev,) if prev else (ln_steps[-1],))
                s_e = P.add("act", lambda s, dt=dt: s.activation(
                    out=b_cs.f32(dt * 40 + 8, 32), in_=b_bs.f32(0, 32), func=AF.Exp), waits=(sl,))
                s_m = P.add("dve", lambda v, dt=dt: [
                    v.tensor_scalar(out=b_cs.f32(dt * 40 + 8, 32), in0=b_cs.f32(dt * 40 + 8, 32),
                                    scalar1=-1.0, scalar2=None, op0=ALU.mult),
                    v.tensor_sub(out=b_cs.f32(dt * 40 + 7, 1), in0=b_bs.f32(32, 1), in1=b_bs.f32(33, 1))],
                    waits=(s_e,))
                prev = s_m
            s_cs = prev

            b_selbf = A.alloc(2048)
            b_sel32 = A.alloc(4096)
            s_sl = P.dma("sp", [(b_sel32.f32(0, 4096)[0:64, :], sel_h[:, :])], waits=(ln_steps[-1],))
            s_sc = P.add("dve", lambda v: v.tensor_copy(out=b_selbf.bf16(0, 4096)[0:64, :],
                                                        in_=b_sel32.f32(0, 4096)[0:64, :]), waits=(s_sl,))
            A.release(b_sel32)
            selB = lambda n: b_selbf.bf16(0, 4096)[0:64, n * 128:(n + 1) * 128]
            selC = lambda n: b_selbf.bf16(0, 4096)[0:64, 2048 + n * 128:2048 + (n + 1) * 128]

            b_ipstage = Buf(big, b_xres.off, 1024)
            b_ipm = Buf(big, b_xres.off + 1024, 1024)
            b_xraw = Buf(big, b_xres.off + 2048, 4114)
            b_ctmp = Buf(big, b_xres.off + 6162, 4096)
            b_uf = Buf(big, b_xres.off + 10258, 2048)
            b_cfin = Buf(big, b_xn.off, 4096)
            b_ulo = Buf(big, b_xn.off + 4096, 2048)
            b_szt = Buf(big, b_xn.off + 6144, 2048)
            b_uhis = A.alloc(2048)
            ipm = lambda mp, k: b_ipm.bf16(mp * 1024 + k * 128, 128)

            xz_sp = []
            u_done = []
            prev_si = None
            prev_cast = None
            prev_mm = {}
            for m in range(2 * NDI):
                pb = m % 2
                pso = psA[:, pb * 1024:pb * 1024 + 1024]
                w = [s_fm, s_spx]
                if prev_cast is not None:
                    w.append(prev_cast)
                if (pb in prev_mm):
                    w.append(prev_mm[pb])
                s_ld = P.dma("sp", [(b_ipstage.f32(k * 128, 128),
                                     inprojT_h[k * 128:(k + 1) * 128, m * 128:(m + 1) * 128])
                                    for k in range(ND)], waits=tuple(w))
                s_ct = P.add("dve", lambda v, pb=pb: v.tensor_copy(
                    out=b_ipm.bf16(pb * 1024, 1024), in_=b_ipstage.f32(0, 1024)), waits=(s_ld,))
                prev_cast = s_ct
                halves = []
                for toff in (0, 1024):
                    w = [s_ct]
                    if len(xz_sp) >= 2:
                        w.append(xz_sp[-2])
                    if halves:
                        w.append(halves[-1])
                    s_mm = P.add("pe", lambda t, m=m, toff=toff, pso=pso, pb=pb: [
                        t.matmul(out=pso[:, q4 * 512:(q4 + 1) * 512], lhsT=ipm(pb, k),
                                 rhs=xn_fm(k, toff + q4 * 512, 512), start=(k == 0), stop=(k == ND - 1))
                        for k in range(ND) for q4 in range(2)], waits=tuple(w))
                    prev_mm[pb] = s_mm
                    if m < NDI:
                        s_ev = P.add("dve", lambda v, pb=pb, toff=toff, pso=pso: v.tensor_copy(
                            out=b_xraw.f32(pb * 2056 + 4 + toff, 1024), in_=pso[:, 0:1024]), waits=(s_mm,))
                    else:
                        s_ev = P.add("act", lambda s, pb=pb, toff=toff, pso=pso: s.activation(
                            out=b_szt.bf16(pb * 2048 + toff, 1024), in_=pso[:, 0:1024], func=AF.Silu),
                            waits=(s_mm,))
                    halves.append(s_ev)
                if m >= NDI:
                    dt_ = m - NDI
                    s_sp = P.dma("gq", [(spill_sz[dt_, :, :], b_szt.bf16(pb * 2048, 2048))], waits=(halves[-1],))
                    xz_sp.append(s_sp)
                    continue
                dt_ = m
                wcv = halves + [s_cs]
                if prev_si is not None:
                    wcv.append(prev_si)
                s_cv = P.add("dve", lambda v, dt_=dt_, pb=pb: [
                    v.memset(b_xraw.f32(pb * 2056, 4), 0.0),
                    v.tensor_scalar(out=b_ctmp.f32(0, 2048), in0=b_xraw.f32(pb * 2056 + 1, 2048),
                                    scalar1=cw(dt_, 0)[:, 0:1], scalar2=None, op0=ALU.mult),
                    v.scalar_tensor_tensor(out=b_ctmp.f32(2048, 2048), in0=b_xraw.f32(pb * 2056 + 2, 2048),
                                           scalar=cw(dt_, 1)[:, 0:1], in1=b_ctmp.f32(0, 2048),
                                           op0=ALU.mult, op1=ALU.add),
                    v.scalar_tensor_tensor(out=b_ctmp.f32(0, 2048), in0=b_xraw.f32(pb * 2056 + 3, 2048),
                                           scalar=cw(dt_, 2)[:, 0:1], in1=b_ctmp.f32(2048, 2048),
                                           op0=ALU.mult, op1=ALU.add),
                    v.scalar_tensor_tensor(out=b_cfin.f32(pb * 2048, 2048), in0=b_xraw.f32(pb * 2056 + 4, 2048),
                                           scalar=cw(dt_, 3)[:, 0:1], in1=b_ctmp.f32(0, 2048),
                                           op0=ALU.mult, op1=ALU.add)], waits=tuple(wcv))
                s_si = P.add("act", lambda s, dt_=dt_, pb=pb: s.activation(
                    out=b_uf.f32(0, 2048), in_=b_cfin.f32(pb * 2048, 2048), func=AF.Silu,
                    bias=cb(dt_)[:, 0:1]), waits=(s_cv,))
                prev_si = s_si
                w_us = [s_si]
                if len(u_done) >= 2:
                    w_us.append(xz_sp[-2])
                s_us = P.add("dve", lambda v, dt_=dt_, pb=pb: [
                    v.tensor_copy(out=b_uhis.bf16(pb * 2048, 2048), in_=b_uf.f32(0, 2048)),
                    v.tensor_sub(out=b_ulo.bf16(pb * 2048, 2048), in0=b_uf.f32(0, 2048),
                                 in1=b_uhis.bf16(pb * 2048, 2048))],
                    waits=tuple(w_us))
                s_ul = P.dma("gq", [(spill_ulo[dt_, :, :], b_ulo.bf16(pb * 2048, 2048)),
                                    (spill_uhi[dt_, :, :], b_uhis.bf16(pb * 2048, 2048))], waits=(s_us,))
                xz_sp.append(s_ul)
                u_done.append(s_us)
            A.release(b_xnfm)
            s_b1 = xz_sp[-1]

            barrier()
            b_xpw = Buf(big, b_xn.off, 768)
            b_xps = Buf(big, b_xn.off + 768, 96)
            b_bclo = Buf(big, b_xn.off + 1024, 2048)
            b_dtr = A.alloc(2 * 1024)
            b_bcs = A.alloc(2 * 1024)
            dtr = lambda br: b_dtr.bf16(br * 2048, 2048)[0:64, :]
            bcs = lambda br: b_bcs.bf16(br * 2048, 2048)[0:64, :]
            b_upx = A.alloc(2048)
            xp_done = []
            for br in range(2):
                xdt_h = xdtT1_h if br == 0 else xdtT2_h
                xbc_h = xbcT1_h if br == 0 else xbcT2_h
                prev = (u_done[-1], xz_sp[-1]) if br == 0 else (xp_done[-1],)
                for k in range(NDI):
                    sl = P.dma("sp", [(b_xps.f32(0, 64), xdt_h[k * 128:(k + 1) * 128, :]),
                                      (b_xps.f32(64, 32), xbc_h[k * 128:(k + 1) * 128, :])], waits=prev)
                    sc = P.add("dve", lambda v, k=k: v.tensor_copy(out=b_xpw.bf16(k * 96, 96),
                                                                   in_=b_xps.f32(0, 96)), waits=(sl,))
                    prev = (sc,)
                xpw = lambda k: b_xpw.bf16(k * 96, 96)
                mm_steps = []
                for k in range(NDI):
                    w = [prev[0], xz_sp[-1]]
                    if len(mm_steps) >= 2:
                        w.append(mm_steps[-2][0])
                    s_uk = P.dma("aq", [(b_upx.bf16((k % 2) * 2048, 2048), spill_uhi[k, :, :])],
                                 waits=tuple(w))
                    s_mm = P.add("pe", lambda t, k=k: [
                        t.matmul(out=psA[0:64, q4 * 512:(q4 + 1) * 512], lhsT=xpw(k)[:, 0:64],
                                 rhs=b_upx.bf16((k % 2) * 2048 + q4 * 512, 512),
                                 start=(k == 0), stop=(k == NDI - 1))
                        for q4 in range(4)] + [
                        t.matmul(out=psB[0:32, q4 * 512:(q4 + 1) * 512], lhsT=xpw(k)[:, 64:96],
                                 rhs=b_upx.bf16((k % 2) * 2048 + q4 * 512, 512),
                                 start=(k == 0), stop=(k == NDI - 1))
                        for q4 in range(4)], waits=(s_uk,) + ((mm_steps[-1][1],) if mm_steps else ()))
                    mm_steps.append((s_mm, s_mm))
                s_dt = mm_steps[-1][0]
                s_de = P.add("dve", lambda v, br=br: v.tensor_copy(out=dtr(br), in_=psA[0:64, 0:2048]),
                             waits=(s_dt,))
                s_bc = s_dt
                s_be = P.add("dve", lambda v, br=br: [
                    v.tensor_copy(out=bcs(br)[0:32, :], in_=psB[0:32, 0:2048]),
                    v.tensor_sub(out=b_bclo.bf16(0, 2048)[0:32, :], in0=psB[0:32, 0:2048],
                                 in1=bcs(br)[0:32, :])], waits=(s_bc,))
                s_bl0 = P.dma("gq", [(bxfer[:, :], b_bclo.bf16(0, 2048)[0:32, :])], waits=(s_be,))
                s_bl = P.dma("gq", [(b_bcs.bf16(br * 2048, 2048)[32:64, :], bxfer[:, :])], waits=(s_bl0,))
                xp_done.append(s_bl)


            b_dtw = A.alloc(2 * 1024)
            b_dws = Buf(big, b_xn.off + 3072, 2048)
            dtw = lambda br: b_dtw.bf16(br * 2048, 2048)[0:64, :]
            s_d1 = P.dma("sp", [(b_dws.f32(0, 2048)[0:64, :], dtT1_h[:, :])], waits=(xp_done[-1],))
            s_c1 = P.add("dve", lambda v: v.tensor_copy(out=dtw(0), in_=b_dws.f32(0, 2048)[0:64, :]),
                         waits=(s_d1,))
            s_d2 = P.dma("sp", [(b_dws.f32(0, 2048)[0:64, :], dtT2_h[:, :])], waits=(s_c1,))
            s_c2 = P.add("dve", lambda v: v.tensor_copy(out=dtw(1), in_=b_dws.f32(0, 2048)[0:64, :]),
                         waits=(s_d2,))
            s_dtw = s_c2

            barrier()
            b_dl = A.alloc(2048)
            b_du = A.alloc(2048)
            b_spt = A.alloc(2048)
            b_dA = Buf(big, b_xres.off, 4096)
            b_dBu = Buf(big, b_xres.off + 4096, 4096)
            b_h = Buf(big, b_xres.off + 8192, 4096)
            b_ypp = Buf(big, b_xres.off + 12288, 4096)
            b_hC = A.alloc(2 * 2048)
            b_up2 = A.alloc(2048)
            b_szt2 = Buf(big, b_xn.off, 2048)
            b_yft = Buf(big, b_xn.off + 2048, 2048)

            yfin_sp = []
            prev_tail = None
            scan_hist = {}
            dbu_hist = {}
            hc_hist = {}
            ya_hist = []
            for dt_ in range(NDI):
                pbt = dt_ % 2
                w = [s_b1, s_dtw]
                if prev_tail:
                    w.append(prev_tail)
                s_ulo = P.dma("sp", [(b_up2.bf16(0, 2048), spill_uhi[dt_, :, :]),
                                     (b_up2.bf16(2048, 2048), spill_ulo[dt_, :, :])], waits=tuple(w))
                s_szl = P.dma("aq", [(b_szt2.bf16(pbt * 2048, 2048), spill_sz[dt_, :, :])], waits=tuple(w))
                s_yi = P.add("dve", lambda v, dt_=dt_: [
                    v.tensor_scalar(out=b_ypp.f32(0, 2048), in0=b_up2.bf16(0, 2048), scalar1=dD(dt_)[:, 0:1],
                                    scalar2=None, op0=ALU.mult),
                    v.scalar_tensor_tensor(out=b_ypp.f32(0, 2048), in0=b_up2.bf16(2048, 2048),
                                           scalar=dD(dt_)[:, 0:1], in1=b_ypp.f32(0, 2048),
                                           op0=ALU.mult, op1=ALU.add)], waits=(s_ulo, s_cs))
                ycur = 0
                last_ya = s_yi
                for br in range(2):
                    w = [s_dtw, s_yi]
                    if dbu_hist:
                        w.append(max(dbu_hist.values()))
                    if prev_tail:
                        w.append(prev_tail)
                    s_dp = P.add("pe", lambda t, dt_=dt_, br=br: [
                        t.matmul(out=psA[:, q4 * 512:(q4 + 1) * 512],
                                 lhsT=dtw(br)[:, dt_ * 128:(dt_ + 1) * 128],
                                 rhs=dtr(br)[:, q4 * 512:(q4 + 1) * 512], start=True, stop=True)
                        for q4 in range(4)], waits=tuple(w))
                    s_sp2 = P.add("act", lambda s, dt_=dt_, br=br: [
                        s.activation(out=b_spt.f32(0, 2048), in_=psA[:, 0:2048], func=AF.Abs,
                                     bias=dtb(dt_, br)[:, 0:1]),
                        s.activation(out=b_spt.f32(0, 2048), in_=b_spt.f32(0, 2048), func=AF.Exp, scale=-1.0),
                        s.activation(out=b_spt.f32(0, 2048), in_=b_spt.f32(0, 2048), func=AF.Ln, bias=b_col.f32(31, 1)[:, 0:1]),
                        s.activation(out=b_dl.f32(0, 2048), in_=psA[:, 0:2048], func=AF.Relu,
                                     bias=dtb(dt_, br)[:, 0:1])], waits=(s_dp,))
                    s_df = P.add("dve", lambda v, dt_=dt_: [
                        v.tensor_add(out=b_dl.f32(0, 2048), in0=b_dl.f32(0, 2048), in1=b_spt.f32(0, 2048)),
                        v.scalar_tensor_tensor(out=b_du.f32(0, 2048), in0=b_up2.bf16(0, 2048), scalar=1.0,
                                               in1=b_dl.f32(0, 2048), op0=ALU.mult, op1=ALU.mult),
                        v.scalar_tensor_tensor(out=b_spt.f32(0, 2048), in0=b_up2.bf16(2048, 2048), scalar=1.0,
                                               in1=b_dl.f32(0, 2048), op0=ALU.mult, op1=ALU.mult),
                        v.tensor_add(out=b_du.f32(0, 2048), in0=b_du.f32(0, 2048), in1=b_spt.f32(0, 2048))],
                        waits=(s_sp2,))
                    for n in range(DS):
                        nb = n % 2
                        w = [s_df]
                        if nb in scan_hist:
                            w.append(scan_hist[nb])
                        s_dA = P.add("act", lambda s, dt_=dt_, br=br, n=n, nb=nb: s.activation(
                            out=b_dA.f32(nb * 2048, 2048), in_=b_dl.f32(0, 2048), func=AF.Exp,
                            scale=Aneg(dt_, br, n)[:, 0:1]), waits=tuple(w))
                        wb = [s_sc, s_df]
                        ob = 1 - nb
                        if ob in dbu_hist:
                            wb.append(dbu_hist[ob])
                        if ob in hc_hist:
                            wb.append(hc_hist[ob])
                        s_bb = P.add("pe", lambda t, br=br, n=n: [
                            t.matmul(out=psA[:, q4 * 512:(q4 + 1) * 512], lhsT=selB(n),
                                     rhs=bcs(br)[:, q4 * 512:(q4 + 1) * 512], start=True, stop=True)
                            for q4 in range(4)] + [
                            t.matmul(out=psB[:, q4 * 512:(q4 + 1) * 512], lhsT=selC(n),
                                     rhs=bcs(br)[:, q4 * 512:(q4 + 1) * 512], start=True, stop=True)
                            for q4 in range(4)], waits=tuple(wb))
                        sgn = 1.0 if br == 0 else -1.0
                        w = [s_bb, s_dA]
                        if nb in scan_hist:
                            w.append(scan_hist[nb])
                        if len(ya_hist) >= 2:
                            w.append(ya_hist[-2])
                        s_hc = P.add("dve", lambda v, nb=nb, sgn=sgn: [
                            v.tensor_tensor(out=b_dBu.f32(nb * 2048, 2048), in0=b_du.f32(0, 2048),
                                            in1=psA[:, 0:2048], op=ALU.mult),
                            v.tensor_tensor_scan(out=b_h.f32(nb * 2048, 2048),
                                                 data0=b_dA.f32(nb * 2048, 2048),
                                                 data1=b_dBu.f32(nb * 2048, 2048), initial=0.0,
                                                 op0=ALU.mult, op1=ALU.add),
                            v.scalar_tensor_tensor(out=b_hC.f32(nb * 2048, 2048),
                                                   in0=b_h.f32(nb * 2048, 2048), scalar=sgn,
                                                   in1=psB[:, 0:2048], op0=ALU.mult, op1=ALU.mult)],
                            waits=tuple(w))
                        dbu_hist[nb] = s_hc
                        scan_hist[nb] = s_hc
                        hc_hist[nb] = s_hc
                        s_ya = P.add("pool", lambda g, nb=nb, yc=ycur: g.tensor_add(
                            out=b_ypp.f32((1 - yc) * 2048, 2048), in0=b_ypp.f32(yc * 2048, 2048),
                            in1=b_hC.f32(nb * 2048, 2048)), waits=(s_hc, last_ya))
                        ycur = 1 - ycur
                        last_ya = s_ya
                        ya_hist.append(s_ya)
                s_gate = P.add("dve", lambda v, yc=ycur, pbt=pbt: v.tensor_tensor(
                    out=b_yft.bf16(pbt * 2048, 2048), in0=b_ypp.f32(yc * 2048, 2048),
                    in1=b_szt2.bf16(pbt * 2048, 2048), op=ALU.mult), waits=(last_ya, s_szl))
                s_ysp = P.dma("gq", [(spill_yfin[dt_, :, :], b_yft.bf16(pbt * 2048, 2048))], waits=(s_gate,))
                prev_tail = s_ysp
                yfin_sp.append(s_ysp)
                barrier()

            A.release(b_dl, b_du, b_spt, b_hC, b_up2, b_upx, b_uhis,
                      b_dtr, b_bcs, b_dtw, b_selbf, b_cs, b_bs)

            barrier()
            b_opw = A.alloc(NDI * 512)
            b_ops = A.alloc(1024)
            b_ysb = A.alloc(2 * 1024)
            b_xrl = A.alloc(2 * 1024)
            opw = lambda k: b_opw.bf16(k * 1024, 1024)
            prev = (yfin_sp[-1],)
            for k in range(NDI):
                prev = (load_cast(b_opw, k * 1024, outprojT_h[k * 128:(k + 1) * 128, :], b_ops, 1024, prev),)
            s_op = prev[0]
            res = []
            for tt in range(NT):
                pb = tt % 2
                w = [s_op]
                if tt >= 2:
                    w.append(res[tt - 2])
                s_yl = P.dma("sp", [(b_ysb.bf16(pb * 2048 + k * 128, 128), spill_yfin[k, :, tt * 128:(tt + 1) * 128])
                                    for k in range(NDI)], waits=tuple(w))
                s_xl = P.dma("aq", [(b_xrl.f32(pb * 1024, 1024), spill_x[tt, :, :])], waits=tuple(w))
                pso = psA[:, pb * 1024:pb * 1024 + 1024]
                s_om = P.add("pe", lambda t, pb=pb, pso=pso: [
                    t.matmul(out=pso[:, hf * 512:(hf + 1) * 512], lhsT=b_ysb.bf16(pb * 2048 + k * 128, 128),
                             rhs=opw(k)[:, hf * 512:(hf + 1) * 512], start=(k == 0), stop=(k == NDI - 1))
                    for k in range(NDI) for hf in range(2)], waits=(s_yl,))
                s_rs = P.add("dve", lambda v, tt=tt, pb=pb, pso=pso: v.tensor_tensor(
                    out=xres(tt), in0=pso[:, 0:1024], in1=b_xrl.f32(pb * 1024, 1024), op=ALU.add),
                    waits=(s_om, s_xl))
                res.append(s_rs)
            A.release(b_opw, b_ops, b_ysb, b_xrl)
            return res

        import sys as _sys
        def mark(nm):
            print(f"[mark] {nm}: step {len(P.steps)}", file=_sys.stderr)

        last = emit_attn_a(emit_ln((s_base,)))
        if N_SUB >= 2:
            barrier()
            last = emit_ffn(emit_ln((last[-1],)), which=0)
        if N_SUB >= 3:
            barrier()
            last = emit_attn_b(emit_ln((last[-1],)))
        if N_SUB >= 4:
            barrier()
            last = emit_ffn(emit_ln((last[-1],)), which=1)

        mark("store")
        s_st = P.dma("gq", [(out_h[tt * 128:(tt + 1) * 128, :], xres(tt)) for tt in range(NT)],
                     waits=(last[-1],))
        P.add("pool", lambda g: g.engine_nop(), waits=(s_st,))

        P.resolve()

        @block.sync
        def _(e):
            P.emit_for(e, {"sp"}, sems)

        @block.tensor
        def _(e):
            P.emit_for(e, {"pe"}, sems)

        @block.vector
        def _(e):
            P.emit_for(e, {"dve"}, sems)

        @block.scalar
        def _(e):
            P.emit_for(e, {"act", "aq"}, sems)

        @block.gpsimd
        def _(e):
            P.emit_for(e, {"pool", "gq"}, sems)

    return nc


def _prep_inputs(inputs):
    g = lambda k: np.asarray(inputs[k], dtype=np.float32)
    b_attn = g("L0_attn_a_b_attn")
    common = {
        "W_attn": np.ascontiguousarray(g("L0_attn_a_W_attn")),
        "battn_qk": np.ascontiguousarray(np.concatenate(
            [b_attn[0:D].reshape(ND, 128).T, b_attn[D:2 * D].reshape(ND, 128).T], axis=1)),
        "bv_row": np.ascontiguousarray(b_attn[2 * D:3 * D].reshape(1, D)),
        "W_proj": np.ascontiguousarray(g("L0_attn_a_W_proj")),
        "bproj_row": np.ascontiguousarray(g("L0_attn_a_b_proj").reshape(1, D)),
        "negmask": np.where(np.arange(128)[None, :] <= np.arange(128)[:, None], 0.0, -1e9).astype(np.float32),
        "trimask": (np.arange(128)[:, None] <= np.arange(128)[None, :]).astype(np.float32),
        "eye": np.eye(128, dtype=np.float32),
        "WrT": np.ascontiguousarray(g("L0_ffn_W_r").T),
        "WkT": np.ascontiguousarray(g("L0_ffn_W_k").T),
        "WvT": np.ascontiguousarray(g("L0_ffn_W_v").T),
        "mur_col": np.ascontiguousarray(g("L0_ffn_mu_r").reshape(ND, 128).T),
        "muk_col": np.ascontiguousarray(g("L0_ffn_mu_k").reshape(ND, 128).T),
        "inprojT": np.ascontiguousarray(g("L0_attn_b_in_proj_w").T),
        "convw": np.ascontiguousarray(g("L0_attn_b_conv_w").reshape(NDI, 128, DCONV)),
        "convb": np.ascontiguousarray(g("L0_attn_b_conv_b").reshape(NDI, 128, 1)),
        "Alog1": np.ascontiguousarray(g("L0_attn_b_A1_log").reshape(NDI, 128, DS)),
        "Alog2": np.ascontiguousarray(g("L0_attn_b_A2_log").reshape(NDI, 128, DS)),
        "D1": np.ascontiguousarray(g("L0_attn_b_D1").reshape(NDI, 128, 1)),
        "D2": np.ascontiguousarray(g("L0_attn_b_D2").reshape(NDI, 128, 1)),
        "xdtT1": np.ascontiguousarray(g("L0_attn_b_x1_proj_w")[0:DTR].T),
        "xbcT1": np.ascontiguousarray(g("L0_attn_b_x1_proj_w")[DTR:].T),
        "xdtT2": np.ascontiguousarray(g("L0_attn_b_x2_proj_w")[0:DTR].T),
        "xbcT2": np.ascontiguousarray(g("L0_attn_b_x2_proj_w")[DTR:].T),
        "dtT1": np.ascontiguousarray(g("L0_attn_b_dt1_proj_w").T),
        "dtT2": np.ascontiguousarray(g("L0_attn_b_dt2_proj_w").T),
        "dtb1": np.ascontiguousarray(g("L0_attn_b_dt1_proj_b").reshape(NDI, 128, 1)),
        "dtb2": np.ascontiguousarray(g("L0_attn_b_dt2_proj_b").reshape(NDI, 128, 1)),
        "outprojT": np.ascontiguousarray(g("L0_attn_b_out_proj_w").T),
    }
    sel = np.zeros((64, 2 * DS * 128), np.float32)
    for n in range(DS):
        sel[n, n * 128:(n + 1) * 128] = 1.0
        sel[32 + n, n * 128:(n + 1) * 128] = 1.0
        sel[16 + n, 2048 + n * 128:2048 + (n + 1) * 128] = 1.0
        sel[48 + n, 2048 + n * 128:2048 + (n + 1) * 128] = 1.0
    common["sel"] = sel
    x_full = g("L0_attn_a_x")
    return [dict(common, x=np.ascontiguousarray(x_full[c])) for c in range(B)]


def kernel(**inputs):
    maps = _prep_inputs(inputs)
    nc = build_program()
    res = run_bass_kernel_spmd(nc, maps, core_ids=list(range(B)))
    try:
        kernel._last_dbg = [np.asarray(r["dbg"]) for r in res.results]
    except Exception:
        kernel._last_dbg = None
    return np.stack([np.asarray(r["out"]) for r in res.results], axis=0).astype(np.float32)


# version 02
# speedup vs baseline: 1.0070x; 1.0070x over previous
import numpy as np
import concourse.bass as bass
import concourse.mybir as mybir
from concourse.bass_utils import run_bass_kernel_spmd
from contextlib import ExitStack

F32 = mybir.dt.float32
BF16 = mybir.dt.bfloat16
AF = mybir.ActivationFunctionType
ALU = mybir.AluOpType
AX = mybir.AxisListType

B, T, D = 8, 2048, 1024
H = 16
DI, DS, DTR, DCONV = 2048, 16, 64, 4
NT, ND, NDI = T // 128, D // 128, DI // 128
EPS = 1e-5
N_SUB = 4
DEBUG_TAP = None


class Plan:
    ENG = ("sp", "pe", "dve", "act", "pool", "gq", "aq")
    SERIAL = False
    SERIAL_FROM = 10**9
    SERIAL_UNTIL = -1

    def __init__(self):
        self.steps = []
        self.waited = set()

    def last_per_engine(self):
        out = {}
        for sid, st in enumerate(self.steps):
            out[st[0]] = sid
        return out

    def add(self, engine, fn, waits=(), n_dma=0, hard=()):
        sid = len(self.steps)
        for w in tuple(waits) + tuple(hard):
            assert w is not None and 0 <= w < sid, (w, sid)
            self.waited.add(w)
        self.steps.append((engine, fn, tuple(waits), n_dma, tuple(hard)))
        return sid

    def dma(self, engine, pairs, waits=()):
        def fn(eng):
            return [eng.dma_start(out=d, in_=s) for d, s in pairs]
        return self.add(engine, fn, waits=waits, n_dma=len(pairs))

    def resolve(self):
        if self.SERIAL or self.SERIAL_UNTIL >= 0 or self.SERIAL_FROM < len(self.steps):
            self.waited = set(range(len(self.steps)))
        cnt = {e: 0 for e in self.ENG}
        self.sig = {}
        for sid, (e, fn, waits, n_dma, hard) in enumerate(self.steps):
            if sid in self.waited or n_dma:
                cnt[e] += 16 * n_dma if n_dma else 1
                self.sig[sid] = (e, cnt[e])

    def emit_for(self, eng, enames, sems):
        for sid, (e, fn, waits, n_dma, hard) in enumerate(self.steps):
            if e not in enames:
                continue
            need = {}
            if self.SERIAL or (self.SERIAL_FROM <= sid <= self.SERIAL_UNTIL):
                for w in range(sid):
                    we, wv = self.sig[w]
                    if self.steps[w][0] in enames and not self.steps[w][3]:
                        continue
                    if need.get(we, 0) < wv:
                        need[we] = wv
            for w in waits:
                we, wv = self.sig[w]
                if we in enames and not self.steps[w][3]:
                    continue
                if need.get(we, 0) < wv:
                    need[we] = wv
            for w in hard:
                we, wv = self.sig[w]
                if need.get(we, 0) < wv:
                    need[we] = wv
            for we in sorted(need):
                eng.wait_ge(sems[we], need[we])
            insts = fn(eng)
            if sid in self.waited or n_dma:
                if n_dma:
                    for di in insts:
                        di.then_inc(sems[e], 16)
                else:
                    last = insts[-1] if isinstance(insts, (list, tuple)) else insts
                    last.then_inc(sems[e], 1)


class Buf:
    __slots__ = ("t", "off", "words")

    def __init__(self, t, off, words):
        self.t, self.off, self.words = t, off, words

    def f32(self, c0=0, n=None):
        if n is None:
            n = self.words - c0
        return self.t[:, self.off + c0:self.off + c0 + n]

    def bf16(self, c0=0, n=None):
        if n is None:
            n = 2 * self.words - c0
        return self.t[:, self.off:self.off + self.words].bitcast(BF16)[:, c0:c0 + n]


class Arena:
    def __init__(self, tensor, words):
        self.t, self.words = tensor, words
        self.free = [(0, words)]
        self.peak = 0

    def alloc(self, words):
        words = (words + 1) & ~1
        for i, (off, sz) in enumerate(self.free):
            if sz >= words:
                if sz == words:
                    self.free.pop(i)
                else:
                    self.free[i] = (off + words, sz - words)
                self.peak = max(self.peak, self.words - sum(s for _, s in self.free))
                b = Buf(self.t, off, words)
                import traceback
                caller = traceback.extract_stack()[-2]
                self.live_dbg = getattr(self, "live_dbg", {})
                self.live_dbg[(off, words)] = caller.lineno
                return b
        raise MemoryError(f"arena OOM: want {words}, free {self.free}, live={sorted(getattr(self, 'live_dbg', {}).items())}")

    def release(self, *bufs):
        for b in bufs:
            self.live_dbg.pop((b.off, b.words), None)
            self.free.append((b.off, b.words))
        self.free.sort()
        out = []
        for o, s in self.free:
            if out and out[-1][0] + out[-1][1] == o:
                out[-1][1] += s
            else:
                out.append([o, s])
        self.free = [(o, s) for o, s in out]


def build_program():
    nc = bass.Bass("TRN2", target_bir_lowering=False)
    din = lambda name, shape, dt=F32: nc.dram_tensor(name, list(shape), dt, kind="ExternalInput").ap()

    x_h = din("x", (T, D))
    Wattn_h = din("W_attn", (D, 3 * D))
    battnqk_h = din("battn_qk", (128, 16))
    bvrow_h = din("bv_row", (1, D))
    Wproj_h = din("W_proj", (D, D))
    bprow_h = din("bproj_row", (1, D))
    negmask_h = din("negmask", (128, 128))
    trimask_h = din("trimask", (128, 128))
    eye_h = din("eye", (128, 128))
    WrT_h = din("WrT", (D, D))
    WkT_h = din("WkT", (D, D))
    WvT_h = din("WvT", (D, D))
    murcol_h = din("mur_col", (128, ND))
    mukcol_h = din("muk_col", (128, ND))
    inprojT_h = din("inprojT", (D, 2 * DI))
    convw_h = din("convw", (NDI, 128, DCONV))
    convb_h = din("convb", (NDI, 128, 1))
    Alog1_h = din("Alog1", (NDI, 128, DS))
    Alog2_h = din("Alog2", (NDI, 128, DS))
    D1_h = din("D1", (NDI, 128, 1))
    D2_h = din("D2", (NDI, 128, 1))
    xdtT1_h = din("xdtT1", (DI, DTR))
    xbcT1_h = din("xbcT1", (DI, 2 * DS))
    xdtT2_h = din("xdtT2", (DI, DTR))
    xbcT2_h = din("xbcT2", (DI, 2 * DS))
    dtT1_h = din("dtT1", (DTR, DI))
    dtT2_h = din("dtT2", (DTR, DI))
    dtb1_h = din("dtb1", (NDI, 128, 1))
    dtb2_h = din("dtb2", (NDI, 128, 1))
    sel_h = din("sel", (64, 2 * DS * 128))
    outprojT_h = din("outprojT", (DI, D))
    out_h = nc.dram_tensor("out", [T, D], F32, kind="ExternalOutput").ap()
    dbg_h = nc.dram_tensor("dbg", [128, 2048], F32, kind="ExternalOutput").ap()

    spill_x = nc.dram_tensor("spill_x", [NT, 128, D], F32, kind="Internal").ap()
    spill_ulo = nc.dram_tensor("spill_ulo", [NDI, 128, T], BF16, kind="Internal").ap()
    spill_uhi = nc.dram_tensor("spill_uhi", [NDI, 128, T], BF16, kind="Internal").ap()
    spill_sz = nc.dram_tensor("spill_sz", [NDI, 128, T], BF16, kind="Internal").ap()
    spill_yfin = nc.dram_tensor("spill_yfin", [NDI, 128, T], BF16, kind="Internal").ap()
    stash_ffnw = nc.dram_tensor("stash_ffnw", [3, D, D], BF16, kind="Internal").ap()
    mxfer = nc.dram_tensor("mxfer", [16, 128], BF16, kind="Internal").ap()
    rxfer = nc.dram_tensor("rxfer", [1, 1024], BF16, kind="Internal").ap()
    bxfer = nc.dram_tensor("bxfer", [32, 2048], BF16, kind="Internal").ap()

    P = Plan()
    with ExitStack() as ctx:
        part_words = nc.SBUF_PARTITION_SIZE_BYTES // 4 - 4608
        big = ctx.enter_context(nc.sbuf_tensor("arena", [128, part_words], F32))
        A = Arena(big, part_words)
        psA = ctx.enter_context(nc.psum_tensor("psA", [128, 2048], F32))
        psB = ctx.enter_context(nc.psum_tensor("psB", [128, 2048], F32))
        sems = {e: ctx.enter_context(nc.semaphore(f"s_{e}")) for e in Plan.ENG}
        block = ctx.enter_context(nc.Block())

        b_eye = A.alloc(64)
        b_eyef = A.alloc(128)
        b_negm = A.alloc(128)
        b_trim = A.alloc(64)
        b_ones2 = A.alloc(64)
        b_xres = A.alloc(NT * 1024)
        b_xn = A.alloc(NT * 512)
        b_col = A.alloc(64)
        b_cst = A.alloc(128)

        xres = lambda tt: b_xres.f32(tt * 1024, 1024)
        xn_t = lambda tt: b_xn.bf16(tt * 1024, 1024)
        eye_bf = b_eye.bf16(0, 128)
        ones2 = b_ones2.bf16(0, 128)[0:2, :]

        s_x = P.dma("sp", [(xres(tt), x_h[tt * 128:(tt + 1) * 128, :]) for tt in range(NT)])
        s_e32 = P.dma("sp", [(b_cst.f32(0, 128), eye_h[:, :]), (b_eyef.f32(0, 128), eye_h[:, :])])
        s_eye = P.add("dve", lambda v: v.tensor_copy(out=eye_bf, in_=b_cst.f32(0, 128)), waits=(s_e32,))
        s_n32 = P.dma("sp", [(b_negm.f32(0, 128), negmask_h[:, :])], waits=(s_eye,))
        s_t32 = P.dma("sp", [(b_cst.f32(0, 128), trimask_h[:, :])], waits=(s_eye,))
        s_tri = P.add("dve", lambda v: v.tensor_copy(out=b_trim.bf16(0, 128), in_=b_cst.f32(0, 128)), waits=(s_t32,))
        s_on2 = P.add("dve", lambda v: [v.memset(ones2, 1.0),
                                        v.memset(b_col.f32(30, 1), EPS),
                                        v.memset(b_col.f32(31, 1), 1.0)])
        s_base = P.add("dve", lambda v: v.engine_nop(), waits=(s_x, s_n32, s_tri, s_on2))

        def tap(name, ap, after_step):
            if DEBUG_TAP == name:
                n = ap.shape[-1]
                P.dma("gq", [(dbg_h[0:ap.shape[0], 0:n], ap)], waits=(after_step,))

        def emit_ln(waits):
            outs = []
            st6 = b_col.f32(0, 12)
            mv = b_col.f32(12, 2)
            rstd = b_col.f32(14, 1)
            bneg = b_col.f32(15, 1)
            lnt = b_col.f32(16, 1)
            prev = None
            for tt in range(NT):
                w = tuple(waits) if prev is None else (prev,)
                s1 = P.add("dve", lambda v, tt=tt: [
                    v.bn_stats(out=b_col.f32(0, 6), in_=xres(tt)[:, 0:512]),
                    v.bn_stats(out=b_col.f32(6, 6), in_=xres(tt)[:, 512:1024]),
                    v.bn_aggr(out=b_col.f32(18, 2), in_=b_col.f32(0, 6)),
                    v.bn_aggr(out=b_col.f32(24, 2), in_=b_col.f32(6, 6))], waits=w)
                s1b = P.add("pool", lambda g: [
                    g.tensor_add(out=b_col.f32(26, 2), in0=b_col.f32(18, 2), in1=b_col.f32(24, 2)),
                    g.tensor_sub(out=b_col.f32(28, 1), in0=b_col.f32(18, 1), in1=b_col.f32(24, 1)),
                    g.tensor_scalar(out=mv[:, 0:1], in0=b_col.f32(26, 1), scalar1=0.5, scalar2=None,
                                    op0=ALU.mult),
                    g.tensor_tensor(out=b_col.f32(29, 1), in0=b_col.f32(28, 1), in1=b_col.f32(28, 1),
                                    op=ALU.mult),
                    g.tensor_scalar(out=b_col.f32(29, 1), in0=b_col.f32(29, 1), scalar1=0.25, scalar2=None,
                                    op0=ALU.mult),
                    g.tensor_scalar(out=b_col.f32(17, 1), in0=b_col.f32(27, 1), scalar1=0.5, scalar2=None,
                                    op0=ALU.mult),
                    g.tensor_add(out=mv[:, 1:2], in0=b_col.f32(17, 1), in1=b_col.f32(29, 1))], waits=(s1,))
                s2 = P.add("act", lambda s: [s.activation(out=lnt, in_=mv[:, 1:2], func=AF.Ln, bias=b_col.f32(30, 1)[:, 0:1]),
                                             s.activation(out=rstd, in_=lnt, func=AF.Exp, scale=-0.5)],
                           waits=(s1b,))
                s2b = P.add("pool", lambda g: [
                    g.tensor_tensor(out=bneg, in0=mv[:, 0:1], in1=rstd, op=ALU.mult),
                    g.tensor_scalar(out=bneg, in0=bneg, scalar1=-1.0, scalar2=None, op0=ALU.mult)],
                    waits=(s2,))
                s3 = P.add("dve", lambda v, tt=tt: v.tensor_scalar(
                    out=xn_t(tt), in0=xres(tt), scalar1=rstd[:, 0:1],
                    scalar2=bneg[:, 0:1], op0=ALU.mult, op1=ALU.add), waits=(s2b,))
                prev = s3
                outs.append(s3)
            return outs

        dummy_tp = lambda t: t.transpose(out=psB[:, 1920:1984].bitcast(BF16)[:, 0:128],
                                         in_=eye_bf, identity=eye_bf)

        def barrier():
            lasts = [s for s in P.last_per_engine().values()]
            g = P.add("dve", lambda v: v.engine_nop(), waits=tuple(lasts))
            bsteps = {"dve": g}
            bsteps["pool"] = P.add("pool", lambda p: p.engine_nop(), waits=(g,))
            bsteps["act"] = P.add("act", lambda s: s.copy(out=b_col.f32(33, 1), in_=b_col.f32(31, 1)),
                                  waits=(g,))
            bsteps["pe"] = P.add("pe", lambda t: dummy_tp(t), waits=(g,))
            return [g, bsteps["pool"], bsteps["act"], bsteps["pe"]]

        def tok2fm(src_tile, dst_buf, waits):
            ps_bf = psB[:, 0:256].bitcast(BF16)
            last = None
            prev_ev = None
            for f in range(ND):
                for g in range(0, NT, 4):
                    w = list(waits) + ([prev_ev] if prev_ev else [])
                    s_tp = P.add("pe", lambda t, f=f, g=g: [
                        t.transpose(out=ps_bf[:, u * 128:(u + 1) * 128],
                                    in_=src_tile(g + u)[:, f * 128:(f + 1) * 128],
                                    identity=eye_bf) for u in range(4)] + [dummy_tp(t)], waits=tuple(w))
                    s_ev = P.add("dve", lambda v, f=f, g=g: v.tensor_copy(
                        out=dst_buf.bf16(f * 2048 + g * 128, 512), in_=ps_bf[:, 0:512]), waits=(s_tp,))
                    prev_ev = s_ev
                    last = s_ev
            return last

        def rowpair(dst_buf, src_row_h, row_stage, lo_stage, waits=()):
            sl = P.dma("sp", [(row_stage.f32(0, 1024)[0:1, :], src_row_h[0:1, :])], waits=waits)
            s_mk = P.add("dve", lambda v: [
                v.tensor_copy(out=dst_buf.bf16(0, 1024)[0:1, :], in_=row_stage.f32(0, 1024)[0:1, :]),
                v.tensor_sub(out=lo_stage.bf16(0, 1024)[0:1, :], in0=row_stage.f32(0, 1024)[0:1, :],
                             in1=dst_buf.bf16(0, 1024)[0:1, :])], waits=(sl,))
            s_lo0 = P.dma("gq", [(rxfer[0:1, :], lo_stage.bf16(0, 1024)[0:1, :])], waits=(s_mk,))
            s_lo = P.dma("gq", [(dst_buf.bf16(0, 1024)[1:2, :], rxfer[0:1, :])], waits=(s_lo0,))
            return s_lo

        def load_cast(dst_buf, dst_c0, src_ap, stage, nwords, waits, eng_c="dve"):
            sl = P.dma("sp", [(stage.f32(0, nwords), src_ap)], waits=waits)
            sc = P.add(eng_c, lambda v: v.tensor_copy(out=dst_buf.bf16(dst_c0, nwords), in_=stage.f32(0, nwords)),
                       waits=(sl,))
            return sc

        def emit_attn_a(ln_steps):
            b_xnfm = A.alloc(ND * 1024)
            xn_fm = lambda f, c0, n: b_xnfm.bf16(f * 2048 + c0, n)
            s_fm = tok2fm(xn_t, b_xnfm, tuple(ln_steps))
            tap("xn0", b_xn.f32(0, 512), ln_steps[-1])
            tap("xnfm0", b_xnfm.f32(0, 1024), s_fm)
            tap("xnfm7", b_xnfm.f32(7 * 1024, 1024), s_fm)

            mark("attn_a:vext")
            b_vext = A.alloc(NT * 520)
            vext = lambda tt: b_vext.bf16(tt * 1040, 1040)
            s_vi = P.add("dve", lambda v: v.memset(b_vext.bf16(0, NT * 1040), 0.0))
            s_vo = P.add("dve", lambda v: [
                v.memset(vext(tt).rearrange("p (h c) -> p h c", c=65)[:, :, 64:65], 1.0)
                for tt in range(NT)], waits=(s_vi,))

            b_wstage = A.alloc(1024)
            b_row = A.alloc(1024)
            b_lo = A.alloc(512)
            b_bvp = A.alloc(512)
            b_bpp = A.alloc(512)
            b_bqk = A.alloc(16)
            s_bv = rowpair(b_bvp, bvrow_h, b_row, b_lo)
            s_bp = rowpair(b_bpp, bprow_h, b_row, b_lo, waits=(s_bv,))
            s_bqk = P.dma("sp", [(b_bqk.f32(0, 16), battnqk_h[:, :])])

            b_wv = A.alloc(ND * 512)
            wv = lambda k: b_wv.bf16(k * 1024, 1024)
            prev = (s_fm,)
            for k in range(ND):
                prev = (load_cast(b_wv, k * 1024, Wattn_h[k * 128:(k + 1) * 128, 2048:3072],
                                  b_wstage, 1024, prev),)
            s_wv = prev[0]
            tap("wv01", b_wv.f32(0, 1024), s_wv)

            v_ev = []
            for tt in range(NT):
                pso = psA[:, (tt % 2) * 1024:(tt % 2) * 1024 + 1024]
                w = [s_wv, s_bv, s_vo, s_fm]
                if tt >= 2:
                    w.append(v_ev[tt - 2])
                s_mm = P.add("pe", lambda t, tt=tt, pso=pso: [
                    t.matmul(out=pso[:, hf * 512:(hf + 1) * 512],
                             lhsT=(xn_fm(k, tt * 128, 128) if k < ND else ones2),
                             rhs=(wv(k)[:, hf * 512:(hf + 1) * 512] if k < ND
                                  else b_bvp.bf16(0, 1024)[0:2, hf * 512:(hf + 1) * 512]),
                             start=(k == 0), stop=(k == ND))
                    for hf in range(2) for k in range(ND + 1)], waits=tuple(w))
                s_ev = P.add("dve", lambda v, tt=tt, pso=pso: v.tensor_copy(
                    out=vext(tt).rearrange("p (h c) -> p h c", c=65)[:, :, 0:64],
                    in_=pso.rearrange("p (h c) -> p h c", c=64)), waits=(s_mm,))
                if tt == 0:
                    tap("vext0", b_vext.f32(0, 520), s_ev)
                v_ev.append(s_ev)

            mark("attn_a:heads")
            bar = barrier()
            A.release(b_wv, b_wstage, b_row, b_lo)
            b_qk = A.alloc(4 * 1024)
            b_wqk32 = A.alloc(2048)
            b_wqkbf = A.alloc(1024)
            b_ytok = b_xn
            b_attT = A.alloc(512)
            b_mm = A.alloc(16)
            b_mmbf = A.alloc(16)
            b_mst = A.alloc(64)
            b_mrow = A.alloc(1024)
            qk_q = lambda pb, c0, n: b_qk.bf16(pb * 4096 + c0, n)
            qk_k = lambda pb, c0, n: b_qk.bf16(pb * 4096 + 2048 + c0, n)
            y_tok = lambda tt, c0, n: b_ytok.bf16(tt * 1024 + c0, n)
            attT = lambda ab, c0, n: b_attT.bf16(ab * 512 + c0, n)
            negm = b_negm.f32(0, 128)
            trim = b_trim.bf16(0, 128)
            mrow = b_mrow.bf16(0, 2048)[0:1, :]

            psA_free = [None]
            last_stm_prev = [None]
            m_guard = [None]
            m_guard2 = [None]
            qk_lastuse = {}
            wqk_lastcast = [None]
            last_head_done = [v_ev[-1]]
            psA_free[0] = v_ev[-1]

            for hp in range(H // 2):
                pb = hp % 2
                pq = [(b_wqk32.f32(c * 128, 128), Wattn_h[c * 128:(c + 1) * 128, hp * 128:(hp + 1) * 128])
                      for c in range(ND)]
                pk = [(b_wqk32.f32(1024 + c * 128, 128), Wattn_h[c * 128:(c + 1) * 128, D + hp * 128:D + (hp + 1) * 128])
                      for c in range(ND)]
                w = [s_bqk, v_ev[-1], s_bp, bar[0]]
                if wqk_lastcast[0] is not None:
                    w.append(wqk_lastcast[0])
                s_wl = P.dma("sp", pq + pk, waits=tuple(w))
                s_wc = P.add("dve", lambda v: v.tensor_copy(out=b_wqkbf.bf16(0, 2048), in_=b_wqk32.f32(0, 2048)),
                             waits=(s_wl,))
                wqk_lastcast[0] = s_wc
                wq = lambda c: b_wqkbf.bf16(c * 128, 128)
                wk = lambda c: b_wqkbf.bf16(1024 + c * 128, 128)

                w = [s_wc, s_fm]
                if pb in qk_lastuse:
                    w.append(qk_lastuse[pb])
                if psA_free[0] is not None:
                    w.append(psA_free[0])
                s_qmm = P.add("pe", lambda t: [
                    t.matmul(out=psA[:, q4 * 512:(q4 + 1) * 512], lhsT=wq(c), rhs=xn_fm(c, q4 * 512, 512),
                             start=(c == 0), stop=(c == ND - 1))
                    for q4 in range(4) for c in range(ND)], waits=tuple(w))
                s_qev = P.add("dve", lambda v, pb=pb, hp=hp: [
                    v.tensor_scalar(out=qk_q(pb, q4 * 512, 512), in0=psA[:, q4 * 512:(q4 + 1) * 512],
                                    scalar1=b_bqk.f32(hp, 1)[:, 0:1], scalar2=None, op0=ALU.add)
                    for q4 in range(4)], waits=(s_qmm,))
                s_kmm = P.add("pe", lambda t: [
                    t.matmul(out=psA[:, q4 * 512:(q4 + 1) * 512], lhsT=wk(c), rhs=xn_fm(c, q4 * 512, 512),
                             start=(c == 0), stop=(c == ND - 1))
                    for q4 in range(4) for c in range(ND)], waits=(s_qev,))
                s_kev = P.add("dve", lambda v, pb=pb, hp=hp: [
                    v.tensor_scalar(out=qk_k(pb, q4 * 512, 512), in0=psA[:, q4 * 512:(q4 + 1) * 512],
                                    scalar1=b_bqk.f32(8 + hp, 1)[:, 0:1], scalar2=None, op0=ALU.add)
                    for q4 in range(4)], waits=(s_kmm,))
                if hp == 0:
                    tap("q0", b_qk.f32(0, 1024), s_qev)
                    tap("k0", b_qk.f32(2048, 1024), s_kev)
                psA_free[0] = s_kev

                for half in range(2):
                    h = 2 * hp + half
                    e0 = 64 * half
                    prev_mx = None
                    for i in range(NT):
                        nk = (i + 1) * 128
                        w = [s_kev]
                        if prev_mx is not None:
                            w.append(prev_mx)
                        elif psA_free[0] is not None:
                            w.append(psA_free[0])
                        s_smm = P.add("pe", lambda t, i=i, nk=nk, pb=pb, e0=e0: [
                            t.matmul(out=psA[:, n0:min(nk, n0 + 512)],
                                     lhsT=qk_q(pb, i * 128, 128)[e0:e0 + 64, :],
                                     rhs=qk_k(pb, n0, min(nk, n0 + 512) - n0)[e0:e0 + 64, :],
                                     start=True, stop=True)
                            for n0 in range(0, nk, 512)], waits=tuple(w))
                        if i == 7:
                            tap(f"sc_h{h}", psA[:, 0:1024], s_smm)
                        w_mx = [s_smm]
                        if i == 0 and m_guard2[0] is not None:
                            w_mx.append(m_guard2[0])
                        s_mx = P.add("dve", lambda v, i=i, nk=nk: [
                            v.tensor_tensor(out=psA[:, i * 128:(i + 1) * 128],
                                            in0=psA[:, i * 128:(i + 1) * 128], in1=negm, op=ALU.add),
                            v.tensor_reduce(out=b_mm.f32(i, 1), in_=psA[:, 0:nk], axis=AX.X,
                                            op=ALU.max)], waits=tuple(w_mx))
                        prev_mx = s_mx
                    psA_free[0] = prev_mx
                    w_mn = [prev_mx]
                    if m_guard[0] is not None:
                        w_mn.append(m_guard[0])
                    s_mneg = P.add("pool", lambda g: g.tensor_scalar(
                        out=b_mmbf.f32(0, 16), in0=b_mm.f32(0, 16), scalar1=-1.0, scalar2=None,
                        op0=ALU.mult), waits=tuple(w_mn))
                    m_guard2[0] = s_mneg
                    ps_m = psB[:, 1536:1664]
                    s_mt = P.add("pe", lambda t: t.transpose(out=ps_m[0:16, 0:128], in_=b_mmbf.f32(0, 16),
                                                             identity=b_eyef.f32(0, 128)), waits=(s_mneg,))
                    s_ms = P.add("dve", lambda v: v.tensor_copy(out=b_mst.bf16(0, 128)[0:16, :],
                                                                in_=ps_m[0:16, 0:128]), waits=(s_mt,))
                    w_mr = [s_ms]
                    if last_stm_prev[0] is not None:
                        w_mr.append(last_stm_prev[0])
                    s_mr0 = P.dma("gq", [(mxfer[:, :], b_mst.bf16(0, 128)[0:16, :])], waits=tuple(w_mr))
                    m_guard[0] = s_mr0
                    s_mr = P.dma("gq", [(mrow, mxfer.rearrange("p f -> () (p f)"))], waits=(s_mr0,))

                    y_evs = []
                    exp_lastuse = {}
                    for i in range(NT):
                        psY = psB[:, 1040 + (i % 2) * 80:1040 + (i % 2) * 80 + 65]
                        n_grp = (i + 4) // 4
                        last_av = None
                        for gi, jg in enumerate(range(0, i + 1, 4)):
                            je = min(jg + 4, i + 1)
                            ab = gi % 2
                            pstile = psB[:, ab * 512:(ab + 1) * 512]
                            w = [s_mr]
                            if (ab in exp_lastuse):
                                w.append(exp_lastuse[ab])
                            s_stm = P.add("pe", lambda t, i=i, jg=jg, je=je, pstile=pstile, pb=pb, e0=e0: [
                                mm for j in range(jg, je) for mm in (
                                    t.matmul(out=pstile[:, (j - jg) * 128:(j - jg + 1) * 128],
                                             lhsT=qk_k(pb, j * 128, 128)[e0:e0 + 64, :],
                                             rhs=qk_q(pb, i * 128, 128)[e0:e0 + 64, :],
                                             start=True, stop=False),
                                    t.matmul(out=pstile[:, (j - jg) * 128:(j - jg + 1) * 128],
                                             lhsT=ones2[0:1, :],
                                             rhs=mrow[:, i * 128:(i + 1) * 128],
                                             start=False, stop=True))], waits=tuple(w))
                            s_ex = P.add("act", lambda s, jg=jg, je=je, ab=ab, pstile=pstile: s.activation(
                                out=attT(ab, 0, (je - jg) * 128), in_=pstile[:, 0:(je - jg) * 128],
                                func=AF.Exp, scale=0.125), waits=(s_stm,))
                            dep = s_ex
                            if je - 1 == i:
                                dep = P.add("dve", lambda v, i=i, jg=jg, ab=ab: v.tensor_tensor(
                                    out=attT(ab, (i - jg) * 128, 128), in0=attT(ab, (i - jg) * 128, 128),
                                    in1=trim, op=ALU.mult), waits=(s_ex,))
                            w2 = [dep]
                            if len(y_evs) >= 2:
                                w2.append(y_evs[i - 2])
                            last_stm_prev[0] = s_stm
                            s_av = P.add("pe", lambda t, i=i, jg=jg, je=je, ab=ab, psY=psY, h=h: [
                                t.matmul(out=psY, lhsT=attT(ab, (j - jg) * 128, 128),
                                         rhs=vext(j)[:, h * 65:h * 65 + 65],
                                         start=(j == 0), stop=(j == i)) for j in range(jg, je)],
                                waits=tuple(w2))
                            exp_lastuse[ab] = s_av
                            last_av = s_av
                        w3 = [last_av]
                        if len(y_evs) >= 1:
                            w3.append(y_evs[-1])
                        s_rc = P.add("dve", lambda v, psY=psY: v.reciprocal(
                            out=b_col.f32(22, 1), in_=psY[:, 64:65]), waits=tuple(w3))
                        s_ye = P.add("dve", lambda v, i=i, psY=psY, h=h: v.tensor_scalar(
                            out=y_tok(i, h * 64, 64), in0=psY[:, 0:64],
                            scalar1=b_col.f32(22, 1)[:, 0:1], scalar2=None, op0=ALU.mult),
                            waits=(), hard=(s_rc,))
                        y_evs.append(s_ye)
                    if h == 0:
                        tap("mrow0", b_mrow.f32(0, 1024), s_mr)
                        tap("mmat0", b_mm.f32(0, 16), s_mt)
                        tap("ytok0", b_xn.f32(0, 512), y_evs[-1])
                    qk_lastuse[pb] = y_evs[-1]
                    last_head_done[0] = y_evs[-1]

            mark("attn_a:proj")
            barrier()
            s_att_done = last_head_done[0]
            A.release(b_xnfm, b_vext, b_qk, b_wqk32, b_wqkbf, b_attT, b_mm, b_mmbf, b_mst, b_mrow)
            b_yfm = A.alloc(ND * 1024)
            y_fm = lambda f, c0, n: b_yfm.bf16(f * 2048 + c0, n)
            s_yfm = tok2fm(lambda tt: y_tok(tt, 0, 1024), b_yfm, (s_att_done,))

            b_wp = A.alloc(ND * 512)
            b_wps = A.alloc(1024)
            wp = lambda k: b_wp.bf16(k * 1024, 1024)
            prev = (s_att_done,)
            for k in range(ND):
                prev = (load_cast(b_wp, k * 1024, Wproj_h[k * 128:(k + 1) * 128, :], b_wps, 1024, prev),)
            s_wp = prev[0]
            res = []
            for tt in range(NT):
                pso = psA[:, (tt % 2) * 1024:(tt % 2) * 1024 + 1024]
                w = [s_wp, s_yfm, s_bp]
                if tt >= 2:
                    w.append(res[tt - 2])
                s_pm = P.add("pe", lambda t, tt=tt, pso=pso: [
                    t.matmul(out=pso[:, hf * 512:(hf + 1) * 512], lhsT=y_fm(k, tt * 128, 128),
                             rhs=wp(k)[:, hf * 512:(hf + 1) * 512], start=(k == 0), stop=False)
                    for k in range(ND) for hf in range(2)] + [
                    t.matmul(out=pso[:, hf * 512:(hf + 1) * 512], lhsT=ones2,
                             rhs=b_bpp.bf16(0, 1024)[0:2, hf * 512:(hf + 1) * 512],
                             start=False, stop=True) for hf in range(2)], waits=tuple(w))
                s_rs = P.add("dve", lambda v, tt=tt, pso=pso: v.tensor_tensor(
                    out=xres(tt), in0=pso[:, 0:1024], in1=xres(tt), op=ALU.add), waits=(s_pm,))
                if tt == 0:
                    tap("res0", b_xres.f32(0, 1024), s_rs)
                    tap("yfm0", b_yfm.f32(0, 1024), s_yfm)
                res.append(s_rs)
            A.release(b_bvp, b_bpp, b_bqk, b_yfm, b_wp, b_wps)
            return res

        def emit_ffn(ln_steps, which):
            mark(f"ffn{which}:start")
            tap(f"ffn{which}_xn0", b_xn.f32(0, 512), ln_steps[-1])
            b_mu = A.alloc(16)
            s_mul = P.dma("sp", [(b_mu.f32(0, ND), murcol_h[:, :]),
                                 (b_mu.f32(8, ND), mukcol_h[:, :])], waits=(ln_steps[-1],))
            mur = lambda f: b_mu.f32(0 + f, 1)
            muk = lambda f: b_mu.f32(8 + f, 1)

            b_xnfm2 = A.alloc(ND * 1024)
            s_fm2 = tok2fm(xn_t, b_xnfm2, tuple(ln_steps))
            xnf = lambda f, c0, n: b_xnfm2.bf16(f * 2048 + c0, n)
            b_xrfm = A.alloc(ND * 1024)
            b_xkfm = A.alloc(ND * 1024)
            b_mt = A.alloc(1024)
            xr_fm = lambda f, c0, n: b_xrfm.bf16(f * 2048 + c0, n)
            xk_fm = lambda f, c0, n: b_xkfm.bf16(f * 2048 + c0, n)
            mix_done = []
            for f in range(ND):
                s_mx1 = P.add("dve", lambda v, f=f: [
                    v.tensor_sub(out=b_mt.bf16(0, 2047), in0=xnf(f, 1, 2047), in1=xnf(f, 0, 2047)),
                    v.scalar_tensor_tensor(out=xr_fm(f, 1, 2047), in0=b_mt.bf16(0, 2047),
                                           scalar=mur(f)[:, 0:1], in1=xnf(f, 0, 2047),
                                           op0=ALU.mult, op1=ALU.add),
                    v.tensor_scalar(out=xr_fm(f, 0, 1), in0=xnf(f, 0, 1), scalar1=mur(f)[:, 0:1],
                                    scalar2=None, op0=ALU.mult),
                    v.scalar_tensor_tensor(out=xk_fm(f, 1, 2047), in0=b_mt.bf16(0, 2047),
                                           scalar=muk(f)[:, 0:1], in1=xnf(f, 0, 2047),
                                           op0=ALU.mult, op1=ALU.add),
                    v.tensor_scalar(out=xk_fm(f, 0, 1), in0=xnf(f, 0, 1), scalar1=muk(f)[:, 0:1],
                                    scalar2=None, op0=ALU.mult)],
                    waits=(s_fm2, s_mul) + ((mix_done[-1],) if mix_done else ()))
                mix_done.append(s_mx1)

            mark(f"ffn{which}:weights")
            barrier()
            A.release(b_xnfm2, b_mt)
            b_wf = A.alloc(ND * 512)
            b_wfs = A.alloc(1024)
            wf = lambda k: b_wf.bf16(k * 1024, 1024)

            def load_w(idx, src_h, waits):
                prev = tuple(waits)
                for k in range(ND):
                    if which == 0:
                        sc = load_cast(b_wf, k * 1024, src_h[k * 128:(k + 1) * 128, :], b_wfs, 1024, prev)
                        st = P.dma("gq", [(stash_ffnw[idx, k * 128:(k + 1) * 128, :], wf(k))], waits=(sc,))
                        prev = (sc, st)
                        last = st
                    else:
                        sl = P.dma("sp", [(wf(k), stash_ffnw[idx, k * 128:(k + 1) * 128, :])], waits=prev)
                        prev = (sl,)
                        last = sl
                return last

            r_tok = lambda tt, c0, n: b_xn.bf16(tt * 1024 + c0, n)
            lw = load_w(0, WrT_h, (mix_done[-1],))
            r_steps = []
            for tt in range(NT):
                pso = psA[:, (tt % 2) * 1024:(tt % 2) * 1024 + 1024]
                w = [lw, mix_done[-1]]
                if tt >= 2:
                    w.append(r_steps[tt - 2])
                s_rm = P.add("pe", lambda t, tt=tt, pso=pso: [
                    t.matmul(out=pso[:, hf * 512:(hf + 1) * 512], lhsT=xr_fm(k, tt * 128, 128),
                             rhs=wf(k)[:, hf * 512:(hf + 1) * 512], start=(k == 0), stop=(k == ND - 1))
                    for k in range(ND) for hf in range(2)], waits=tuple(w))
                s_rs = P.add("act", lambda s, tt=tt, pso=pso: s.activation(
                    out=r_tok(tt, 0, 1024), in_=pso[:, 0:1024], func=AF.Sigmoid), waits=(s_rm,))
                r_steps.append(s_rs)

            mark(f"ffn{which}:k")
            barrier()
            k2_fm = lambda f, c0, n: b_xrfm.bf16(f * 2048 + c0, n)
            tap(f"ffn{which}_r0", b_xn.f32(0, 512), r_steps[-1])
            tap(f"ffn{which}_xk0", b_xkfm.f32(0, 1024), r_steps[-1])
            tap(f"ffn{which}_xr0", b_xrfm.f32(0, 1024), mix_done[-1])
            b_krel = A.alloc(2048)
            lw2 = load_w(1, WkT_h, (r_steps[-1],))
            k_steps = []
            for m in range(ND):
                for toff in (0, 1024):
                    pso = psA[:, (len(k_steps) % 2) * 1024:(len(k_steps) % 2) * 1024 + 1024]
                    w = [lw2, r_steps[-1]]
                    if len(k_steps) >= 2:
                        w.append(k_steps[-2])
                    s_km = P.add("pe", lambda t, m=m, toff=toff, pso=pso: [
                        t.matmul(out=pso[:, q4 * 512:(q4 + 1) * 512], lhsT=wf(k)[:, m * 128:(m + 1) * 128],
                                 rhs=xk_fm(k, toff + q4 * 512, 512), start=(k == 0), stop=(k == ND - 1))
                        for k in range(ND) for q4 in range(2)], waits=tuple(w))
                    kp = len(k_steps) % 2
                    s_kr = P.add("dve", lambda v, pso=pso, kp=kp: v.tensor_scalar(
                        out=b_krel.f32(kp * 1024, 1024), in0=pso[:, 0:1024], scalar1=0.0, scalar2=None,
                        op0=ALU.max), waits=(s_km,))
                    s_sq = P.add("pool", lambda g, m=m, toff=toff, kp=kp: g.tensor_mul(
                        out=k2_fm(m, toff, 1024), in0=b_krel.f32(kp * 1024, 1024),
                        in1=b_krel.f32(kp * 1024, 1024)), waits=(s_kr,))
                    k_steps.append(s_sq)

            mark(f"ffn{which}:v")
            barrier()
            tap(f"ffn{which}_k20", b_xrfm.f32(0, 1024), k_steps[-1])
            b_g = A.alloc(1024)
            lw3 = load_w(2, WvT_h, (k_steps[-1],))
            res = []
            for tt in range(NT):
                pso = psA[:, (tt % 2) * 1024:(tt % 2) * 1024 + 1024]
                w = [lw3, k_steps[-1]]
                if tt >= 2:
                    w.append(res[tt - 2])
                s_vm = P.add("pe", lambda t, tt=tt, pso=pso: [
                    t.matmul(out=pso[:, hf * 512:(hf + 1) * 512], lhsT=k2_fm(k, tt * 128, 128),
                             rhs=wf(k)[:, hf * 512:(hf + 1) * 512], start=(k == 0), stop=(k == ND - 1))
                    for k in range(ND) for hf in range(2)], waits=tuple(w))
                s_g = P.add("dve", lambda v, tt=tt, pso=pso: [
                    v.tensor_tensor(out=b_g.f32(0, 1024), in0=pso[:, 0:1024], in1=r_tok(tt, 0, 1024),
                                    op=ALU.mult),
                    v.tensor_tensor(out=xres(tt), in0=b_g.f32(0, 1024), in1=xres(tt), op=ALU.add)],
                    waits=(s_vm,))
                res.append(s_g)
            A.release(b_mu, b_xrfm, b_xkfm, b_wf, b_wfs, b_krel, b_g)
            return res

        def emit_attn_b(ln_steps):
            b_xnfm = A.alloc(ND * 1024)
            xn_fm = lambda f, c0, n: b_xnfm.bf16(f * 2048 + c0, n)
            s_fm = tok2fm(xn_t, b_xnfm, tuple(ln_steps))
            s_spx = P.dma("gq", [(spill_x[tt, :, :], xres(tt)) for tt in range(NT)], waits=tuple(ln_steps))

            b_cs = A.alloc(NDI * 40)
            b_bs = A.alloc(64)
            cw = lambda dt, wi: b_cs.f32(dt * 40 + wi, 1)
            cb = lambda dt: b_cs.f32(dt * 40 + 4, 1)
            dtb = lambda dt, br: b_cs.f32(dt * 40 + 5 + br, 1)
            dD = lambda dt: b_cs.f32(dt * 40 + 7, 1)
            Aneg = lambda dt, br, n: b_cs.f32(dt * 40 + 8 + br * 16 + n, 1)
            prev = None
            for dt in range(NDI):
                pairs = [(b_cs.f32(dt * 40, 4), convw_h[dt, :, :]),
                         (b_cs.f32(dt * 40 + 4, 1), convb_h[dt, :, :]),
                         (b_bs.f32(0, 16), Alog1_h[dt, :, :]),
                         (b_bs.f32(16, 16), Alog2_h[dt, :, :]),
                         (b_cs.f32(dt * 40 + 5, 1), dtb1_h[dt, :, :]),
                         (b_cs.f32(dt * 40 + 6, 1), dtb2_h[dt, :, :]),
                         (b_bs.f32(32, 1), D1_h[dt, :, :]),
                         (b_bs.f32(33, 1), D2_h[dt, :, :])]
                sl = P.dma("sp", pairs, waits=(prev,) if prev else (ln_steps[-1],))
                s_e = P.add("act", lambda s, dt=dt: s.activation(
                    out=b_cs.f32(dt * 40 + 8, 32), in_=b_bs.f32(0, 32), func=AF.Exp), waits=(sl,))
                s_m = P.add("dve", lambda v, dt=dt: [
                    v.tensor_scalar(out=b_cs.f32(dt * 40 + 8, 32), in0=b_cs.f32(dt * 40 + 8, 32),
                                    scalar1=-1.0, scalar2=None, op0=ALU.mult),
                    v.tensor_sub(out=b_cs.f32(dt * 40 + 7, 1), in0=b_bs.f32(32, 1), in1=b_bs.f32(33, 1))],
                    waits=(s_e,))
                prev = s_m
            s_cs = prev

            b_selbf = A.alloc(2048)
            b_sel32 = A.alloc(4096)
            s_sl = P.dma("sp", [(b_sel32.f32(0, 4096)[0:64, :], sel_h[:, :])], waits=(ln_steps[-1],))
            s_sc = P.add("dve", lambda v: v.tensor_copy(out=b_selbf.bf16(0, 4096)[0:64, :],
                                                        in_=b_sel32.f32(0, 4096)[0:64, :]), waits=(s_sl,))
            A.release(b_sel32)
            selB = lambda n: b_selbf.bf16(0, 4096)[0:64, n * 128:(n + 1) * 128]
            selC = lambda n: b_selbf.bf16(0, 4096)[0:64, 2048 + n * 128:2048 + (n + 1) * 128]

            b_ipstage = Buf(big, b_xres.off, 1024)
            b_ipm = Buf(big, b_xres.off + 1024, 1024)
            b_xraw = Buf(big, b_xres.off + 2048, 4114)
            b_ctmp = Buf(big, b_xres.off + 6162, 4096)
            b_uf = Buf(big, b_xres.off + 10258, 2048)
            b_cfin = Buf(big, b_xn.off, 4096)
            b_ulo = Buf(big, b_xn.off + 4096, 2048)
            b_szt = Buf(big, b_xn.off + 6144, 2048)
            b_uhis = A.alloc(2048)
            ipm = lambda mp, k: b_ipm.bf16(mp * 1024 + k * 128, 128)

            xz_sp = []
            u_done = []
            prev_si = None
            prev_cast = None
            prev_mm = {}
            for m in range(2 * NDI):
                pb = m % 2
                pso = psA[:, pb * 1024:pb * 1024 + 1024]
                w = [s_fm, s_spx]
                if prev_cast is not None:
                    w.append(prev_cast)
                if (pb in prev_mm):
                    w.append(prev_mm[pb])
                s_ld = P.dma("sp", [(b_ipstage.f32(k * 128, 128),
                                     inprojT_h[k * 128:(k + 1) * 128, m * 128:(m + 1) * 128])
                                    for k in range(ND)], waits=tuple(w))
                s_ct = P.add("dve", lambda v, pb=pb: v.tensor_copy(
                    out=b_ipm.bf16(pb * 1024, 1024), in_=b_ipstage.f32(0, 1024)), waits=(s_ld,))
                prev_cast = s_ct
                halves = []
                for toff in (0, 1024):
                    w = [s_ct]
                    if len(xz_sp) >= 2:
                        w.append(xz_sp[-2])
                    if halves:
                        w.append(halves[-1])
                    s_mm = P.add("pe", lambda t, m=m, toff=toff, pso=pso, pb=pb: [
                        t.matmul(out=pso[:, q4 * 512:(q4 + 1) * 512], lhsT=ipm(pb, k),
                                 rhs=xn_fm(k, toff + q4 * 512, 512), start=(k == 0), stop=(k == ND - 1))
                        for k in range(ND) for q4 in range(2)], waits=tuple(w))
                    prev_mm[pb] = s_mm
                    if m < NDI:
                        s_ev = P.add("dve", lambda v, pb=pb, toff=toff, pso=pso: v.tensor_copy(
                            out=b_xraw.f32(pb * 2056 + 4 + toff, 1024), in_=pso[:, 0:1024]), waits=(s_mm,))
                    else:
                        s_ev = P.add("act", lambda s, pb=pb, toff=toff, pso=pso: s.activation(
                            out=b_szt.bf16(pb * 2048 + toff, 1024), in_=pso[:, 0:1024], func=AF.Silu),
                            waits=(s_mm,))
                    halves.append(s_ev)
                if m >= NDI:
                    dt_ = m - NDI
                    s_sp = P.dma("gq", [(spill_sz[dt_, :, :], b_szt.bf16(pb * 2048, 2048))], waits=(halves[-1],))
                    xz_sp.append(s_sp)
                    continue
                dt_ = m
                wcv = halves + [s_cs]
                if prev_si is not None:
                    wcv.append(prev_si)
                s_cv = P.add("dve", lambda v, dt_=dt_, pb=pb: [
                    v.memset(b_xraw.f32(pb * 2056, 4), 0.0),
                    v.tensor_scalar(out=b_ctmp.f32(0, 2048), in0=b_xraw.f32(pb * 2056 + 1, 2048),
                                    scalar1=cw(dt_, 0)[:, 0:1], scalar2=None, op0=ALU.mult),
                    v.scalar_tensor_tensor(out=b_ctmp.f32(2048, 2048), in0=b_xraw.f32(pb * 2056 + 2, 2048),
                                           scalar=cw(dt_, 1)[:, 0:1], in1=b_ctmp.f32(0, 2048),
                                           op0=ALU.mult, op1=ALU.add),
                    v.scalar_tensor_tensor(out=b_ctmp.f32(0, 2048), in0=b_xraw.f32(pb * 2056 + 3, 2048),
                                           scalar=cw(dt_, 2)[:, 0:1], in1=b_ctmp.f32(2048, 2048),
                                           op0=ALU.mult, op1=ALU.add),
                    v.scalar_tensor_tensor(out=b_cfin.f32(pb * 2048, 2048), in0=b_xraw.f32(pb * 2056 + 4, 2048),
                                           scalar=cw(dt_, 3)[:, 0:1], in1=b_ctmp.f32(0, 2048),
                                           op0=ALU.mult, op1=ALU.add)], waits=tuple(wcv))
                s_si = P.add("act", lambda s, dt_=dt_, pb=pb: s.activation(
                    out=b_uf.f32(0, 2048), in_=b_cfin.f32(pb * 2048, 2048), func=AF.Silu,
                    bias=cb(dt_)[:, 0:1]), waits=(s_cv,))
                prev_si = s_si
                w_us = [s_si]
                if len(u_done) >= 2:
                    w_us.append(xz_sp[-2])
                s_us = P.add("dve", lambda v, dt_=dt_, pb=pb: [
                    v.tensor_copy(out=b_uhis.bf16(pb * 2048, 2048), in_=b_uf.f32(0, 2048)),
                    v.tensor_sub(out=b_ulo.bf16(pb * 2048, 2048), in0=b_uf.f32(0, 2048),
                                 in1=b_uhis.bf16(pb * 2048, 2048))],
                    waits=tuple(w_us))
                s_ul = P.dma("gq", [(spill_ulo[dt_, :, :], b_ulo.bf16(pb * 2048, 2048)),
                                    (spill_uhi[dt_, :, :], b_uhis.bf16(pb * 2048, 2048))], waits=(s_us,))
                xz_sp.append(s_ul)
                u_done.append(s_us)
            A.release(b_xnfm)
            s_b1 = xz_sp[-1]

            barrier()
            b_xpw = Buf(big, b_xn.off, 768)
            b_xps = Buf(big, b_xn.off + 768, 96)
            b_bclo = Buf(big, b_xn.off + 1024, 2048)
            b_dtr = A.alloc(2 * 1024)
            b_bcs = A.alloc(2 * 1024)
            dtr = lambda br: b_dtr.bf16(br * 2048, 2048)[0:64, :]
            bcs = lambda br: b_bcs.bf16(br * 2048, 2048)[0:64, :]
            b_upx = A.alloc(2048)
            xp_done = []
            for br in range(2):
                xdt_h = xdtT1_h if br == 0 else xdtT2_h
                xbc_h = xbcT1_h if br == 0 else xbcT2_h
                prev = (u_done[-1], xz_sp[-1]) if br == 0 else (xp_done[-1],)
                for k in range(NDI):
                    sl = P.dma("sp", [(b_xps.f32(0, 64), xdt_h[k * 128:(k + 1) * 128, :]),
                                      (b_xps.f32(64, 32), xbc_h[k * 128:(k + 1) * 128, :])], waits=prev)
                    sc = P.add("dve", lambda v, k=k: v.tensor_copy(out=b_xpw.bf16(k * 96, 96),
                                                                   in_=b_xps.f32(0, 96)), waits=(sl,))
                    prev = (sc,)
                xpw = lambda k: b_xpw.bf16(k * 96, 96)
                mm_steps = []
                for k in range(NDI):
                    w = [prev[0], xz_sp[-1]]
                    if len(mm_steps) >= 2:
                        w.append(mm_steps[-2][0])
                    s_uk = P.dma("aq", [(b_upx.bf16((k % 2) * 2048, 2048), spill_uhi[k, :, :])],
                                 waits=tuple(w))
                    s_mm = P.add("pe", lambda t, k=k: [
                        t.matmul(out=psA[0:64, q4 * 512:(q4 + 1) * 512], lhsT=xpw(k)[:, 0:64],
                                 rhs=b_upx.bf16((k % 2) * 2048 + q4 * 512, 512),
                                 start=(k == 0), stop=(k == NDI - 1))
                        for q4 in range(4)] + [
                        t.matmul(out=psB[0:32, q4 * 512:(q4 + 1) * 512], lhsT=xpw(k)[:, 64:96],
                                 rhs=b_upx.bf16((k % 2) * 2048 + q4 * 512, 512),
                                 start=(k == 0), stop=(k == NDI - 1))
                        for q4 in range(4)], waits=(s_uk,) + ((mm_steps[-1][1],) if mm_steps else ()))
                    mm_steps.append((s_mm, s_mm))
                s_dt = mm_steps[-1][0]
                s_de = P.add("dve", lambda v, br=br: v.tensor_copy(out=dtr(br), in_=psA[0:64, 0:2048]),
                             waits=(s_dt,))
                s_bc = s_dt
                s_be = P.add("dve", lambda v, br=br: [
                    v.tensor_copy(out=bcs(br)[0:32, :], in_=psB[0:32, 0:2048]),
                    v.tensor_sub(out=b_bclo.bf16(0, 2048)[0:32, :], in0=psB[0:32, 0:2048],
                                 in1=bcs(br)[0:32, :])], waits=(s_bc,))
                s_bl0 = P.dma("gq", [(bxfer[:, :], b_bclo.bf16(0, 2048)[0:32, :])], waits=(s_be,))
                s_bl = P.dma("gq", [(b_bcs.bf16(br * 2048, 2048)[32:64, :], bxfer[:, :])], waits=(s_bl0,))
                xp_done.append(s_bl)


            b_dtw = A.alloc(2 * 1024)
            b_dws = Buf(big, b_xn.off + 3072, 2048)
            dtw = lambda br: b_dtw.bf16(br * 2048, 2048)[0:64, :]
            s_d1 = P.dma("sp", [(b_dws.f32(0, 2048)[0:64, :], dtT1_h[:, :])], waits=(xp_done[-1],))
            s_c1 = P.add("dve", lambda v: v.tensor_copy(out=dtw(0), in_=b_dws.f32(0, 2048)[0:64, :]),
                         waits=(s_d1,))
            s_d2 = P.dma("sp", [(b_dws.f32(0, 2048)[0:64, :], dtT2_h[:, :])], waits=(s_c1,))
            s_c2 = P.add("dve", lambda v: v.tensor_copy(out=dtw(1), in_=b_dws.f32(0, 2048)[0:64, :]),
                         waits=(s_d2,))
            s_dtw = s_c2

            barrier()
            b_dl = A.alloc(2048)
            b_du = A.alloc(2048)
            b_spt = A.alloc(2048)
            b_dA = Buf(big, b_xres.off, 4096)
            b_dBu = Buf(big, b_xres.off + 4096, 4096)
            b_h = Buf(big, b_xres.off + 8192, 4096)
            b_ypp = Buf(big, b_xres.off + 12288, 4096)
            b_hC = A.alloc(2 * 2048)
            b_up2 = A.alloc(2048)
            b_szt2 = Buf(big, b_xn.off, 2048)
            b_yft = Buf(big, b_xn.off + 2048, 2048)

            yfin_sp = []
            prev_tail = None
            scan_hist = {}
            dbu_hist = {}
            hc_hist = {}
            ya_hist = []
            for dt_ in range(NDI):
                pbt = dt_ % 2
                w = [s_b1, s_dtw]
                if prev_tail:
                    w.append(prev_tail)
                s_ulo = P.dma("sp", [(b_up2.bf16(0, 2048), spill_uhi[dt_, :, :]),
                                     (b_up2.bf16(2048, 2048), spill_ulo[dt_, :, :])], waits=tuple(w))
                s_szl = P.dma("aq", [(b_szt2.bf16(pbt * 2048, 2048), spill_sz[dt_, :, :])], waits=tuple(w))
                s_yi = P.add("dve", lambda v, dt_=dt_: [
                    v.tensor_scalar(out=b_ypp.f32(0, 2048), in0=b_up2.bf16(0, 2048), scalar1=dD(dt_)[:, 0:1],
                                    scalar2=None, op0=ALU.mult),
                    v.scalar_tensor_tensor(out=b_ypp.f32(0, 2048), in0=b_up2.bf16(2048, 2048),
                                           scalar=dD(dt_)[:, 0:1], in1=b_ypp.f32(0, 2048),
                                           op0=ALU.mult, op1=ALU.add)], waits=(s_ulo, s_cs))
                ycur = 0
                last_ya = s_yi
                for br in range(2):
                    w = [s_dtw, s_yi]
                    if dbu_hist:
                        w.append(max(dbu_hist.values()))
                    if prev_tail:
                        w.append(prev_tail)
                    s_dp = P.add("pe", lambda t, dt_=dt_, br=br: [
                        t.matmul(out=psA[:, q4 * 512:(q4 + 1) * 512],
                                 lhsT=dtw(br)[:, dt_ * 128:(dt_ + 1) * 128],
                                 rhs=dtr(br)[:, q4 * 512:(q4 + 1) * 512], start=True, stop=True)
                        for q4 in range(4)], waits=tuple(w))
                    s_sp2 = P.add("act", lambda s, dt_=dt_, br=br: [
                        s.activation(out=b_spt.f32(0, 2048), in_=psA[:, 0:2048], func=AF.Abs,
                                     bias=dtb(dt_, br)[:, 0:1]),
                        s.activation(out=b_spt.f32(0, 2048), in_=b_spt.f32(0, 2048), func=AF.Exp, scale=-1.0),
                        s.activation(out=b_spt.f32(0, 2048), in_=b_spt.f32(0, 2048), func=AF.Ln, bias=b_col.f32(31, 1)[:, 0:1]),
                        s.activation(out=b_dl.f32(0, 2048), in_=psA[:, 0:2048], func=AF.Relu,
                                     bias=dtb(dt_, br)[:, 0:1])], waits=(s_dp,))
                    s_df = P.add("dve", lambda v, dt_=dt_: [
                        v.tensor_add(out=b_dl.f32(0, 2048), in0=b_dl.f32(0, 2048), in1=b_spt.f32(0, 2048)),
                        v.scalar_tensor_tensor(out=b_du.f32(0, 2048), in0=b_up2.bf16(0, 2048), scalar=1.0,
                                               in1=b_dl.f32(0, 2048), op0=ALU.mult, op1=ALU.mult),
                        v.scalar_tensor_tensor(out=b_spt.f32(0, 2048), in0=b_up2.bf16(2048, 2048), scalar=1.0,
                                               in1=b_dl.f32(0, 2048), op0=ALU.mult, op1=ALU.mult),
                        v.tensor_add(out=b_du.f32(0, 2048), in0=b_du.f32(0, 2048), in1=b_spt.f32(0, 2048))],
                        waits=(s_sp2,))
                    for n in range(DS):
                        nb = n % 2
                        w = [s_df]
                        if nb in scan_hist:
                            w.append(scan_hist[nb])
                        s_dA = P.add("act", lambda s, dt_=dt_, br=br, n=n, nb=nb: s.activation(
                            out=b_dA.f32(nb * 2048, 2048), in_=b_dl.f32(0, 2048), func=AF.Exp,
                            scale=Aneg(dt_, br, n)[:, 0:1]), waits=tuple(w))
                        wb = [s_sc, s_df]
                        ob = 1 - nb
                        if ob in dbu_hist:
                            wb.append(dbu_hist[ob])
                        if ob in hc_hist:
                            wb.append(hc_hist[ob])
                        s_bb = P.add("pe", lambda t, br=br, n=n: [
                            t.matmul(out=psA[:, q4 * 512:(q4 + 1) * 512], lhsT=selB(n),
                                     rhs=bcs(br)[:, q4 * 512:(q4 + 1) * 512], start=True, stop=True)
                            for q4 in range(4)] + [
                            t.matmul(out=psB[:, q4 * 512:(q4 + 1) * 512], lhsT=selC(n),
                                     rhs=bcs(br)[:, q4 * 512:(q4 + 1) * 512], start=True, stop=True)
                            for q4 in range(4)], waits=tuple(wb))
                        sgn = 1.0 if br == 0 else -1.0
                        w = [s_bb, s_dA]
                        if nb in scan_hist:
                            w.append(scan_hist[nb])
                        if len(ya_hist) >= 2:
                            w.append(ya_hist[-2])
                        s_hc = P.add("dve", lambda v, nb=nb, sgn=sgn: [
                            v.tensor_tensor(out=b_dBu.f32(nb * 2048, 2048), in0=b_du.f32(0, 2048),
                                            in1=psA[:, 0:2048], op=ALU.mult),
                            v.tensor_tensor_scan(out=b_h.f32(nb * 2048, 2048),
                                                 data0=b_dA.f32(nb * 2048, 2048),
                                                 data1=b_dBu.f32(nb * 2048, 2048), initial=0.0,
                                                 op0=ALU.mult, op1=ALU.add),
                            v.scalar_tensor_tensor(out=b_hC.f32(nb * 2048, 2048),
                                                   in0=b_h.f32(nb * 2048, 2048), scalar=sgn,
                                                   in1=psB[:, 0:2048], op0=ALU.mult, op1=ALU.mult)],
                            waits=tuple(w))
                        dbu_hist[nb] = s_hc
                        scan_hist[nb] = s_hc
                        hc_hist[nb] = s_hc
                        s_ya = P.add("pool", lambda g, nb=nb, yc=ycur: g.tensor_add(
                            out=b_ypp.f32((1 - yc) * 2048, 2048), in0=b_ypp.f32(yc * 2048, 2048),
                            in1=b_hC.f32(nb * 2048, 2048)), waits=(s_hc, last_ya))
                        ycur = 1 - ycur
                        last_ya = s_ya
                        ya_hist.append(s_ya)
                s_gate = P.add("dve", lambda v, yc=ycur, pbt=pbt: v.tensor_tensor(
                    out=b_yft.bf16(pbt * 2048, 2048), in0=b_ypp.f32(yc * 2048, 2048),
                    in1=b_szt2.bf16(pbt * 2048, 2048), op=ALU.mult), waits=(last_ya, s_szl))
                s_ysp = P.dma("gq", [(spill_yfin[dt_, :, :], b_yft.bf16(pbt * 2048, 2048))], waits=(s_gate,))
                prev_tail = s_ysp
                yfin_sp.append(s_ysp)

            A.release(b_dl, b_du, b_spt, b_hC, b_up2, b_upx, b_uhis,
                      b_dtr, b_bcs, b_dtw, b_selbf, b_cs, b_bs)

            barrier()
            b_opw = A.alloc(NDI * 512)
            b_ops = A.alloc(1024)
            b_ysb = A.alloc(2 * 1024)
            b_xrl = A.alloc(2 * 1024)
            opw = lambda k: b_opw.bf16(k * 1024, 1024)
            prev = (yfin_sp[-1],)
            for k in range(NDI):
                prev = (load_cast(b_opw, k * 1024, outprojT_h[k * 128:(k + 1) * 128, :], b_ops, 1024, prev),)
            s_op = prev[0]
            res = []
            for tt in range(NT):
                pb = tt % 2
                w = [s_op]
                if tt >= 2:
                    w.append(res[tt - 2])
                s_yl = P.dma("sp", [(b_ysb.bf16(pb * 2048 + k * 128, 128), spill_yfin[k, :, tt * 128:(tt + 1) * 128])
                                    for k in range(NDI)], waits=tuple(w))
                s_xl = P.dma("aq", [(b_xrl.f32(pb * 1024, 1024), spill_x[tt, :, :])], waits=tuple(w))
                pso = psA[:, pb * 1024:pb * 1024 + 1024]
                s_om = P.add("pe", lambda t, pb=pb, pso=pso: [
                    t.matmul(out=pso[:, hf * 512:(hf + 1) * 512], lhsT=b_ysb.bf16(pb * 2048 + k * 128, 128),
                             rhs=opw(k)[:, hf * 512:(hf + 1) * 512], start=(k == 0), stop=(k == NDI - 1))
                    for k in range(NDI) for hf in range(2)], waits=(s_yl,))
                s_rs = P.add("dve", lambda v, tt=tt, pb=pb, pso=pso: v.tensor_tensor(
                    out=xres(tt), in0=pso[:, 0:1024], in1=b_xrl.f32(pb * 1024, 1024), op=ALU.add),
                    waits=(s_om, s_xl))
                res.append(s_rs)
            A.release(b_opw, b_ops, b_ysb, b_xrl)
            return res

        import sys as _sys
        def mark(nm):
            print(f"[mark] {nm}: step {len(P.steps)}", file=_sys.stderr)

        last = emit_attn_a(emit_ln((s_base,)))
        if N_SUB >= 2:
            barrier()
            last = emit_ffn(emit_ln((last[-1],)), which=0)
        if N_SUB >= 3:
            barrier()
            last = emit_attn_b(emit_ln((last[-1],)))
        if N_SUB >= 4:
            barrier()
            last = emit_ffn(emit_ln((last[-1],)), which=1)

        mark("store")
        s_st = P.dma("gq", [(out_h[tt * 128:(tt + 1) * 128, :], xres(tt)) for tt in range(NT)],
                     waits=(last[-1],))
        P.add("pool", lambda g: g.engine_nop(), waits=(s_st,))

        P.resolve()

        @block.sync
        def _(e):
            P.emit_for(e, {"sp"}, sems)

        @block.tensor
        def _(e):
            P.emit_for(e, {"pe"}, sems)

        @block.vector
        def _(e):
            P.emit_for(e, {"dve"}, sems)

        @block.scalar
        def _(e):
            P.emit_for(e, {"act", "aq"}, sems)

        @block.gpsimd
        def _(e):
            P.emit_for(e, {"pool", "gq"}, sems)

    return nc


def _prep_inputs(inputs):
    g = lambda k: np.asarray(inputs[k], dtype=np.float32)
    b_attn = g("L0_attn_a_b_attn")
    common = {
        "W_attn": np.ascontiguousarray(g("L0_attn_a_W_attn")),
        "battn_qk": np.ascontiguousarray(np.concatenate(
            [b_attn[0:D].reshape(ND, 128).T, b_attn[D:2 * D].reshape(ND, 128).T], axis=1)),
        "bv_row": np.ascontiguousarray(b_attn[2 * D:3 * D].reshape(1, D)),
        "W_proj": np.ascontiguousarray(g("L0_attn_a_W_proj")),
        "bproj_row": np.ascontiguousarray(g("L0_attn_a_b_proj").reshape(1, D)),
        "negmask": np.where(np.arange(128)[None, :] <= np.arange(128)[:, None], 0.0, -1e9).astype(np.float32),
        "trimask": (np.arange(128)[:, None] <= np.arange(128)[None, :]).astype(np.float32),
        "eye": np.eye(128, dtype=np.float32),
        "WrT": np.ascontiguousarray(g("L0_ffn_W_r").T),
        "WkT": np.ascontiguousarray(g("L0_ffn_W_k").T),
        "WvT": np.ascontiguousarray(g("L0_ffn_W_v").T),
        "mur_col": np.ascontiguousarray(g("L0_ffn_mu_r").reshape(ND, 128).T),
        "muk_col": np.ascontiguousarray(g("L0_ffn_mu_k").reshape(ND, 128).T),
        "inprojT": np.ascontiguousarray(g("L0_attn_b_in_proj_w").T),
        "convw": np.ascontiguousarray(g("L0_attn_b_conv_w").reshape(NDI, 128, DCONV)),
        "convb": np.ascontiguousarray(g("L0_attn_b_conv_b").reshape(NDI, 128, 1)),
        "Alog1": np.ascontiguousarray(g("L0_attn_b_A1_log").reshape(NDI, 128, DS)),
        "Alog2": np.ascontiguousarray(g("L0_attn_b_A2_log").reshape(NDI, 128, DS)),
        "D1": np.ascontiguousarray(g("L0_attn_b_D1").reshape(NDI, 128, 1)),
        "D2": np.ascontiguousarray(g("L0_attn_b_D2").reshape(NDI, 128, 1)),
        "xdtT1": np.ascontiguousarray(g("L0_attn_b_x1_proj_w")[0:DTR].T),
        "xbcT1": np.ascontiguousarray(g("L0_attn_b_x1_proj_w")[DTR:].T),
        "xdtT2": np.ascontiguousarray(g("L0_attn_b_x2_proj_w")[0:DTR].T),
        "xbcT2": np.ascontiguousarray(g("L0_attn_b_x2_proj_w")[DTR:].T),
        "dtT1": np.ascontiguousarray(g("L0_attn_b_dt1_proj_w").T),
        "dtT2": np.ascontiguousarray(g("L0_attn_b_dt2_proj_w").T),
        "dtb1": np.ascontiguousarray(g("L0_attn_b_dt1_proj_b").reshape(NDI, 128, 1)),
        "dtb2": np.ascontiguousarray(g("L0_attn_b_dt2_proj_b").reshape(NDI, 128, 1)),
        "outprojT": np.ascontiguousarray(g("L0_attn_b_out_proj_w").T),
    }
    sel = np.zeros((64, 2 * DS * 128), np.float32)
    for n in range(DS):
        sel[n, n * 128:(n + 1) * 128] = 1.0
        sel[32 + n, n * 128:(n + 1) * 128] = 1.0
        sel[16 + n, 2048 + n * 128:2048 + (n + 1) * 128] = 1.0
        sel[48 + n, 2048 + n * 128:2048 + (n + 1) * 128] = 1.0
    common["sel"] = sel
    x_full = g("L0_attn_a_x")
    return [dict(common, x=np.ascontiguousarray(x_full[c])) for c in range(B)]


def kernel(**inputs):
    maps = _prep_inputs(inputs)
    nc = build_program()
    res = run_bass_kernel_spmd(nc, maps, core_ids=list(range(B)))
    try:
        kernel._last_dbg = [np.asarray(r["dbg"]) for r in res.results]
    except Exception:
        kernel._last_dbg = None
    return np.stack([np.asarray(r["out"]) for r in res.results], axis=0).astype(np.float32)


# version 03
# speedup vs baseline: 1.0552x; 1.0479x over previous
import numpy as np
import concourse.bass as bass
import concourse.mybir as mybir
from concourse.bass_utils import run_bass_kernel_spmd
from contextlib import ExitStack

F32 = mybir.dt.float32
BF16 = mybir.dt.bfloat16
AF = mybir.ActivationFunctionType
ALU = mybir.AluOpType
AX = mybir.AxisListType

B, T, D = 8, 2048, 1024
H = 16
DI, DS, DTR, DCONV = 2048, 16, 64, 4
NT, ND, NDI = T // 128, D // 128, DI // 128
EPS = 1e-5
N_SUB = 4
DEBUG_TAP = None


class Plan:
    ENG = ("sp", "pe", "dve", "act", "pool", "gq", "aq")
    SERIAL = False
    SERIAL_FROM = 10**9
    SERIAL_UNTIL = -1

    def __init__(self):
        self.steps = []
        self.waited = set()

    def last_per_engine(self):
        out = {}
        for sid, st in enumerate(self.steps):
            out[st[0]] = sid
        return out

    def add(self, engine, fn, waits=(), n_dma=0, hard=()):
        sid = len(self.steps)
        for w in tuple(waits) + tuple(hard):
            assert w is not None and 0 <= w < sid, (w, sid)
            self.waited.add(w)
        self.steps.append((engine, fn, tuple(waits), n_dma, tuple(hard)))
        return sid

    def dma(self, engine, pairs, waits=()):
        def fn(eng):
            return [eng.dma_start(out=d, in_=s) for d, s in pairs]
        return self.add(engine, fn, waits=waits, n_dma=len(pairs))

    def resolve(self):
        if self.SERIAL or self.SERIAL_UNTIL >= 0 or self.SERIAL_FROM < len(self.steps):
            self.waited = set(range(len(self.steps)))
        cnt = {e: 0 for e in self.ENG}
        self.sig = {}
        for sid, (e, fn, waits, n_dma, hard) in enumerate(self.steps):
            if sid in self.waited or n_dma:
                cnt[e] += 16 * n_dma if n_dma else 1
                self.sig[sid] = (e, cnt[e])

    def emit_for(self, eng, enames, sems):
        for sid, (e, fn, waits, n_dma, hard) in enumerate(self.steps):
            if e not in enames:
                continue
            need = {}
            if self.SERIAL or (self.SERIAL_FROM <= sid <= self.SERIAL_UNTIL):
                for w in range(sid):
                    we, wv = self.sig[w]
                    if self.steps[w][0] in enames and not self.steps[w][3]:
                        continue
                    if need.get(we, 0) < wv:
                        need[we] = wv
            for w in waits:
                we, wv = self.sig[w]
                if we in enames and not self.steps[w][3]:
                    continue
                if need.get(we, 0) < wv:
                    need[we] = wv
            for w in hard:
                we, wv = self.sig[w]
                if need.get(we, 0) < wv:
                    need[we] = wv
            for we in sorted(need):
                eng.wait_ge(sems[we], need[we])
            insts = fn(eng)
            if sid in self.waited or n_dma:
                if n_dma:
                    for di in insts:
                        di.then_inc(sems[e], 16)
                else:
                    last = insts[-1] if isinstance(insts, (list, tuple)) else insts
                    last.then_inc(sems[e], 1)


class Buf:
    __slots__ = ("t", "off", "words")

    def __init__(self, t, off, words):
        self.t, self.off, self.words = t, off, words

    def f32(self, c0=0, n=None):
        if n is None:
            n = self.words - c0
        return self.t[:, self.off + c0:self.off + c0 + n]

    def bf16(self, c0=0, n=None):
        if n is None:
            n = 2 * self.words - c0
        return self.t[:, self.off:self.off + self.words].bitcast(BF16)[:, c0:c0 + n]


class Arena:
    def __init__(self, tensor, words):
        self.t, self.words = tensor, words
        self.free = [(0, words)]
        self.peak = 0

    def alloc(self, words):
        words = (words + 1) & ~1
        for i, (off, sz) in enumerate(self.free):
            if sz >= words:
                if sz == words:
                    self.free.pop(i)
                else:
                    self.free[i] = (off + words, sz - words)
                self.peak = max(self.peak, self.words - sum(s for _, s in self.free))
                b = Buf(self.t, off, words)
                import traceback
                caller = traceback.extract_stack()[-2]
                self.live_dbg = getattr(self, "live_dbg", {})
                self.live_dbg[(off, words)] = caller.lineno
                return b
        raise MemoryError(f"arena OOM: want {words}, free {self.free}, live={sorted(getattr(self, 'live_dbg', {}).items())}")

    def release(self, *bufs):
        for b in bufs:
            self.live_dbg.pop((b.off, b.words), None)
            self.free.append((b.off, b.words))
        self.free.sort()
        out = []
        for o, s in self.free:
            if out and out[-1][0] + out[-1][1] == o:
                out[-1][1] += s
            else:
                out.append([o, s])
        self.free = [(o, s) for o, s in out]


def build_program():
    nc = bass.Bass("TRN2", target_bir_lowering=False)
    din = lambda name, shape, dt=F32: nc.dram_tensor(name, list(shape), dt, kind="ExternalInput").ap()

    x_h = din("x", (T, D))
    Wattn_h = din("W_attn", (D, 3 * D))
    battnqk_h = din("battn_qk", (128, 16))
    bvrow_h = din("bv_row", (1, D))
    Wproj_h = din("W_proj", (D, D))
    bprow_h = din("bproj_row", (1, D))
    negmask_h = din("negmask", (128, 128))
    trimask_h = din("trimask", (128, 128))
    eye_h = din("eye", (128, 128))
    WrT_h = din("WrT", (D, D))
    WkT_h = din("WkT", (D, D))
    WvT_h = din("WvT", (D, D))
    murcol_h = din("mur_col", (128, ND))
    mukcol_h = din("muk_col", (128, ND))
    inprojT_h = din("inprojT", (D, 2 * DI))
    convw_h = din("convw", (NDI, 128, DCONV))
    convb_h = din("convb", (NDI, 128, 1))
    Alog1_h = din("Alog1", (NDI, 128, DS))
    Alog2_h = din("Alog2", (NDI, 128, DS))
    D1_h = din("D1", (NDI, 128, 1))
    D2_h = din("D2", (NDI, 128, 1))
    xdtT1_h = din("xdtT1", (DI, DTR))
    xbcT1_h = din("xbcT1", (DI, 2 * DS))
    xdtT2_h = din("xdtT2", (DI, DTR))
    xbcT2_h = din("xbcT2", (DI, 2 * DS))
    dtT1_h = din("dtT1", (DTR, DI))
    dtT2_h = din("dtT2", (DTR, DI))
    dtb1_h = din("dtb1", (NDI, 128, 1))
    dtb2_h = din("dtb2", (NDI, 128, 1))
    sel_h = din("sel", (64, 2 * DS * 128))
    outprojT_h = din("outprojT", (DI, D))
    out_h = nc.dram_tensor("out", [T, D], F32, kind="ExternalOutput").ap()
    dbg_h = nc.dram_tensor("dbg", [128, 2048], F32, kind="ExternalOutput").ap()

    spill_x = nc.dram_tensor("spill_x", [NT, 128, D], F32, kind="Internal").ap()
    spill_ulo = nc.dram_tensor("spill_ulo", [NDI, 128, T], BF16, kind="Internal").ap()
    spill_uhi = nc.dram_tensor("spill_uhi", [NDI, 128, T], BF16, kind="Internal").ap()
    spill_sz = nc.dram_tensor("spill_sz", [NDI, 128, T], BF16, kind="Internal").ap()
    spill_yfin = nc.dram_tensor("spill_yfin", [NDI, 128, T], BF16, kind="Internal").ap()
    stash_ffnw = nc.dram_tensor("stash_ffnw", [3, D, D], BF16, kind="Internal").ap()
    mxfer = nc.dram_tensor("mxfer", [16, 128], BF16, kind="Internal").ap()
    rxfer = nc.dram_tensor("rxfer", [1, 1024], BF16, kind="Internal").ap()
    bxfer = nc.dram_tensor("bxfer", [32, 2048], BF16, kind="Internal").ap()

    P = Plan()
    with ExitStack() as ctx:
        part_words = nc.SBUF_PARTITION_SIZE_BYTES // 4 - 4608
        big = ctx.enter_context(nc.sbuf_tensor("arena", [128, part_words], F32))
        A = Arena(big, part_words)
        psA = ctx.enter_context(nc.psum_tensor("psA", [128, 2048], F32))
        psB = ctx.enter_context(nc.psum_tensor("psB", [128, 2048], F32))
        sems = {e: ctx.enter_context(nc.semaphore(f"s_{e}")) for e in Plan.ENG}
        block = ctx.enter_context(nc.Block())

        b_eye = A.alloc(64)
        b_eyef = A.alloc(128)
        b_negm = A.alloc(128)
        b_trim = A.alloc(64)
        b_ones2 = A.alloc(64)
        b_xres = A.alloc(NT * 1024)
        b_xn = A.alloc(NT * 512)
        b_col = A.alloc(64)
        b_cst = A.alloc(128)

        xres = lambda tt: b_xres.f32(tt * 1024, 1024)
        xn_t = lambda tt: b_xn.bf16(tt * 1024, 1024)
        eye_bf = b_eye.bf16(0, 128)
        ones2 = b_ones2.bf16(0, 128)[0:2, :]

        s_x = P.dma("sp", [(xres(tt), x_h[tt * 128:(tt + 1) * 128, :]) for tt in range(NT)])
        s_e32 = P.dma("sp", [(b_cst.f32(0, 128), eye_h[:, :]), (b_eyef.f32(0, 128), eye_h[:, :])])
        s_eye = P.add("dve", lambda v: v.tensor_copy(out=eye_bf, in_=b_cst.f32(0, 128)), waits=(s_e32,))
        s_n32 = P.dma("sp", [(b_negm.f32(0, 128), negmask_h[:, :])], waits=(s_eye,))
        s_t32 = P.dma("sp", [(b_cst.f32(0, 128), trimask_h[:, :])], waits=(s_eye,))
        s_tri = P.add("dve", lambda v: v.tensor_copy(out=b_trim.bf16(0, 128), in_=b_cst.f32(0, 128)), waits=(s_t32,))
        s_on2 = P.add("dve", lambda v: [v.memset(ones2, 1.0),
                                        v.memset(b_col.f32(30, 1), EPS),
                                        v.memset(b_col.f32(31, 1), 1.0)])
        s_base = P.add("dve", lambda v: v.engine_nop(), waits=(s_x, s_n32, s_tri, s_on2))

        def tap(name, ap, after_step):
            if DEBUG_TAP == name:
                n = ap.shape[-1]
                P.dma("gq", [(dbg_h[0:ap.shape[0], 0:n], ap)], waits=(after_step,))

        def emit_ln(waits):
            outs = []
            st6 = b_col.f32(0, 12)
            mv = b_col.f32(12, 2)
            rstd = b_col.f32(14, 1)
            bneg = b_col.f32(15, 1)
            lnt = b_col.f32(16, 1)
            prev = None
            for tt in range(NT):
                w = tuple(waits) if prev is None else (prev,)
                s1 = P.add("dve", lambda v, tt=tt: [
                    v.bn_stats(out=b_col.f32(0, 6), in_=xres(tt)[:, 0:512]),
                    v.bn_stats(out=b_col.f32(6, 6), in_=xres(tt)[:, 512:1024]),
                    v.bn_aggr(out=b_col.f32(18, 2), in_=b_col.f32(0, 6)),
                    v.bn_aggr(out=b_col.f32(24, 2), in_=b_col.f32(6, 6))], waits=w)
                s1b = P.add("pool", lambda g: [
                    g.tensor_add(out=b_col.f32(26, 2), in0=b_col.f32(18, 2), in1=b_col.f32(24, 2)),
                    g.tensor_sub(out=b_col.f32(28, 1), in0=b_col.f32(18, 1), in1=b_col.f32(24, 1)),
                    g.tensor_scalar(out=mv[:, 0:1], in0=b_col.f32(26, 1), scalar1=0.5, scalar2=None,
                                    op0=ALU.mult),
                    g.tensor_tensor(out=b_col.f32(29, 1), in0=b_col.f32(28, 1), in1=b_col.f32(28, 1),
                                    op=ALU.mult),
                    g.tensor_scalar(out=b_col.f32(29, 1), in0=b_col.f32(29, 1), scalar1=0.25, scalar2=None,
                                    op0=ALU.mult),
                    g.tensor_scalar(out=b_col.f32(17, 1), in0=b_col.f32(27, 1), scalar1=0.5, scalar2=None,
                                    op0=ALU.mult),
                    g.tensor_add(out=mv[:, 1:2], in0=b_col.f32(17, 1), in1=b_col.f32(29, 1))], waits=(s1,))
                s2 = P.add("act", lambda s: [s.activation(out=lnt, in_=mv[:, 1:2], func=AF.Ln, bias=b_col.f32(30, 1)[:, 0:1]),
                                             s.activation(out=rstd, in_=lnt, func=AF.Exp, scale=-0.5)],
                           waits=(s1b,))
                s2b = P.add("pool", lambda g: [
                    g.tensor_tensor(out=bneg, in0=mv[:, 0:1], in1=rstd, op=ALU.mult),
                    g.tensor_scalar(out=bneg, in0=bneg, scalar1=-1.0, scalar2=None, op0=ALU.mult)],
                    waits=(s2,))
                s3 = P.add("dve", lambda v, tt=tt: v.tensor_scalar(
                    out=xn_t(tt), in0=xres(tt), scalar1=rstd[:, 0:1],
                    scalar2=bneg[:, 0:1], op0=ALU.mult, op1=ALU.add), waits=(s2b,))
                prev = s3
                outs.append(s3)
            return outs

        dummy_tp = lambda t: t.transpose(out=psB[:, 1920:1984].bitcast(BF16)[:, 0:128],
                                         in_=eye_bf, identity=eye_bf)

        def barrier():
            lasts = [s for s in P.last_per_engine().values()]
            g = P.add("dve", lambda v: v.engine_nop(), waits=tuple(lasts))
            bsteps = {"dve": g}
            bsteps["pool"] = P.add("pool", lambda p: p.engine_nop(), waits=(g,))
            bsteps["act"] = P.add("act", lambda s: s.copy(out=b_col.f32(33, 1), in_=b_col.f32(31, 1)),
                                  waits=(g,))
            bsteps["pe"] = P.add("pe", lambda t: dummy_tp(t), waits=(g,))
            return [g, bsteps["pool"], bsteps["act"], bsteps["pe"]]

        def tok2fm(src_tile, dst_buf, waits):
            ps_bf = psB[:, 0:256].bitcast(BF16)
            last = None
            prev_ev = None
            for f in range(ND):
                for g in range(0, NT, 4):
                    w = list(waits) + ([prev_ev] if prev_ev else [])
                    s_tp = P.add("pe", lambda t, f=f, g=g: [
                        t.transpose(out=ps_bf[:, u * 128:(u + 1) * 128],
                                    in_=src_tile(g + u)[:, f * 128:(f + 1) * 128],
                                    identity=eye_bf) for u in range(4)] + [dummy_tp(t)], waits=tuple(w))
                    s_ev = P.add("dve", lambda v, f=f, g=g: v.tensor_copy(
                        out=dst_buf.bf16(f * 2048 + g * 128, 512), in_=ps_bf[:, 0:512]), waits=(s_tp,))
                    prev_ev = s_ev
                    last = s_ev
            return last

        def rowpair(dst_buf, src_row_h, row_stage, lo_stage, waits=()):
            sl = P.dma("sp", [(row_stage.f32(0, 1024)[0:1, :], src_row_h[0:1, :])], waits=waits)
            s_mk = P.add("dve", lambda v: [
                v.tensor_copy(out=dst_buf.bf16(0, 1024)[0:1, :], in_=row_stage.f32(0, 1024)[0:1, :]),
                v.tensor_sub(out=lo_stage.bf16(0, 1024)[0:1, :], in0=row_stage.f32(0, 1024)[0:1, :],
                             in1=dst_buf.bf16(0, 1024)[0:1, :])], waits=(sl,))
            s_lo0 = P.dma("gq", [(rxfer[0:1, :], lo_stage.bf16(0, 1024)[0:1, :])], waits=(s_mk,))
            s_lo = P.dma("gq", [(dst_buf.bf16(0, 1024)[1:2, :], rxfer[0:1, :])], waits=(s_lo0,))
            return s_lo

        def load_cast(dst_buf, dst_c0, src_ap, stage, nwords, waits, eng_c="dve"):
            sl = P.dma("sp", [(stage.f32(0, nwords), src_ap)], waits=waits)
            sc = P.add(eng_c, lambda v: v.tensor_copy(out=dst_buf.bf16(dst_c0, nwords), in_=stage.f32(0, nwords)),
                       waits=(sl,))
            return sc

        def emit_attn_a(ln_steps):
            b_xnfm = A.alloc(ND * 1024)
            xn_fm = lambda f, c0, n: b_xnfm.bf16(f * 2048 + c0, n)
            s_fm = tok2fm(xn_t, b_xnfm, tuple(ln_steps))
            tap("xn0", b_xn.f32(0, 512), ln_steps[-1])
            tap("xnfm0", b_xnfm.f32(0, 1024), s_fm)
            tap("xnfm7", b_xnfm.f32(7 * 1024, 1024), s_fm)

            mark("attn_a:vext")
            b_vext = A.alloc(NT * 520)
            vext = lambda tt: b_vext.bf16(tt * 1040, 1040)
            s_vi = P.add("dve", lambda v: v.memset(b_vext.bf16(0, NT * 1040), 0.0))
            s_vo = P.add("dve", lambda v: [
                v.memset(vext(tt).rearrange("p (h c) -> p h c", c=65)[:, :, 64:65], 1.0)
                for tt in range(NT)], waits=(s_vi,))

            b_wstage = A.alloc(1024)
            b_row = A.alloc(1024)
            b_lo = A.alloc(512)
            b_bvp = A.alloc(512)
            b_bpp = A.alloc(512)
            b_bqk = A.alloc(16)
            s_bv = rowpair(b_bvp, bvrow_h, b_row, b_lo)
            s_bp = rowpair(b_bpp, bprow_h, b_row, b_lo, waits=(s_bv,))
            s_bqk = P.dma("sp", [(b_bqk.f32(0, 16), battnqk_h[:, :])])

            b_wv = A.alloc(ND * 512)
            wv = lambda k: b_wv.bf16(k * 1024, 1024)
            prev = (s_fm,)
            for k in range(ND):
                prev = (load_cast(b_wv, k * 1024, Wattn_h[k * 128:(k + 1) * 128, 2048:3072],
                                  b_wstage, 1024, prev),)
            s_wv = prev[0]
            tap("wv01", b_wv.f32(0, 1024), s_wv)

            v_ev = []
            for tt in range(NT):
                pso = psA[:, (tt % 2) * 1024:(tt % 2) * 1024 + 1024]
                w = [s_wv, s_bv, s_vo, s_fm]
                if tt >= 2:
                    w.append(v_ev[tt - 2])
                s_mm = P.add("pe", lambda t, tt=tt, pso=pso: [
                    t.matmul(out=pso[:, hf * 512:(hf + 1) * 512],
                             lhsT=(xn_fm(k, tt * 128, 128) if k < ND else ones2),
                             rhs=(wv(k)[:, hf * 512:(hf + 1) * 512] if k < ND
                                  else b_bvp.bf16(0, 1024)[0:2, hf * 512:(hf + 1) * 512]),
                             start=(k == 0), stop=(k == ND))
                    for hf in range(2) for k in range(ND + 1)], waits=tuple(w))
                s_ev = P.add("dve", lambda v, tt=tt, pso=pso: v.tensor_copy(
                    out=vext(tt).rearrange("p (h c) -> p h c", c=65)[:, :, 0:64],
                    in_=pso.rearrange("p (h c) -> p h c", c=64)), waits=(s_mm,))
                if tt == 0:
                    tap("vext0", b_vext.f32(0, 520), s_ev)
                v_ev.append(s_ev)

            mark("attn_a:heads")
            bar = barrier()
            A.release(b_wv, b_wstage, b_row, b_lo)
            b_qk = A.alloc(4 * 1024)
            b_wqk32 = A.alloc(2048)
            b_wqkbf = A.alloc(1024)
            b_ytok = b_xn
            b_attT = A.alloc(512)
            b_mm = A.alloc(16)
            b_mmbf = A.alloc(16)
            b_mst = A.alloc(64)
            b_mrow = A.alloc(1024)
            qk_q = lambda pb, c0, n: b_qk.bf16(pb * 4096 + c0, n)
            qk_k = lambda pb, c0, n: b_qk.bf16(pb * 4096 + 2048 + c0, n)
            y_tok = lambda tt, c0, n: b_ytok.bf16(tt * 1024 + c0, n)
            attT = lambda ab, c0, n: b_attT.bf16(ab * 512 + c0, n)
            negm = b_negm.f32(0, 128)
            trim = b_trim.bf16(0, 128)
            mrow = b_mrow.bf16(0, 2048)[0:1, :]

            psA_free = [None]
            last_stm_prev = [None]
            m_guard = [None]
            m_guard2 = [None]
            qk_lastuse = {}
            wqk_lastcast = [None]
            last_head_done = [v_ev[-1]]
            psA_free[0] = v_ev[-1]

            for hp in range(H // 2):
                pb = hp % 2
                pq = [(b_wqk32.f32(c * 128, 128), Wattn_h[c * 128:(c + 1) * 128, hp * 128:(hp + 1) * 128])
                      for c in range(ND)]
                pk = [(b_wqk32.f32(1024 + c * 128, 128), Wattn_h[c * 128:(c + 1) * 128, D + hp * 128:D + (hp + 1) * 128])
                      for c in range(ND)]
                w = [s_bqk, v_ev[-1], s_bp, bar[0]]
                if wqk_lastcast[0] is not None:
                    w.append(wqk_lastcast[0])
                s_wl = P.dma("sp", pq + pk, waits=tuple(w))
                s_wc = P.add("dve", lambda v: v.tensor_copy(out=b_wqkbf.bf16(0, 2048), in_=b_wqk32.f32(0, 2048)),
                             waits=(s_wl,))
                wqk_lastcast[0] = s_wc
                wq = lambda c: b_wqkbf.bf16(c * 128, 128)
                wk = lambda c: b_wqkbf.bf16(1024 + c * 128, 128)

                w = [s_wc, s_fm]
                if pb in qk_lastuse:
                    w.append(qk_lastuse[pb])
                if psA_free[0] is not None:
                    w.append(psA_free[0])
                s_qmm = P.add("pe", lambda t: [
                    t.matmul(out=psA[:, q4 * 512:(q4 + 1) * 512], lhsT=wq(c), rhs=xn_fm(c, q4 * 512, 512),
                             start=(c == 0), stop=(c == ND - 1))
                    for q4 in range(4) for c in range(ND)], waits=tuple(w))
                s_qev = P.add("dve", lambda v, pb=pb, hp=hp: [
                    v.tensor_scalar(out=qk_q(pb, q4 * 512, 512), in0=psA[:, q4 * 512:(q4 + 1) * 512],
                                    scalar1=b_bqk.f32(hp, 1)[:, 0:1], scalar2=None, op0=ALU.add)
                    for q4 in range(4)], waits=(s_qmm,))
                s_kmm = P.add("pe", lambda t: [
                    t.matmul(out=psA[:, q4 * 512:(q4 + 1) * 512], lhsT=wk(c), rhs=xn_fm(c, q4 * 512, 512),
                             start=(c == 0), stop=(c == ND - 1))
                    for q4 in range(4) for c in range(ND)], waits=(s_qev,))
                s_kev = P.add("dve", lambda v, pb=pb, hp=hp: [
                    v.tensor_scalar(out=qk_k(pb, q4 * 512, 512), in0=psA[:, q4 * 512:(q4 + 1) * 512],
                                    scalar1=b_bqk.f32(8 + hp, 1)[:, 0:1], scalar2=None, op0=ALU.add)
                    for q4 in range(4)], waits=(s_kmm,))
                if hp == 0:
                    tap("q0", b_qk.f32(0, 1024), s_qev)
                    tap("k0", b_qk.f32(2048, 1024), s_kev)
                psA_free[0] = s_kev

                for half in range(2):
                    h = 2 * hp + half
                    e0 = 64 * half
                    prev_mx = None
                    for i in range(NT):
                        nk = (i + 1) * 128
                        w = [s_kev]
                        if prev_mx is not None:
                            w.append(prev_mx)
                        elif psA_free[0] is not None:
                            w.append(psA_free[0])
                        s_smm = P.add("pe", lambda t, i=i, nk=nk, pb=pb, e0=e0: [
                            t.matmul(out=psA[:, n0:min(nk, n0 + 512)],
                                     lhsT=qk_q(pb, i * 128, 128)[e0:e0 + 64, :],
                                     rhs=qk_k(pb, n0, min(nk, n0 + 512) - n0)[e0:e0 + 64, :],
                                     start=True, stop=True)
                            for n0 in range(0, nk, 512)], waits=tuple(w))
                        if i == 7:
                            tap(f"sc_h{h}", psA[:, 0:1024], s_smm)
                        w_mx = [s_smm]
                        if i == 0 and m_guard2[0] is not None:
                            w_mx.append(m_guard2[0])
                        s_mx = P.add("dve", lambda v, i=i, nk=nk: [
                            v.tensor_tensor(out=psA[:, i * 128:(i + 1) * 128],
                                            in0=psA[:, i * 128:(i + 1) * 128], in1=negm, op=ALU.add),
                            v.tensor_reduce(out=b_mm.f32(i, 1), in_=psA[:, 0:nk], axis=AX.X,
                                            op=ALU.max)], waits=tuple(w_mx))
                        prev_mx = s_mx
                    psA_free[0] = prev_mx
                    w_mn = [prev_mx]
                    if m_guard[0] is not None:
                        w_mn.append(m_guard[0])
                    s_mneg = P.add("pool", lambda g: g.tensor_scalar(
                        out=b_mmbf.f32(0, 16), in0=b_mm.f32(0, 16), scalar1=-1.0, scalar2=None,
                        op0=ALU.mult), waits=tuple(w_mn))
                    m_guard2[0] = s_mneg
                    ps_m = psB[:, 1536:1664]
                    s_mt = P.add("pe", lambda t: t.transpose(out=ps_m[0:16, 0:128], in_=b_mmbf.f32(0, 16),
                                                             identity=b_eyef.f32(0, 128)), waits=(s_mneg,))
                    s_ms = P.add("dve", lambda v: v.tensor_copy(out=b_mst.bf16(0, 128)[0:16, :],
                                                                in_=ps_m[0:16, 0:128]), waits=(s_mt,))
                    w_mr = [s_ms]
                    if last_stm_prev[0] is not None:
                        w_mr.append(last_stm_prev[0])
                    s_mr0 = P.dma("gq", [(mxfer[:, :], b_mst.bf16(0, 128)[0:16, :])], waits=tuple(w_mr))
                    m_guard[0] = s_mr0
                    s_mr = P.dma("gq", [(mrow, mxfer.rearrange("p f -> () (p f)"))], waits=(s_mr0,))

                    y_evs = []
                    exp_lastuse = {}
                    for i in range(NT):
                        psY = psB[:, 1040 + (i % 2) * 80:1040 + (i % 2) * 80 + 65]
                        n_grp = (i + 4) // 4
                        last_av = None
                        for gi, jg in enumerate(range(0, i + 1, 4)):
                            je = min(jg + 4, i + 1)
                            ab = gi % 2
                            pstile = psB[:, ab * 512:(ab + 1) * 512]
                            w = [s_mr]
                            if (ab in exp_lastuse):
                                w.append(exp_lastuse[ab])
                            s_stm = P.add("pe", lambda t, i=i, jg=jg, je=je, pstile=pstile, pb=pb, e0=e0: [
                                mm for j in range(jg, je) for mm in (
                                    t.matmul(out=pstile[:, (j - jg) * 128:(j - jg + 1) * 128],
                                             lhsT=qk_k(pb, j * 128, 128)[e0:e0 + 64, :],
                                             rhs=qk_q(pb, i * 128, 128)[e0:e0 + 64, :],
                                             start=True, stop=False),
                                    t.matmul(out=pstile[:, (j - jg) * 128:(j - jg + 1) * 128],
                                             lhsT=ones2[0:1, :],
                                             rhs=mrow[:, i * 128:(i + 1) * 128],
                                             start=False, stop=True))], waits=tuple(w))
                            s_ex = P.add("act", lambda s, jg=jg, je=je, ab=ab, pstile=pstile: s.activation(
                                out=attT(ab, 0, (je - jg) * 128), in_=pstile[:, 0:(je - jg) * 128],
                                func=AF.Exp, scale=0.125), waits=(s_stm,))
                            dep = s_ex
                            if je - 1 == i:
                                dep = P.add("dve", lambda v, i=i, jg=jg, ab=ab: v.tensor_tensor(
                                    out=attT(ab, (i - jg) * 128, 128), in0=attT(ab, (i - jg) * 128, 128),
                                    in1=trim, op=ALU.mult), waits=(s_ex,))
                            w2 = [dep]
                            if len(y_evs) >= 2:
                                w2.append(y_evs[i - 2])
                            last_stm_prev[0] = s_stm
                            s_av = P.add("pe", lambda t, i=i, jg=jg, je=je, ab=ab, psY=psY, h=h: [
                                t.matmul(out=psY, lhsT=attT(ab, (j - jg) * 128, 128),
                                         rhs=vext(j)[:, h * 65:h * 65 + 65],
                                         start=(j == 0), stop=(j == i)) for j in range(jg, je)],
                                waits=tuple(w2))
                            exp_lastuse[ab] = s_av
                            last_av = s_av
                        w3 = [last_av]
                        if len(y_evs) >= 1:
                            w3.append(y_evs[-1])
                        s_rc = P.add("dve", lambda v, psY=psY: v.reciprocal(
                            out=b_col.f32(22, 1), in_=psY[:, 64:65]), waits=tuple(w3))
                        s_ye = P.add("dve", lambda v, i=i, psY=psY, h=h: v.tensor_scalar(
                            out=y_tok(i, h * 64, 64), in0=psY[:, 0:64],
                            scalar1=b_col.f32(22, 1)[:, 0:1], scalar2=None, op0=ALU.mult),
                            waits=(), hard=(s_rc,))
                        y_evs.append(s_ye)
                    if h == 0:
                        tap("mrow0", b_mrow.f32(0, 1024), s_mr)
                        tap("mmat0", b_mm.f32(0, 16), s_mt)
                        tap("ytok0", b_xn.f32(0, 512), y_evs[-1])
                    qk_lastuse[pb] = y_evs[-1]
                    last_head_done[0] = y_evs[-1]

            mark("attn_a:proj")
            barrier()
            s_att_done = last_head_done[0]
            A.release(b_xnfm, b_vext, b_qk, b_wqk32, b_wqkbf, b_attT, b_mm, b_mmbf, b_mst, b_mrow)
            b_yfm = A.alloc(ND * 1024)
            y_fm = lambda f, c0, n: b_yfm.bf16(f * 2048 + c0, n)
            s_yfm = tok2fm(lambda tt: y_tok(tt, 0, 1024), b_yfm, (s_att_done,))

            b_wp = A.alloc(ND * 512)
            b_wps = A.alloc(1024)
            wp = lambda k: b_wp.bf16(k * 1024, 1024)
            prev = (s_att_done,)
            for k in range(ND):
                prev = (load_cast(b_wp, k * 1024, Wproj_h[k * 128:(k + 1) * 128, :], b_wps, 1024, prev),)
            s_wp = prev[0]
            res = []
            for tt in range(NT):
                pso = psA[:, (tt % 2) * 1024:(tt % 2) * 1024 + 1024]
                w = [s_wp, s_yfm, s_bp]
                if tt >= 2:
                    w.append(res[tt - 2])
                s_pm = P.add("pe", lambda t, tt=tt, pso=pso: [
                    t.matmul(out=pso[:, hf * 512:(hf + 1) * 512], lhsT=y_fm(k, tt * 128, 128),
                             rhs=wp(k)[:, hf * 512:(hf + 1) * 512], start=(k == 0), stop=False)
                    for k in range(ND) for hf in range(2)] + [
                    t.matmul(out=pso[:, hf * 512:(hf + 1) * 512], lhsT=ones2,
                             rhs=b_bpp.bf16(0, 1024)[0:2, hf * 512:(hf + 1) * 512],
                             start=False, stop=True) for hf in range(2)], waits=tuple(w))
                s_rs = P.add("dve", lambda v, tt=tt, pso=pso: v.tensor_tensor(
                    out=xres(tt), in0=pso[:, 0:1024], in1=xres(tt), op=ALU.add), waits=(s_pm,))
                if tt == 0:
                    tap("res0", b_xres.f32(0, 1024), s_rs)
                    tap("yfm0", b_yfm.f32(0, 1024), s_yfm)
                res.append(s_rs)
            A.release(b_bvp, b_bpp, b_bqk, b_yfm, b_wp, b_wps)
            return res

        def emit_ffn(ln_steps, which):
            mark(f"ffn{which}:start")
            tap(f"ffn{which}_xn0", b_xn.f32(0, 512), ln_steps[-1])
            b_mu = A.alloc(16)
            s_mul = P.dma("sp", [(b_mu.f32(0, ND), murcol_h[:, :]),
                                 (b_mu.f32(8, ND), mukcol_h[:, :])], waits=(ln_steps[-1],))
            mur = lambda f: b_mu.f32(0 + f, 1)
            muk = lambda f: b_mu.f32(8 + f, 1)

            b_xnfm2 = A.alloc(ND * 1024)
            s_fm2 = tok2fm(xn_t, b_xnfm2, tuple(ln_steps))
            xnf = lambda f, c0, n: b_xnfm2.bf16(f * 2048 + c0, n)
            b_xrfm = A.alloc(ND * 1024)
            b_xkfm = A.alloc(ND * 1024)
            b_mt = A.alloc(1024)
            xr_fm = lambda f, c0, n: b_xrfm.bf16(f * 2048 + c0, n)
            xk_fm = lambda f, c0, n: b_xkfm.bf16(f * 2048 + c0, n)
            mix_done = []
            for f in range(ND):
                s_mx1 = P.add("dve", lambda v, f=f: [
                    v.tensor_sub(out=b_mt.bf16(0, 2047), in0=xnf(f, 1, 2047), in1=xnf(f, 0, 2047)),
                    v.scalar_tensor_tensor(out=xr_fm(f, 1, 2047), in0=b_mt.bf16(0, 2047),
                                           scalar=mur(f)[:, 0:1], in1=xnf(f, 0, 2047),
                                           op0=ALU.mult, op1=ALU.add),
                    v.tensor_scalar(out=xr_fm(f, 0, 1), in0=xnf(f, 0, 1), scalar1=mur(f)[:, 0:1],
                                    scalar2=None, op0=ALU.mult),
                    v.scalar_tensor_tensor(out=xk_fm(f, 1, 2047), in0=b_mt.bf16(0, 2047),
                                           scalar=muk(f)[:, 0:1], in1=xnf(f, 0, 2047),
                                           op0=ALU.mult, op1=ALU.add),
                    v.tensor_scalar(out=xk_fm(f, 0, 1), in0=xnf(f, 0, 1), scalar1=muk(f)[:, 0:1],
                                    scalar2=None, op0=ALU.mult)],
                    waits=(s_fm2, s_mul) + ((mix_done[-1],) if mix_done else ()))
                mix_done.append(s_mx1)

            mark(f"ffn{which}:weights")
            barrier()
            A.release(b_xnfm2, b_mt)
            b_wf = A.alloc(ND * 512)
            b_wfs = A.alloc(1024)
            wf = lambda k: b_wf.bf16(k * 1024, 1024)

            def load_w(idx, src_h, waits):
                prev = tuple(waits)
                for k in range(ND):
                    if which == 0:
                        sc = load_cast(b_wf, k * 1024, src_h[k * 128:(k + 1) * 128, :], b_wfs, 1024, prev)
                        st = P.dma("gq", [(stash_ffnw[idx, k * 128:(k + 1) * 128, :], wf(k))], waits=(sc,))
                        prev = (sc, st)
                        last = st
                    else:
                        sl = P.dma("sp", [(wf(k), stash_ffnw[idx, k * 128:(k + 1) * 128, :])], waits=prev)
                        prev = (sl,)
                        last = sl
                return last

            r_tok = lambda tt, c0, n: b_xn.bf16(tt * 1024 + c0, n)
            lw = load_w(0, WrT_h, (mix_done[-1],))
            r_steps = []
            for tt in range(NT):
                pso = psA[:, (tt % 2) * 1024:(tt % 2) * 1024 + 1024]
                w = [lw, mix_done[-1]]
                if tt >= 2:
                    w.append(r_steps[tt - 2])
                s_rm = P.add("pe", lambda t, tt=tt, pso=pso: [
                    t.matmul(out=pso[:, hf * 512:(hf + 1) * 512], lhsT=xr_fm(k, tt * 128, 128),
                             rhs=wf(k)[:, hf * 512:(hf + 1) * 512], start=(k == 0), stop=(k == ND - 1))
                    for k in range(ND) for hf in range(2)], waits=tuple(w))
                s_rs = P.add("act", lambda s, tt=tt, pso=pso: s.activation(
                    out=r_tok(tt, 0, 1024), in_=pso[:, 0:1024], func=AF.Sigmoid), waits=(s_rm,))
                r_steps.append(s_rs)

            mark(f"ffn{which}:k")
            barrier()
            k2_fm = lambda f, c0, n: b_xrfm.bf16(f * 2048 + c0, n)
            tap(f"ffn{which}_r0", b_xn.f32(0, 512), r_steps[-1])
            tap(f"ffn{which}_xk0", b_xkfm.f32(0, 1024), r_steps[-1])
            tap(f"ffn{which}_xr0", b_xrfm.f32(0, 1024), mix_done[-1])
            b_krel = A.alloc(2048)
            lw2 = load_w(1, WkT_h, (r_steps[-1],))
            k_steps = []
            for m in range(ND):
                for toff in (0, 1024):
                    pso = psA[:, (len(k_steps) % 2) * 1024:(len(k_steps) % 2) * 1024 + 1024]
                    w = [lw2, r_steps[-1]]
                    if len(k_steps) >= 2:
                        w.append(k_steps[-2])
                    s_km = P.add("pe", lambda t, m=m, toff=toff, pso=pso: [
                        t.matmul(out=pso[:, q4 * 512:(q4 + 1) * 512], lhsT=wf(k)[:, m * 128:(m + 1) * 128],
                                 rhs=xk_fm(k, toff + q4 * 512, 512), start=(k == 0), stop=(k == ND - 1))
                        for k in range(ND) for q4 in range(2)], waits=tuple(w))
                    kp = len(k_steps) % 2
                    s_kr = P.add("dve", lambda v, pso=pso, kp=kp: v.tensor_scalar(
                        out=b_krel.f32(kp * 1024, 1024), in0=pso[:, 0:1024], scalar1=0.0, scalar2=None,
                        op0=ALU.max), waits=(s_km,))
                    s_sq = P.add("pool", lambda g, m=m, toff=toff, kp=kp: g.tensor_mul(
                        out=k2_fm(m, toff, 1024), in0=b_krel.f32(kp * 1024, 1024),
                        in1=b_krel.f32(kp * 1024, 1024)), waits=(s_kr,))
                    k_steps.append(s_sq)

            mark(f"ffn{which}:v")
            barrier()
            tap(f"ffn{which}_k20", b_xrfm.f32(0, 1024), k_steps[-1])
            b_g = A.alloc(1024)
            lw3 = load_w(2, WvT_h, (k_steps[-1],))
            res = []
            for tt in range(NT):
                pso = psA[:, (tt % 2) * 1024:(tt % 2) * 1024 + 1024]
                w = [lw3, k_steps[-1]]
                if tt >= 2:
                    w.append(res[tt - 2])
                s_vm = P.add("pe", lambda t, tt=tt, pso=pso: [
                    t.matmul(out=pso[:, hf * 512:(hf + 1) * 512], lhsT=k2_fm(k, tt * 128, 128),
                             rhs=wf(k)[:, hf * 512:(hf + 1) * 512], start=(k == 0), stop=(k == ND - 1))
                    for k in range(ND) for hf in range(2)], waits=tuple(w))
                s_g = P.add("dve", lambda v, tt=tt, pso=pso: [
                    v.tensor_tensor(out=b_g.f32(0, 1024), in0=pso[:, 0:1024], in1=r_tok(tt, 0, 1024),
                                    op=ALU.mult),
                    v.tensor_tensor(out=xres(tt), in0=b_g.f32(0, 1024), in1=xres(tt), op=ALU.add)],
                    waits=(s_vm,))
                res.append(s_g)
            A.release(b_mu, b_xrfm, b_xkfm, b_wf, b_wfs, b_krel, b_g)
            return res

        def emit_attn_b(ln_steps):
            b_xnfm = A.alloc(ND * 1024)
            xn_fm = lambda f, c0, n: b_xnfm.bf16(f * 2048 + c0, n)
            s_fm = tok2fm(xn_t, b_xnfm, tuple(ln_steps))
            s_spx = P.dma("gq", [(spill_x[tt, :, :], xres(tt)) for tt in range(NT)], waits=tuple(ln_steps))

            b_cs = A.alloc(NDI * 40)
            b_bs = A.alloc(64)
            cw = lambda dt, wi: b_cs.f32(dt * 40 + wi, 1)
            cb = lambda dt: b_cs.f32(dt * 40 + 4, 1)
            dtb = lambda dt, br: b_cs.f32(dt * 40 + 5 + br, 1)
            dD = lambda dt: b_cs.f32(dt * 40 + 7, 1)
            Aneg = lambda dt, br, n: b_cs.f32(dt * 40 + 8 + br * 16 + n, 1)
            prev = None
            for dt in range(NDI):
                pairs = [(b_cs.f32(dt * 40, 4), convw_h[dt, :, :]),
                         (b_cs.f32(dt * 40 + 4, 1), convb_h[dt, :, :]),
                         (b_bs.f32(0, 16), Alog1_h[dt, :, :]),
                         (b_bs.f32(16, 16), Alog2_h[dt, :, :]),
                         (b_cs.f32(dt * 40 + 5, 1), dtb1_h[dt, :, :]),
                         (b_cs.f32(dt * 40 + 6, 1), dtb2_h[dt, :, :]),
                         (b_bs.f32(32, 1), D1_h[dt, :, :]),
                         (b_bs.f32(33, 1), D2_h[dt, :, :])]
                sl = P.dma("sp", pairs, waits=(prev,) if prev else (ln_steps[-1],))
                s_e = P.add("act", lambda s, dt=dt: s.activation(
                    out=b_cs.f32(dt * 40 + 8, 32), in_=b_bs.f32(0, 32), func=AF.Exp), waits=(sl,))
                s_m = P.add("dve", lambda v, dt=dt: [
                    v.tensor_scalar(out=b_cs.f32(dt * 40 + 8, 32), in0=b_cs.f32(dt * 40 + 8, 32),
                                    scalar1=-1.0, scalar2=None, op0=ALU.mult),
                    v.tensor_sub(out=b_cs.f32(dt * 40 + 7, 1), in0=b_bs.f32(32, 1), in1=b_bs.f32(33, 1))],
                    waits=(s_e,))
                prev = s_m
            s_cs = prev

            b_selbf = A.alloc(2048)
            b_sel32 = A.alloc(4096)
            s_sl = P.dma("sp", [(b_sel32.f32(0, 4096)[0:64, :], sel_h[:, :])], waits=(ln_steps[-1],))
            s_sc = P.add("dve", lambda v: v.tensor_copy(out=b_selbf.bf16(0, 4096)[0:64, :],
                                                        in_=b_sel32.f32(0, 4096)[0:64, :]), waits=(s_sl,))
            A.release(b_sel32)
            selB = lambda n: b_selbf.bf16(0, 4096)[0:64, n * 128:(n + 1) * 128]
            selC = lambda n: b_selbf.bf16(0, 4096)[0:64, 2048 + n * 128:2048 + (n + 1) * 128]

            b_ipstage = Buf(big, b_xres.off, 1024)
            b_ipm = Buf(big, b_xres.off + 1024, 1024)
            b_xraw = Buf(big, b_xres.off + 2048, 4114)
            b_ctmp = Buf(big, b_xres.off + 6162, 4096)
            b_uf = Buf(big, b_xres.off + 10258, 2048)
            b_cfin = Buf(big, b_xn.off, 4096)
            b_ulo = Buf(big, b_xn.off + 4096, 2048)
            b_szt = Buf(big, b_xn.off + 6144, 2048)
            b_uhis = A.alloc(2048)
            ipm = lambda mp, k: b_ipm.bf16(mp * 1024 + k * 128, 128)

            xz_sp = []
            u_done = []
            prev_si = None
            prev_cast = None
            prev_mm = {}
            for m in range(2 * NDI):
                pb = m % 2
                pso = psA[:, pb * 1024:pb * 1024 + 1024]
                w = [s_fm, s_spx]
                if prev_cast is not None:
                    w.append(prev_cast)
                if (pb in prev_mm):
                    w.append(prev_mm[pb])
                s_ld = P.dma("sp", [(b_ipstage.f32(k * 128, 128),
                                     inprojT_h[k * 128:(k + 1) * 128, m * 128:(m + 1) * 128])
                                    for k in range(ND)], waits=tuple(w))
                s_ct = P.add("dve", lambda v, pb=pb: v.tensor_copy(
                    out=b_ipm.bf16(pb * 1024, 1024), in_=b_ipstage.f32(0, 1024)), waits=(s_ld,))
                prev_cast = s_ct
                halves = []
                for toff in (0, 1024):
                    w = [s_ct]
                    if len(xz_sp) >= 2:
                        w.append(xz_sp[-2])
                    if halves:
                        w.append(halves[-1])
                    s_mm = P.add("pe", lambda t, m=m, toff=toff, pso=pso, pb=pb: [
                        t.matmul(out=pso[:, q4 * 512:(q4 + 1) * 512], lhsT=ipm(pb, k),
                                 rhs=xn_fm(k, toff + q4 * 512, 512), start=(k == 0), stop=(k == ND - 1))
                        for k in range(ND) for q4 in range(2)], waits=tuple(w))
                    prev_mm[pb] = s_mm
                    if m < NDI:
                        s_ev = P.add("dve", lambda v, pb=pb, toff=toff, pso=pso: v.tensor_copy(
                            out=b_xraw.f32(pb * 2056 + 4 + toff, 1024), in_=pso[:, 0:1024]), waits=(s_mm,))
                    else:
                        s_ev = P.add("act", lambda s, pb=pb, toff=toff, pso=pso: s.activation(
                            out=b_szt.bf16(pb * 2048 + toff, 1024), in_=pso[:, 0:1024], func=AF.Silu),
                            waits=(s_mm,))
                    halves.append(s_ev)
                if m >= NDI:
                    dt_ = m - NDI
                    s_sp = P.dma("gq", [(spill_sz[dt_, :, :], b_szt.bf16(pb * 2048, 2048))], waits=(halves[-1],))
                    xz_sp.append(s_sp)
                    continue
                dt_ = m
                wcv = halves + [s_cs]
                if prev_si is not None:
                    wcv.append(prev_si)
                s_cv = P.add("dve", lambda v, dt_=dt_, pb=pb: [
                    v.memset(b_xraw.f32(pb * 2056, 4), 0.0),
                    v.tensor_scalar(out=b_ctmp.f32(0, 2048), in0=b_xraw.f32(pb * 2056 + 1, 2048),
                                    scalar1=cw(dt_, 0)[:, 0:1], scalar2=None, op0=ALU.mult),
                    v.scalar_tensor_tensor(out=b_ctmp.f32(2048, 2048), in0=b_xraw.f32(pb * 2056 + 2, 2048),
                                           scalar=cw(dt_, 1)[:, 0:1], in1=b_ctmp.f32(0, 2048),
                                           op0=ALU.mult, op1=ALU.add),
                    v.scalar_tensor_tensor(out=b_ctmp.f32(0, 2048), in0=b_xraw.f32(pb * 2056 + 3, 2048),
                                           scalar=cw(dt_, 2)[:, 0:1], in1=b_ctmp.f32(2048, 2048),
                                           op0=ALU.mult, op1=ALU.add),
                    v.scalar_tensor_tensor(out=b_cfin.f32(pb * 2048, 2048), in0=b_xraw.f32(pb * 2056 + 4, 2048),
                                           scalar=cw(dt_, 3)[:, 0:1], in1=b_ctmp.f32(0, 2048),
                                           op0=ALU.mult, op1=ALU.add)], waits=tuple(wcv))
                s_si = P.add("act", lambda s, dt_=dt_, pb=pb: s.activation(
                    out=b_uf.f32(0, 2048), in_=b_cfin.f32(pb * 2048, 2048), func=AF.Silu,
                    bias=cb(dt_)[:, 0:1]), waits=(s_cv,))
                prev_si = s_si
                w_us = [s_si]
                if len(u_done) >= 2:
                    w_us.append(xz_sp[-2])
                s_us = P.add("dve", lambda v, dt_=dt_, pb=pb: [
                    v.tensor_copy(out=b_uhis.bf16(pb * 2048, 2048), in_=b_uf.f32(0, 2048)),
                    v.tensor_sub(out=b_ulo.bf16(pb * 2048, 2048), in0=b_uf.f32(0, 2048),
                                 in1=b_uhis.bf16(pb * 2048, 2048))],
                    waits=tuple(w_us))
                s_ul = P.dma("gq", [(spill_ulo[dt_, :, :], b_ulo.bf16(pb * 2048, 2048)),
                                    (spill_uhi[dt_, :, :], b_uhis.bf16(pb * 2048, 2048))], waits=(s_us,))
                xz_sp.append(s_ul)
                u_done.append(s_us)
            A.release(b_xnfm)
            s_b1 = xz_sp[-1]

            barrier()
            b_xpw = Buf(big, b_xn.off, 768)
            b_xps = Buf(big, b_xn.off + 768, 96)
            b_bclo = Buf(big, b_xn.off + 1024, 2048)
            b_dtr = A.alloc(2 * 1024)
            b_bcs = A.alloc(2 * 1024)
            dtr = lambda br: b_dtr.bf16(br * 2048, 2048)[0:64, :]
            bcs = lambda br: b_bcs.bf16(br * 2048, 2048)[0:64, :]
            b_upx = A.alloc(2048)
            xp_done = []
            for br in range(2):
                xdt_h = xdtT1_h if br == 0 else xdtT2_h
                xbc_h = xbcT1_h if br == 0 else xbcT2_h
                prev = (u_done[-1], xz_sp[-1]) if br == 0 else (xp_done[-1],)
                for k in range(NDI):
                    sl = P.dma("sp", [(b_xps.f32(0, 64), xdt_h[k * 128:(k + 1) * 128, :]),
                                      (b_xps.f32(64, 32), xbc_h[k * 128:(k + 1) * 128, :])], waits=prev)
                    sc = P.add("dve", lambda v, k=k: v.tensor_copy(out=b_xpw.bf16(k * 96, 96),
                                                                   in_=b_xps.f32(0, 96)), waits=(sl,))
                    prev = (sc,)
                xpw = lambda k: b_xpw.bf16(k * 96, 96)
                mm_steps = []
                for k in range(NDI):
                    w = [prev[0], xz_sp[-1]]
                    if len(mm_steps) >= 2:
                        w.append(mm_steps[-2][0])
                    s_uk = P.dma("aq", [(b_upx.bf16((k % 2) * 2048, 2048), spill_uhi[k, :, :])],
                                 waits=tuple(w))
                    s_mm = P.add("pe", lambda t, k=k: [
                        t.matmul(out=psA[0:64, q4 * 512:(q4 + 1) * 512], lhsT=xpw(k)[:, 0:64],
                                 rhs=b_upx.bf16((k % 2) * 2048 + q4 * 512, 512),
                                 start=(k == 0), stop=(k == NDI - 1))
                        for q4 in range(4)] + [
                        t.matmul(out=psB[0:32, q4 * 512:(q4 + 1) * 512], lhsT=xpw(k)[:, 64:96],
                                 rhs=b_upx.bf16((k % 2) * 2048 + q4 * 512, 512),
                                 start=(k == 0), stop=(k == NDI - 1))
                        for q4 in range(4)], waits=(s_uk,) + ((mm_steps[-1][1],) if mm_steps else ()))
                    mm_steps.append((s_mm, s_mm))
                s_dt = mm_steps[-1][0]
                s_de = P.add("dve", lambda v, br=br: v.tensor_copy(out=dtr(br), in_=psA[0:64, 0:2048]),
                             waits=(s_dt,))
                s_bc = s_dt
                s_be = P.add("dve", lambda v, br=br: [
                    v.tensor_copy(out=bcs(br)[0:32, :], in_=psB[0:32, 0:2048]),
                    v.tensor_sub(out=b_bclo.bf16(0, 2048)[0:32, :], in0=psB[0:32, 0:2048],
                                 in1=bcs(br)[0:32, :])], waits=(s_bc,))
                s_bl0 = P.dma("gq", [(bxfer[:, :], b_bclo.bf16(0, 2048)[0:32, :])], waits=(s_be,))
                s_bl = P.dma("gq", [(b_bcs.bf16(br * 2048, 2048)[32:64, :], bxfer[:, :])], waits=(s_bl0,))
                xp_done.append(s_bl)


            b_dtw = A.alloc(2 * 1024)
            b_dws = Buf(big, b_xn.off + 3072, 2048)
            dtw = lambda br: b_dtw.bf16(br * 2048, 2048)[0:64, :]
            s_d1 = P.dma("sp", [(b_dws.f32(0, 2048)[0:64, :], dtT1_h[:, :])], waits=(xp_done[-1],))
            s_c1 = P.add("dve", lambda v: v.tensor_copy(out=dtw(0), in_=b_dws.f32(0, 2048)[0:64, :]),
                         waits=(s_d1,))
            s_d2 = P.dma("sp", [(b_dws.f32(0, 2048)[0:64, :], dtT2_h[:, :])], waits=(s_c1,))
            s_c2 = P.add("dve", lambda v: v.tensor_copy(out=dtw(1), in_=b_dws.f32(0, 2048)[0:64, :]),
                         waits=(s_d2,))
            s_dtw = s_c2

            barrier()
            b_dl = A.alloc(2048)
            b_du = A.alloc(2048)
            b_spt = A.alloc(2048)
            b_dA = Buf(big, b_xres.off, 4096)
            b_dBu = Buf(big, b_xres.off + 4096, 4096)
            b_h = Buf(big, b_xres.off + 8192, 4096)
            b_ypp = Buf(big, b_xres.off + 12288, 4096)
            b_hC = A.alloc(2 * 2048)
            b_up2 = A.alloc(2048)
            b_szt2 = Buf(big, b_xn.off, 2048)
            b_yft = Buf(big, b_xn.off + 2048, 2048)

            yfin_sp = []
            prev_tail = None
            scan_hist = {}
            dbu_hist = {}
            hc_hist = {}
            ya_hist = []
            for dt_ in range(NDI):
                pbt = dt_ % 2
                w = [s_b1, s_dtw]
                if prev_tail:
                    w.append(prev_tail)
                s_ulo = P.dma("sp", [(b_up2.bf16(0, 2048), spill_uhi[dt_, :, :]),
                                     (b_up2.bf16(2048, 2048), spill_ulo[dt_, :, :])], waits=tuple(w))
                s_szl = P.dma("aq", [(b_szt2.bf16(pbt * 2048, 2048), spill_sz[dt_, :, :])], waits=tuple(w))
                s_yi = P.add("dve", lambda v, dt_=dt_: [
                    v.tensor_scalar(out=b_ypp.f32(0, 2048), in0=b_up2.bf16(0, 2048), scalar1=dD(dt_)[:, 0:1],
                                    scalar2=None, op0=ALU.mult),
                    v.scalar_tensor_tensor(out=b_ypp.f32(0, 2048), in0=b_up2.bf16(2048, 2048),
                                           scalar=dD(dt_)[:, 0:1], in1=b_ypp.f32(0, 2048),
                                           op0=ALU.mult, op1=ALU.add)], waits=(s_ulo, s_cs))
                ycur = 0
                last_ya = s_yi
                for br in range(2):
                    w = [s_dtw, s_yi]
                    if dbu_hist:
                        w.append(max(dbu_hist.values()))
                    if prev_tail:
                        w.append(prev_tail)
                    s_dp = P.add("pe", lambda t, dt_=dt_, br=br: [
                        t.matmul(out=psA[:, q4 * 512:(q4 + 1) * 512],
                                 lhsT=dtw(br)[:, dt_ * 128:(dt_ + 1) * 128],
                                 rhs=dtr(br)[:, q4 * 512:(q4 + 1) * 512], start=True, stop=True)
                        for q4 in range(4)], waits=tuple(w))
                    s_sp2 = P.add("act", lambda s, dt_=dt_, br=br: [
                        s.activation(out=b_spt.f32(0, 2048), in_=psA[:, 0:2048], func=AF.Abs,
                                     bias=dtb(dt_, br)[:, 0:1]),
                        s.activation(out=b_spt.f32(0, 2048), in_=b_spt.f32(0, 2048), func=AF.Exp, scale=-1.0),
                        s.activation(out=b_spt.f32(0, 2048), in_=b_spt.f32(0, 2048), func=AF.Ln, bias=b_col.f32(31, 1)[:, 0:1]),
                        s.activation(out=b_dl.f32(0, 2048), in_=psA[:, 0:2048], func=AF.Relu,
                                     bias=dtb(dt_, br)[:, 0:1])], waits=(s_dp,))
                    s_df = P.add("dve", lambda v, dt_=dt_: [
                        v.tensor_add(out=b_dl.f32(0, 2048), in0=b_dl.f32(0, 2048), in1=b_spt.f32(0, 2048)),
                        v.scalar_tensor_tensor(out=b_du.f32(0, 2048), in0=b_up2.bf16(0, 2048), scalar=1.0,
                                               in1=b_dl.f32(0, 2048), op0=ALU.mult, op1=ALU.mult),
                        v.scalar_tensor_tensor(out=b_spt.f32(0, 2048), in0=b_up2.bf16(2048, 2048), scalar=1.0,
                                               in1=b_dl.f32(0, 2048), op0=ALU.mult, op1=ALU.mult),
                        v.tensor_add(out=b_du.f32(0, 2048), in0=b_du.f32(0, 2048), in1=b_spt.f32(0, 2048))],
                        waits=(s_sp2,))
                    for n in range(DS):
                        nb = n % 2
                        sgn = 1.0 if br == 0 else -1.0
                        prev_merged = None
                        for lh in range(2):
                            p2 = (2 * n + lh) % 2
                            hoff = nb * 2048 + lh * 1024
                            w = [s_df]
                            if (nb, lh) in scan_hist:
                                w.append(scan_hist[(nb, lh)])
                            s_dA = P.add("act", lambda s, dt_=dt_, br=br, n=n, hoff=hoff, lh=lh: s.activation(
                                out=b_dA.f32(hoff, 1024), in_=b_dl.f32(lh * 1024, 1024), func=AF.Exp,
                                scale=Aneg(dt_, br, n)[:, 0:1]), waits=tuple(w))
                            wb = [s_sc, s_df]
                            if (p2, "mg") in dbu_hist:
                                wb.append(dbu_hist[(p2, "mg")])
                            s_bb = P.add("pe", lambda t, br=br, n=n, lh=lh, p2=p2: [
                                t.matmul(out=psA[:, p2 * 1024 + q4 * 512:p2 * 1024 + (q4 + 1) * 512],
                                         lhsT=selB(n),
                                         rhs=bcs(br)[:, lh * 1024 + q4 * 512:lh * 1024 + (q4 + 1) * 512],
                                         start=True, stop=True)
                                for q4 in range(2)] + [
                                t.matmul(out=psB[:, p2 * 1024 + q4 * 512:p2 * 1024 + (q4 + 1) * 512],
                                         lhsT=selC(n),
                                         rhs=bcs(br)[:, lh * 1024 + q4 * 512:lh * 1024 + (q4 + 1) * 512],
                                         start=True, stop=True)
                                for q4 in range(2)], waits=tuple(wb))
                            w = [s_bb, s_dA]
                            if (nb, lh) in scan_hist:
                                w.append(scan_hist[(nb, lh)])
                            if len(ya_hist) >= 4:
                                w.append(ya_hist[-4])
                            hard = ()
                            if lh == 1 and prev_merged is not None:
                                hard = (prev_merged,)
                            init = 0.0 if lh == 0 else b_h.f32(nb * 2048 + 1023, 1)[:, 0:1]
                            s_hc = P.add("dve", lambda v, nb=nb, sgn=sgn, hoff=hoff, p2=p2, init=init: [
                                v.tensor_tensor(out=b_dBu.f32(hoff, 1024), in0=b_du.f32((hoff % 2048), 1024),
                                                in1=psA[:, p2 * 1024:p2 * 1024 + 1024], op=ALU.mult),
                                v.tensor_tensor_scan(out=b_h.f32(hoff, 1024),
                                                     data0=b_dA.f32(hoff, 1024),
                                                     data1=b_dBu.f32(hoff, 1024), initial=init,
                                                     op0=ALU.mult, op1=ALU.add),
                                v.scalar_tensor_tensor(out=b_hC.f32(hoff, 1024),
                                                       in0=b_h.f32(hoff, 1024), scalar=sgn,
                                                       in1=psB[:, p2 * 1024:p2 * 1024 + 1024],
                                                       op0=ALU.mult, op1=ALU.mult)],
                                waits=tuple(w), hard=hard)
                            prev_merged = s_hc
                            scan_hist[(nb, lh)] = s_hc
                            dbu_hist[(p2, "mg")] = s_hc
                            s_ya = P.add("pool", lambda g, hoff=hoff, yc=ycur, lh=lh: g.tensor_add(
                                out=b_ypp.f32((1 - yc) * 2048 + lh * 1024, 1024),
                                in0=b_ypp.f32(yc * 2048 + lh * 1024, 1024),
                                in1=b_hC.f32(hoff, 1024)), waits=(s_hc, last_ya))
                            last_ya = s_ya
                            ya_hist.append(s_ya)
                        ycur = 1 - ycur
                s_gate = P.add("dve", lambda v, yc=ycur, pbt=pbt: v.tensor_tensor(
                    out=b_yft.bf16(pbt * 2048, 2048), in0=b_ypp.f32(yc * 2048, 2048),
                    in1=b_szt2.bf16(pbt * 2048, 2048), op=ALU.mult), waits=(last_ya, s_szl))
                s_ysp = P.dma("gq", [(spill_yfin[dt_, :, :], b_yft.bf16(pbt * 2048, 2048))], waits=(s_gate,))
                prev_tail = s_ysp
                yfin_sp.append(s_ysp)

            A.release(b_dl, b_du, b_spt, b_hC, b_up2, b_upx, b_uhis,
                      b_dtr, b_bcs, b_dtw, b_selbf, b_cs, b_bs)

            barrier()
            b_opw = A.alloc(NDI * 512)
            b_ops = A.alloc(1024)
            b_ysb = A.alloc(2 * 1024)
            b_xrl = A.alloc(2 * 1024)
            opw = lambda k: b_opw.bf16(k * 1024, 1024)
            prev = (yfin_sp[-1],)
            for k in range(NDI):
                prev = (load_cast(b_opw, k * 1024, outprojT_h[k * 128:(k + 1) * 128, :], b_ops, 1024, prev),)
            s_op = prev[0]
            res = []
            for tt in range(NT):
                pb = tt % 2
                w = [s_op]
                if tt >= 2:
                    w.append(res[tt - 2])
                s_yl = P.dma("sp", [(b_ysb.bf16(pb * 2048 + k * 128, 128), spill_yfin[k, :, tt * 128:(tt + 1) * 128])
                                    for k in range(NDI)], waits=tuple(w))
                s_xl = P.dma("aq", [(b_xrl.f32(pb * 1024, 1024), spill_x[tt, :, :])], waits=tuple(w))
                pso = psA[:, pb * 1024:pb * 1024 + 1024]
                s_om = P.add("pe", lambda t, pb=pb, pso=pso: [
                    t.matmul(out=pso[:, hf * 512:(hf + 1) * 512], lhsT=b_ysb.bf16(pb * 2048 + k * 128, 128),
                             rhs=opw(k)[:, hf * 512:(hf + 1) * 512], start=(k == 0), stop=(k == NDI - 1))
                    for k in range(NDI) for hf in range(2)], waits=(s_yl,))
                s_rs = P.add("dve", lambda v, tt=tt, pb=pb, pso=pso: v.tensor_tensor(
                    out=xres(tt), in0=pso[:, 0:1024], in1=b_xrl.f32(pb * 1024, 1024), op=ALU.add),
                    waits=(s_om, s_xl))
                res.append(s_rs)
            A.release(b_opw, b_ops, b_ysb, b_xrl)
            return res

        import sys as _sys
        def mark(nm):
            print(f"[mark] {nm}: step {len(P.steps)}", file=_sys.stderr)

        last = emit_attn_a(emit_ln((s_base,)))
        if N_SUB >= 2:
            barrier()
            last = emit_ffn(emit_ln((last[-1],)), which=0)
        if N_SUB >= 3:
            barrier()
            last = emit_attn_b(emit_ln((last[-1],)))
        if N_SUB >= 4:
            barrier()
            last = emit_ffn(emit_ln((last[-1],)), which=1)

        mark("store")
        s_st = P.dma("gq", [(out_h[tt * 128:(tt + 1) * 128, :], xres(tt)) for tt in range(NT)],
                     waits=(last[-1],))
        P.add("pool", lambda g: g.engine_nop(), waits=(s_st,))

        P.resolve()

        @block.sync
        def _(e):
            P.emit_for(e, {"sp"}, sems)

        @block.tensor
        def _(e):
            P.emit_for(e, {"pe"}, sems)

        @block.vector
        def _(e):
            P.emit_for(e, {"dve"}, sems)

        @block.scalar
        def _(e):
            P.emit_for(e, {"act", "aq"}, sems)

        @block.gpsimd
        def _(e):
            P.emit_for(e, {"pool", "gq"}, sems)

    return nc


def _prep_inputs(inputs):
    g = lambda k: np.asarray(inputs[k], dtype=np.float32)
    b_attn = g("L0_attn_a_b_attn")
    common = {
        "W_attn": np.ascontiguousarray(g("L0_attn_a_W_attn")),
        "battn_qk": np.ascontiguousarray(np.concatenate(
            [b_attn[0:D].reshape(ND, 128).T, b_attn[D:2 * D].reshape(ND, 128).T], axis=1)),
        "bv_row": np.ascontiguousarray(b_attn[2 * D:3 * D].reshape(1, D)),
        "W_proj": np.ascontiguousarray(g("L0_attn_a_W_proj")),
        "bproj_row": np.ascontiguousarray(g("L0_attn_a_b_proj").reshape(1, D)),
        "negmask": np.where(np.arange(128)[None, :] <= np.arange(128)[:, None], 0.0, -1e9).astype(np.float32),
        "trimask": (np.arange(128)[:, None] <= np.arange(128)[None, :]).astype(np.float32),
        "eye": np.eye(128, dtype=np.float32),
        "WrT": np.ascontiguousarray(g("L0_ffn_W_r").T),
        "WkT": np.ascontiguousarray(g("L0_ffn_W_k").T),
        "WvT": np.ascontiguousarray(g("L0_ffn_W_v").T),
        "mur_col": np.ascontiguousarray(g("L0_ffn_mu_r").reshape(ND, 128).T),
        "muk_col": np.ascontiguousarray(g("L0_ffn_mu_k").reshape(ND, 128).T),
        "inprojT": np.ascontiguousarray(g("L0_attn_b_in_proj_w").T),
        "convw": np.ascontiguousarray(g("L0_attn_b_conv_w").reshape(NDI, 128, DCONV)),
        "convb": np.ascontiguousarray(g("L0_attn_b_conv_b").reshape(NDI, 128, 1)),
        "Alog1": np.ascontiguousarray(g("L0_attn_b_A1_log").reshape(NDI, 128, DS)),
        "Alog2": np.ascontiguousarray(g("L0_attn_b_A2_log").reshape(NDI, 128, DS)),
        "D1": np.ascontiguousarray(g("L0_attn_b_D1").reshape(NDI, 128, 1)),
        "D2": np.ascontiguousarray(g("L0_attn_b_D2").reshape(NDI, 128, 1)),
        "xdtT1": np.ascontiguousarray(g("L0_attn_b_x1_proj_w")[0:DTR].T),
        "xbcT1": np.ascontiguousarray(g("L0_attn_b_x1_proj_w")[DTR:].T),
        "xdtT2": np.ascontiguousarray(g("L0_attn_b_x2_proj_w")[0:DTR].T),
        "xbcT2": np.ascontiguousarray(g("L0_attn_b_x2_proj_w")[DTR:].T),
        "dtT1": np.ascontiguousarray(g("L0_attn_b_dt1_proj_w").T),
        "dtT2": np.ascontiguousarray(g("L0_attn_b_dt2_proj_w").T),
        "dtb1": np.ascontiguousarray(g("L0_attn_b_dt1_proj_b").reshape(NDI, 128, 1)),
        "dtb2": np.ascontiguousarray(g("L0_attn_b_dt2_proj_b").reshape(NDI, 128, 1)),
        "outprojT": np.ascontiguousarray(g("L0_attn_b_out_proj_w").T),
    }
    sel = np.zeros((64, 2 * DS * 128), np.float32)
    for n in range(DS):
        sel[n, n * 128:(n + 1) * 128] = 1.0
        sel[32 + n, n * 128:(n + 1) * 128] = 1.0
        sel[16 + n, 2048 + n * 128:2048 + (n + 1) * 128] = 1.0
        sel[48 + n, 2048 + n * 128:2048 + (n + 1) * 128] = 1.0
    common["sel"] = sel
    x_full = g("L0_attn_a_x")
    return [dict(common, x=np.ascontiguousarray(x_full[c])) for c in range(B)]


def kernel(**inputs):
    maps = _prep_inputs(inputs)
    nc = build_program()
    res = run_bass_kernel_spmd(nc, maps, core_ids=list(range(B)))
    try:
        kernel._last_dbg = [np.asarray(r["dbg"]) for r in res.results]
    except Exception:
        kernel._last_dbg = None
    return np.stack([np.asarray(r["out"]) for r in res.results], axis=0).astype(np.float32)


# version 04 (final)
# speedup vs baseline: 1.0950x; 1.0377x over previous
import numpy as np
import concourse.bass as bass
import concourse.mybir as mybir
from concourse.bass_utils import run_bass_kernel_spmd
from contextlib import ExitStack

F32 = mybir.dt.float32
BF16 = mybir.dt.bfloat16
AF = mybir.ActivationFunctionType
ALU = mybir.AluOpType
AX = mybir.AxisListType

B, T, D = 8, 2048, 1024
H = 16
DI, DS, DTR, DCONV = 2048, 16, 64, 4
NT, ND, NDI = T // 128, D // 128, DI // 128
EPS = 1e-5
N_SUB = 4
DEBUG_TAP = None


class Plan:
    ENG = ("sp", "pe", "dve", "act", "pool", "gq", "aq")
    SERIAL = False
    SERIAL_FROM = 10**9
    SERIAL_UNTIL = -1

    def __init__(self):
        self.steps = []
        self.waited = set()

    def last_per_engine(self):
        out = {}
        for sid, st in enumerate(self.steps):
            out[st[0]] = sid
        return out

    def add(self, engine, fn, waits=(), n_dma=0, hard=()):
        sid = len(self.steps)
        for w in tuple(waits) + tuple(hard):
            assert w is not None and 0 <= w < sid, (w, sid)
            self.waited.add(w)
        self.steps.append((engine, fn, tuple(waits), n_dma, tuple(hard)))
        return sid

    def dma(self, engine, pairs, waits=()):
        def fn(eng):
            return [eng.dma_start(out=d, in_=s) for d, s in pairs]
        return self.add(engine, fn, waits=waits, n_dma=len(pairs))

    def resolve(self):
        if self.SERIAL or self.SERIAL_UNTIL >= 0 or self.SERIAL_FROM < len(self.steps):
            self.waited = set(range(len(self.steps)))
        cnt = {e: 0 for e in self.ENG}
        self.sig = {}
        for sid, (e, fn, waits, n_dma, hard) in enumerate(self.steps):
            if sid in self.waited or n_dma:
                cnt[e] += 16 * n_dma if n_dma else 1
                self.sig[sid] = (e, cnt[e])

    def emit_for(self, eng, enames, sems):
        for sid, (e, fn, waits, n_dma, hard) in enumerate(self.steps):
            if e not in enames:
                continue
            need = {}
            if self.SERIAL or (self.SERIAL_FROM <= sid <= self.SERIAL_UNTIL):
                for w in range(sid):
                    we, wv = self.sig[w]
                    if self.steps[w][0] in enames and not self.steps[w][3]:
                        continue
                    if need.get(we, 0) < wv:
                        need[we] = wv
            for w in waits:
                we, wv = self.sig[w]
                if we in enames and not self.steps[w][3]:
                    continue
                if need.get(we, 0) < wv:
                    need[we] = wv
            for w in hard:
                we, wv = self.sig[w]
                if need.get(we, 0) < wv:
                    need[we] = wv
            for we in sorted(need):
                eng.wait_ge(sems[we], need[we])
            insts = fn(eng)
            if sid in self.waited or n_dma:
                if n_dma:
                    for di in insts:
                        di.then_inc(sems[e], 16)
                else:
                    last = insts[-1] if isinstance(insts, (list, tuple)) else insts
                    last.then_inc(sems[e], 1)


class Buf:
    __slots__ = ("t", "off", "words")

    def __init__(self, t, off, words):
        self.t, self.off, self.words = t, off, words

    def f32(self, c0=0, n=None):
        if n is None:
            n = self.words - c0
        return self.t[:, self.off + c0:self.off + c0 + n]

    def bf16(self, c0=0, n=None):
        if n is None:
            n = 2 * self.words - c0
        return self.t[:, self.off:self.off + self.words].bitcast(BF16)[:, c0:c0 + n]


class Arena:
    def __init__(self, tensor, words):
        self.t, self.words = tensor, words
        self.free = [(0, words)]
        self.peak = 0

    def alloc(self, words):
        words = (words + 1) & ~1
        for i, (off, sz) in enumerate(self.free):
            if sz >= words:
                if sz == words:
                    self.free.pop(i)
                else:
                    self.free[i] = (off + words, sz - words)
                self.peak = max(self.peak, self.words - sum(s for _, s in self.free))
                b = Buf(self.t, off, words)
                import traceback
                caller = traceback.extract_stack()[-2]
                self.live_dbg = getattr(self, "live_dbg", {})
                self.live_dbg[(off, words)] = caller.lineno
                return b
        raise MemoryError(f"arena OOM: want {words}, free {self.free}, live={sorted(getattr(self, 'live_dbg', {}).items())}")

    def release(self, *bufs):
        for b in bufs:
            self.live_dbg.pop((b.off, b.words), None)
            self.free.append((b.off, b.words))
        self.free.sort()
        out = []
        for o, s in self.free:
            if out and out[-1][0] + out[-1][1] == o:
                out[-1][1] += s
            else:
                out.append([o, s])
        self.free = [(o, s) for o, s in out]


def build_program():
    nc = bass.Bass("TRN2", target_bir_lowering=False)
    din = lambda name, shape, dt=F32: nc.dram_tensor(name, list(shape), dt, kind="ExternalInput").ap()

    x_h = din("x", (T, D))
    Wattn_h = din("W_attn", (D, 3 * D))
    battnqk_h = din("battn_qk", (128, 16))
    bvrow_h = din("bv_row", (1, D))
    Wproj_h = din("W_proj", (D, D))
    bprow_h = din("bproj_row", (1, D))
    negmask_h = din("negmask", (128, 128))
    trimask_h = din("trimask", (128, 128))
    eye_h = din("eye", (128, 128))
    WrT_h = din("WrT", (D, D))
    WkT_h = din("WkT", (D, D))
    WvT_h = din("WvT", (D, D))
    murcol_h = din("mur_col", (128, ND))
    mukcol_h = din("muk_col", (128, ND))
    inprojT_h = din("inprojT", (D, 2 * DI))
    convw_h = din("convw", (NDI, 128, DCONV))
    convb_h = din("convb", (NDI, 128, 1))
    Alog1_h = din("Alog1", (NDI, 128, DS))
    Alog2_h = din("Alog2", (NDI, 128, DS))
    D1_h = din("D1", (NDI, 128, 1))
    D2_h = din("D2", (NDI, 128, 1))
    xdtT1_h = din("xdtT1", (DI, DTR))
    xbcT1_h = din("xbcT1", (DI, 2 * DS))
    xdtT2_h = din("xdtT2", (DI, DTR))
    xbcT2_h = din("xbcT2", (DI, 2 * DS))
    dtT1_h = din("dtT1", (DTR, DI))
    dtT2_h = din("dtT2", (DTR, DI))
    dtb1_h = din("dtb1", (NDI, 128, 1))
    dtb2_h = din("dtb2", (NDI, 128, 1))
    sel_h = din("sel", (64, 2 * DS * 128))
    outprojT_h = din("outprojT", (DI, D))
    out_h = nc.dram_tensor("out", [T, D], F32, kind="ExternalOutput").ap()
    dbg_h = nc.dram_tensor("dbg", [128, 2048], F32, kind="ExternalOutput").ap()

    spill_x = nc.dram_tensor("spill_x", [NT, 128, D], F32, kind="Internal").ap()
    spill_ulo = nc.dram_tensor("spill_ulo", [NDI, 128, T], BF16, kind="Internal").ap()
    spill_uhi = nc.dram_tensor("spill_uhi", [NDI, 128, T], BF16, kind="Internal").ap()
    spill_sz = nc.dram_tensor("spill_sz", [NDI, 128, T], BF16, kind="Internal").ap()
    spill_yfin = nc.dram_tensor("spill_yfin", [NDI, 128, T], BF16, kind="Internal").ap()
    stash_ffnw = nc.dram_tensor("stash_ffnw", [3, D, D], BF16, kind="Internal").ap()
    mxfer = nc.dram_tensor("mxfer", [16, 128], BF16, kind="Internal").ap()
    rxfer = nc.dram_tensor("rxfer", [1, 1024], BF16, kind="Internal").ap()
    bxfer = nc.dram_tensor("bxfer", [32, 2048], BF16, kind="Internal").ap()

    P = Plan()
    with ExitStack() as ctx:
        part_words = nc.SBUF_PARTITION_SIZE_BYTES // 4 - 4608
        big = ctx.enter_context(nc.sbuf_tensor("arena", [128, part_words], F32))
        A = Arena(big, part_words)
        psA = ctx.enter_context(nc.psum_tensor("psA", [128, 2048], F32))
        psB = ctx.enter_context(nc.psum_tensor("psB", [128, 2048], F32))
        sems = {e: ctx.enter_context(nc.semaphore(f"s_{e}")) for e in Plan.ENG}
        block = ctx.enter_context(nc.Block())

        b_eye = A.alloc(64)
        b_eyef = A.alloc(128)
        b_negm = A.alloc(128)
        b_trim = A.alloc(64)
        b_ones2 = A.alloc(64)
        b_xres = A.alloc(NT * 1024)
        b_xn = A.alloc(NT * 512)
        b_col = A.alloc(64)
        b_cst = A.alloc(128)

        xres = lambda tt: b_xres.f32(tt * 1024, 1024)
        xn_t = lambda tt: b_xn.bf16(tt * 1024, 1024)
        eye_bf = b_eye.bf16(0, 128)
        ones2 = b_ones2.bf16(0, 128)[0:2, :]

        s_x = P.dma("sp", [(xres(tt), x_h[tt * 128:(tt + 1) * 128, :]) for tt in range(NT)])
        s_e32 = P.dma("sp", [(b_cst.f32(0, 128), eye_h[:, :]), (b_eyef.f32(0, 128), eye_h[:, :])])
        s_eye = P.add("dve", lambda v: v.tensor_copy(out=eye_bf, in_=b_cst.f32(0, 128)), waits=(s_e32,))
        s_n32 = P.dma("sp", [(b_negm.f32(0, 128), negmask_h[:, :])], waits=(s_eye,))
        s_t32 = P.dma("sp", [(b_cst.f32(0, 128), trimask_h[:, :])], waits=(s_eye,))
        s_tri = P.add("dve", lambda v: v.tensor_copy(out=b_trim.bf16(0, 128), in_=b_cst.f32(0, 128)), waits=(s_t32,))
        s_on2 = P.add("dve", lambda v: [v.memset(ones2, 1.0),
                                        v.memset(b_col.f32(30, 1), EPS),
                                        v.memset(b_col.f32(31, 1), 1.0)])
        s_base = P.add("dve", lambda v: v.engine_nop(), waits=(s_x, s_n32, s_tri, s_on2))

        def tap(name, ap, after_step):
            if DEBUG_TAP == name:
                n = ap.shape[-1]
                P.dma("gq", [(dbg_h[0:ap.shape[0], 0:n], ap)], waits=(after_step,))

        def emit_ln(waits):
            outs = []
            st6 = b_col.f32(0, 12)
            mv = b_col.f32(12, 2)
            rstd = b_col.f32(14, 1)
            bneg = b_col.f32(15, 1)
            lnt = b_col.f32(16, 1)
            prev = None
            for tt in range(NT):
                w = tuple(waits) if prev is None else (prev,)
                s1 = P.add("dve", lambda v, tt=tt: [
                    v.bn_stats(out=b_col.f32(0, 6), in_=xres(tt)[:, 0:512]),
                    v.bn_stats(out=b_col.f32(6, 6), in_=xres(tt)[:, 512:1024]),
                    v.bn_aggr(out=b_col.f32(18, 2), in_=b_col.f32(0, 6)),
                    v.bn_aggr(out=b_col.f32(24, 2), in_=b_col.f32(6, 6))], waits=w)
                s1b = P.add("pool", lambda g: [
                    g.tensor_add(out=b_col.f32(26, 2), in0=b_col.f32(18, 2), in1=b_col.f32(24, 2)),
                    g.tensor_sub(out=b_col.f32(28, 1), in0=b_col.f32(18, 1), in1=b_col.f32(24, 1)),
                    g.tensor_scalar(out=mv[:, 0:1], in0=b_col.f32(26, 1), scalar1=0.5, scalar2=None,
                                    op0=ALU.mult),
                    g.tensor_tensor(out=b_col.f32(29, 1), in0=b_col.f32(28, 1), in1=b_col.f32(28, 1),
                                    op=ALU.mult),
                    g.tensor_scalar(out=b_col.f32(29, 1), in0=b_col.f32(29, 1), scalar1=0.25, scalar2=None,
                                    op0=ALU.mult),
                    g.tensor_scalar(out=b_col.f32(17, 1), in0=b_col.f32(27, 1), scalar1=0.5, scalar2=None,
                                    op0=ALU.mult),
                    g.tensor_add(out=mv[:, 1:2], in0=b_col.f32(17, 1), in1=b_col.f32(29, 1))], waits=(s1,))
                s2 = P.add("act", lambda s: [s.activation(out=lnt, in_=mv[:, 1:2], func=AF.Ln, bias=b_col.f32(30, 1)[:, 0:1]),
                                             s.activation(out=rstd, in_=lnt, func=AF.Exp, scale=-0.5)],
                           waits=(s1b,))
                s2b = P.add("pool", lambda g: [
                    g.tensor_tensor(out=bneg, in0=mv[:, 0:1], in1=rstd, op=ALU.mult),
                    g.tensor_scalar(out=bneg, in0=bneg, scalar1=-1.0, scalar2=None, op0=ALU.mult)],
                    waits=(s2,))
                s3 = P.add("dve", lambda v, tt=tt: v.tensor_scalar(
                    out=xn_t(tt), in0=xres(tt), scalar1=rstd[:, 0:1],
                    scalar2=bneg[:, 0:1], op0=ALU.mult, op1=ALU.add), waits=(s2b,))
                prev = s3
                outs.append(s3)
            return outs

        dummy_tp = lambda t: t.transpose(out=psB[:, 1920:1984].bitcast(BF16)[:, 0:128],
                                         in_=eye_bf, identity=eye_bf)

        def barrier():
            lasts = [s for s in P.last_per_engine().values()]
            g = P.add("dve", lambda v: v.engine_nop(), waits=tuple(lasts))
            bsteps = {"dve": g}
            bsteps["pool"] = P.add("pool", lambda p: p.engine_nop(), waits=(g,))
            bsteps["act"] = P.add("act", lambda s: s.copy(out=b_col.f32(33, 1), in_=b_col.f32(31, 1)),
                                  waits=(g,))
            bsteps["pe"] = P.add("pe", lambda t: dummy_tp(t), waits=(g,))
            return [g, bsteps["pool"], bsteps["act"], bsteps["pe"]]

        def tok2fm(src_tile, dst_buf, waits):
            ps_bf = psB[:, 0:256].bitcast(BF16)
            last = None
            prev_ev = None
            for f in range(ND):
                for g in range(0, NT, 4):
                    w = list(waits) + ([prev_ev] if prev_ev else [])
                    s_tp = P.add("pe", lambda t, f=f, g=g: [
                        t.transpose(out=ps_bf[:, u * 128:(u + 1) * 128],
                                    in_=src_tile(g + u)[:, f * 128:(f + 1) * 128],
                                    identity=eye_bf) for u in range(4)] + [dummy_tp(t)], waits=tuple(w))
                    s_ev = P.add("dve", lambda v, f=f, g=g: v.tensor_copy(
                        out=dst_buf.bf16(f * 2048 + g * 128, 512), in_=ps_bf[:, 0:512]), waits=(s_tp,))
                    prev_ev = s_ev
                    last = s_ev
            return last

        def rowpair(dst_buf, src_row_h, row_stage, lo_stage, waits=()):
            sl = P.dma("sp", [(row_stage.f32(0, 1024)[0:1, :], src_row_h[0:1, :])], waits=waits)
            s_mk = P.add("dve", lambda v: [
                v.tensor_copy(out=dst_buf.bf16(0, 1024)[0:1, :], in_=row_stage.f32(0, 1024)[0:1, :]),
                v.tensor_sub(out=lo_stage.bf16(0, 1024)[0:1, :], in0=row_stage.f32(0, 1024)[0:1, :],
                             in1=dst_buf.bf16(0, 1024)[0:1, :])], waits=(sl,))
            s_lo0 = P.dma("gq", [(rxfer[0:1, :], lo_stage.bf16(0, 1024)[0:1, :])], waits=(s_mk,))
            s_lo = P.dma("gq", [(dst_buf.bf16(0, 1024)[1:2, :], rxfer[0:1, :])], waits=(s_lo0,))
            return s_lo

        def load_cast(dst_buf, dst_c0, src_ap, stage, nwords, waits, eng_c="dve"):
            sl = P.dma("sp", [(stage.f32(0, nwords), src_ap)], waits=waits)
            sc = P.add(eng_c, lambda v: v.tensor_copy(out=dst_buf.bf16(dst_c0, nwords), in_=stage.f32(0, nwords)),
                       waits=(sl,))
            return sc

        def emit_attn_a(ln_steps):
            b_xnfm = A.alloc(ND * 1024)
            xn_fm = lambda f, c0, n: b_xnfm.bf16(f * 2048 + c0, n)
            s_fm = tok2fm(xn_t, b_xnfm, tuple(ln_steps))
            tap("xn0", b_xn.f32(0, 512), ln_steps[-1])
            tap("xnfm0", b_xnfm.f32(0, 1024), s_fm)
            tap("xnfm7", b_xnfm.f32(7 * 1024, 1024), s_fm)

            mark("attn_a:vext")
            b_vext = A.alloc(NT * 520)
            vext = lambda tt: b_vext.bf16(tt * 1040, 1040)
            s_vi = P.add("dve", lambda v: v.memset(b_vext.bf16(0, NT * 1040), 0.0))
            s_vo = P.add("dve", lambda v: [
                v.memset(vext(tt).rearrange("p (h c) -> p h c", c=65)[:, :, 64:65], 1.0)
                for tt in range(NT)], waits=(s_vi,))

            b_wstage = A.alloc(1024)
            b_row = A.alloc(1024)
            b_lo = A.alloc(512)
            b_bvp = A.alloc(512)
            b_bpp = A.alloc(512)
            b_bqk = A.alloc(16)
            s_bv = rowpair(b_bvp, bvrow_h, b_row, b_lo)
            s_bp = rowpair(b_bpp, bprow_h, b_row, b_lo, waits=(s_bv,))
            s_bqk = P.dma("sp", [(b_bqk.f32(0, 16), battnqk_h[:, :])])

            b_wv = A.alloc(ND * 512)
            wv = lambda k: b_wv.bf16(k * 1024, 1024)
            prev = (s_fm,)
            for k in range(ND):
                prev = (load_cast(b_wv, k * 1024, Wattn_h[k * 128:(k + 1) * 128, 2048:3072],
                                  b_wstage, 1024, prev),)
            s_wv = prev[0]
            tap("wv01", b_wv.f32(0, 1024), s_wv)

            v_ev = []
            for tt in range(NT):
                pso = psA[:, (tt % 2) * 1024:(tt % 2) * 1024 + 1024]
                w = [s_wv, s_bv, s_vo, s_fm]
                if tt >= 2:
                    w.append(v_ev[tt - 2])
                s_mm = P.add("pe", lambda t, tt=tt, pso=pso: [
                    t.matmul(out=pso[:, hf * 512:(hf + 1) * 512],
                             lhsT=(xn_fm(k, tt * 128, 128) if k < ND else ones2),
                             rhs=(wv(k)[:, hf * 512:(hf + 1) * 512] if k < ND
                                  else b_bvp.bf16(0, 1024)[0:2, hf * 512:(hf + 1) * 512]),
                             start=(k == 0), stop=(k == ND))
                    for hf in range(2) for k in range(ND + 1)], waits=tuple(w))
                s_ev = P.add("dve", lambda v, tt=tt, pso=pso: v.tensor_copy(
                    out=vext(tt).rearrange("p (h c) -> p h c", c=65)[:, :, 0:64],
                    in_=pso.rearrange("p (h c) -> p h c", c=64)), waits=(s_mm,))
                if tt == 0:
                    tap("vext0", b_vext.f32(0, 520), s_ev)
                v_ev.append(s_ev)

            mark("attn_a:heads")
            bar = barrier()
            A.release(b_wv, b_wstage, b_row, b_lo)
            b_qk = A.alloc(4 * 1024)
            b_wqk32 = A.alloc(2048)
            b_wqkbf = A.alloc(1024)
            b_ytok = b_xn
            b_attT = A.alloc(1024)
            attTf = lambda i, c0, n: b_attT.bf16(c0, n)
            b_mm = A.alloc(16)
            b_mmbf = A.alloc(16)
            b_mst = A.alloc(64)
            b_mrow = A.alloc(1024)
            qk_q = lambda pb, c0, n: b_qk.bf16(pb * 4096 + c0, n)
            qk_k = lambda pb, c0, n: b_qk.bf16(pb * 4096 + 2048 + c0, n)
            y_tok = lambda tt, c0, n: b_ytok.bf16(tt * 1024 + c0, n)
            attT = lambda ab, c0, n: b_attT.bf16(ab * 512 + c0, n)
            negm = b_negm.f32(0, 128)
            trim = b_trim.bf16(0, 128)
            mrow = b_mrow.bf16(0, 2048)[0:1, :]

            psA_free = [None]
            last_stm_prev = [None]
            m_guard = [None]
            m_guard2 = [None]
            qk_lastuse = {}
            wqk_lastcast = [None]
            last_head_done = [v_ev[-1]]
            psA_free[0] = v_ev[-1]

            for hp in range(H // 2):
                pb = hp % 2
                pq = [(b_wqk32.f32(c * 128, 128), Wattn_h[c * 128:(c + 1) * 128, hp * 128:(hp + 1) * 128])
                      for c in range(ND)]
                pk = [(b_wqk32.f32(1024 + c * 128, 128), Wattn_h[c * 128:(c + 1) * 128, D + hp * 128:D + (hp + 1) * 128])
                      for c in range(ND)]
                w = [s_bqk, v_ev[-1], s_bp, bar[0]]
                if wqk_lastcast[0] is not None:
                    w.append(wqk_lastcast[0])
                s_wl = P.dma("sp", pq + pk, waits=tuple(w))
                s_wc = P.add("dve", lambda v: v.tensor_copy(out=b_wqkbf.bf16(0, 2048), in_=b_wqk32.f32(0, 2048)),
                             waits=(s_wl,))
                wqk_lastcast[0] = s_wc
                wq = lambda c: b_wqkbf.bf16(c * 128, 128)
                wk = lambda c: b_wqkbf.bf16(1024 + c * 128, 128)

                w = [s_wc, s_fm]
                if pb in qk_lastuse:
                    w.append(qk_lastuse[pb])
                if psA_free[0] is not None:
                    w.append(psA_free[0])
                s_qmm = P.add("pe", lambda t: [
                    t.matmul(out=psA[:, q4 * 512:(q4 + 1) * 512], lhsT=wq(c), rhs=xn_fm(c, q4 * 512, 512),
                             start=(c == 0), stop=(c == ND - 1))
                    for q4 in range(4) for c in range(ND)], waits=tuple(w))
                s_qev = P.add("dve", lambda v, pb=pb, hp=hp: [
                    v.tensor_scalar(out=qk_q(pb, q4 * 512, 512), in0=psA[:, q4 * 512:(q4 + 1) * 512],
                                    scalar1=b_bqk.f32(hp, 1)[:, 0:1], scalar2=None, op0=ALU.add)
                    for q4 in range(4)], waits=(s_qmm,))
                s_kmm = P.add("pe", lambda t: [
                    t.matmul(out=psA[:, q4 * 512:(q4 + 1) * 512], lhsT=wk(c), rhs=xn_fm(c, q4 * 512, 512),
                             start=(c == 0), stop=(c == ND - 1))
                    for q4 in range(4) for c in range(ND)], waits=(s_qev,))
                s_kev = P.add("dve", lambda v, pb=pb, hp=hp: [
                    v.tensor_scalar(out=qk_k(pb, q4 * 512, 512), in0=psA[:, q4 * 512:(q4 + 1) * 512],
                                    scalar1=b_bqk.f32(8 + hp, 1)[:, 0:1], scalar2=None, op0=ALU.add)
                    for q4 in range(4)], waits=(s_kmm,))
                if hp == 0:
                    tap("q0", b_qk.f32(0, 1024), s_qev)
                    tap("k0", b_qk.f32(2048, 1024), s_kev)
                psA_free[0] = s_kev

                for half in range(2):
                    h = 2 * hp + half
                    e0 = 64 * half
                    prev_mx = None
                    for i in range(NT):
                        nk = (i + 1) * 128
                        w = [s_kev]
                        if prev_mx is not None:
                            w.append(prev_mx)
                        elif psA_free[0] is not None:
                            w.append(psA_free[0])
                        s_smm = P.add("pe", lambda t, i=i, nk=nk, pb=pb, e0=e0: [
                            t.matmul(out=psA[:, n0:min(nk, n0 + 512)],
                                     lhsT=qk_q(pb, i * 128, 128)[e0:e0 + 64, :],
                                     rhs=qk_k(pb, n0, min(nk, n0 + 512) - n0)[e0:e0 + 64, :],
                                     start=True, stop=True)
                            for n0 in range(0, nk, 512)], waits=tuple(w))
                        if i == 7:
                            tap(f"sc_h{h}", psA[:, 0:1024], s_smm)
                        w_mx = [s_smm]
                        if i == 0 and m_guard2[0] is not None:
                            w_mx.append(m_guard2[0])
                        s_mx = P.add("dve", lambda v, i=i, nk=nk: [
                            v.tensor_tensor(out=psA[:, i * 128:(i + 1) * 128],
                                            in0=psA[:, i * 128:(i + 1) * 128], in1=negm, op=ALU.add),
                            v.tensor_reduce(out=b_mm.f32(i, 1), in_=psA[:, 0:nk], axis=AX.X,
                                            op=ALU.max)], waits=tuple(w_mx))
                        prev_mx = s_mx
                    psA_free[0] = prev_mx
                    w_mn = [prev_mx]
                    if m_guard[0] is not None:
                        w_mn.append(m_guard[0])
                    s_mneg = P.add("pool", lambda g: g.tensor_scalar(
                        out=b_mmbf.f32(0, 16), in0=b_mm.f32(0, 16), scalar1=-1.0, scalar2=None,
                        op0=ALU.mult), waits=tuple(w_mn))
                    m_guard2[0] = s_mneg
                    ps_m = psB[:, 1536:1664]
                    s_mt = P.add("pe", lambda t: t.transpose(out=ps_m[0:16, 0:128], in_=b_mmbf.f32(0, 16),
                                                             identity=b_eyef.f32(0, 128)), waits=(s_mneg,))
                    s_ms = P.add("dve", lambda v: v.tensor_copy(out=b_mst.bf16(0, 128)[0:16, :],
                                                                in_=ps_m[0:16, 0:128]), waits=(s_mt,))
                    w_mr = [s_ms]
                    if last_stm_prev[0] is not None:
                        w_mr.append(last_stm_prev[0])
                    s_mr0 = P.dma("gq", [(mxfer[:, :], b_mst.bf16(0, 128)[0:16, :])], waits=tuple(w_mr))
                    m_guard[0] = s_mr0
                    s_mr = P.dma("gq", [(mrow, mxfer.rearrange("p f -> () (p f)"))], waits=(s_mr0,))

                    y_evs = []
                    exp_lastuse = {}
                    av_hist = []
                    for i in range(NT):
                        psY = psB[:, 1040 + (i % 2) * 80:1040 + (i % 2) * 80 + 65]
                        exps = []
                        for gi, jg in enumerate(range(0, i + 1, 4)):
                            je = min(jg + 4, i + 1)
                            ab = gi % 2
                            pstile = psB[:, ab * 512:(ab + 1) * 512]
                            w = [s_mr]
                            if (ab in exp_lastuse):
                                w.append(exp_lastuse[ab])
                            s_stm = P.add("pe", lambda t, i=i, jg=jg, je=je, pstile=pstile, pb=pb, e0=e0: [
                                mm for j in range(jg, je) for mm in (
                                    t.matmul(out=pstile[:, (j - jg) * 128:(j - jg + 1) * 128],
                                             lhsT=qk_k(pb, j * 128, 128)[e0:e0 + 64, :],
                                             rhs=qk_q(pb, i * 128, 128)[e0:e0 + 64, :],
                                             start=True, stop=False),
                                    t.matmul(out=pstile[:, (j - jg) * 128:(j - jg + 1) * 128],
                                             lhsT=ones2[0:1, :],
                                             rhs=mrow[:, i * 128:(i + 1) * 128],
                                             start=False, stop=True))], waits=tuple(w))
                            last_stm_prev[0] = s_stm
                            wex = [s_stm]
                            if gi == 0 and av_hist:
                                wex.append(av_hist[-1])
                            s_ex = P.add("act", lambda s, i=i, jg=jg, je=je, pstile=pstile: s.activation(
                                out=attTf(i, jg * 128, (je - jg) * 128), in_=pstile[:, 0:(je - jg) * 128],
                                func=AF.Exp, scale=0.125), waits=tuple(wex))
                            exp_lastuse[ab] = s_ex
                            exps.append(s_ex)
                        s_dm = P.add("dve", lambda v, i=i: v.tensor_tensor(
                            out=attTf(i, i * 128, 128), in0=attTf(i, i * 128, 128),
                            in1=trim, op=ALU.mult), waits=(exps[-1],))
                        w2 = exps[:-1] + [s_dm]
                        if len(y_evs) >= 2:
                            w2.append(y_evs[i - 2])
                        s_av = P.add("pe", lambda t, i=i, psY=psY, h=h: [
                            t.matmul(out=psY, lhsT=attTf(i, j * 128, 128),
                                     rhs=vext(j)[:, h * 65:h * 65 + 65],
                                     start=(j == 0), stop=(j == i)) for j in range(i + 1)],
                            waits=tuple(w2))
                        av_hist.append(s_av)
                        w3 = [s_av]
                        if len(y_evs) >= 1:
                            w3.append(y_evs[-1])
                        s_rc = P.add("dve", lambda v, psY=psY: v.reciprocal(
                            out=b_col.f32(22, 1), in_=psY[:, 64:65]), waits=tuple(w3))
                        s_ye = P.add("dve", lambda v, i=i, psY=psY, h=h: v.tensor_scalar(
                            out=y_tok(i, h * 64, 64), in0=psY[:, 0:64],
                            scalar1=b_col.f32(22, 1)[:, 0:1], scalar2=None, op0=ALU.mult),
                            waits=(), hard=(s_rc,))
                        y_evs.append(s_ye)
                    if h == 0:
                        tap("mrow0", b_mrow.f32(0, 1024), s_mr)
                        tap("mmat0", b_mm.f32(0, 16), s_mt)
                        tap("ytok0", b_xn.f32(0, 512), y_evs[-1])
                    qk_lastuse[pb] = y_evs[-1]
                    last_head_done[0] = y_evs[-1]

            mark("attn_a:proj")
            barrier()
            s_att_done = last_head_done[0]
            A.release(b_xnfm, b_vext, b_qk, b_wqk32, b_wqkbf, b_attT, b_mm, b_mmbf, b_mst, b_mrow)
            b_yfm = A.alloc(ND * 1024)
            y_fm = lambda f, c0, n: b_yfm.bf16(f * 2048 + c0, n)
            s_yfm = tok2fm(lambda tt: y_tok(tt, 0, 1024), b_yfm, (s_att_done,))

            b_wp = A.alloc(ND * 512)
            b_wps = A.alloc(1024)
            wp = lambda k: b_wp.bf16(k * 1024, 1024)
            prev = (s_att_done,)
            for k in range(ND):
                prev = (load_cast(b_wp, k * 1024, Wproj_h[k * 128:(k + 1) * 128, :], b_wps, 1024, prev),)
            s_wp = prev[0]
            res = []
            for tt in range(NT):
                pso = psA[:, (tt % 2) * 1024:(tt % 2) * 1024 + 1024]
                w = [s_wp, s_yfm, s_bp]
                if tt >= 2:
                    w.append(res[tt - 2])
                s_pm = P.add("pe", lambda t, tt=tt, pso=pso: [
                    t.matmul(out=pso[:, hf * 512:(hf + 1) * 512], lhsT=y_fm(k, tt * 128, 128),
                             rhs=wp(k)[:, hf * 512:(hf + 1) * 512], start=(k == 0), stop=False)
                    for k in range(ND) for hf in range(2)] + [
                    t.matmul(out=pso[:, hf * 512:(hf + 1) * 512], lhsT=ones2,
                             rhs=b_bpp.bf16(0, 1024)[0:2, hf * 512:(hf + 1) * 512],
                             start=False, stop=True) for hf in range(2)], waits=tuple(w))
                s_rs = P.add("dve", lambda v, tt=tt, pso=pso: v.tensor_tensor(
                    out=xres(tt), in0=pso[:, 0:1024], in1=xres(tt), op=ALU.add), waits=(s_pm,))
                if tt == 0:
                    tap("res0", b_xres.f32(0, 1024), s_rs)
                    tap("yfm0", b_yfm.f32(0, 1024), s_yfm)
                res.append(s_rs)
            A.release(b_bvp, b_bpp, b_bqk, b_yfm, b_wp, b_wps)
            return res

        def emit_ffn(ln_steps, which):
            mark(f"ffn{which}:start")
            tap(f"ffn{which}_xn0", b_xn.f32(0, 512), ln_steps[-1])
            b_mu = A.alloc(16)
            s_mul = P.dma("sp", [(b_mu.f32(0, ND), murcol_h[:, :]),
                                 (b_mu.f32(8, ND), mukcol_h[:, :])], waits=(ln_steps[-1],))
            mur = lambda f: b_mu.f32(0 + f, 1)
            muk = lambda f: b_mu.f32(8 + f, 1)

            b_xnfm2 = A.alloc(ND * 1024)
            s_fm2 = tok2fm(xn_t, b_xnfm2, tuple(ln_steps))
            xnf = lambda f, c0, n: b_xnfm2.bf16(f * 2048 + c0, n)
            b_xrfm = A.alloc(ND * 1024)
            b_xkfm = A.alloc(ND * 1024)
            b_mt = A.alloc(1024)
            xr_fm = lambda f, c0, n: b_xrfm.bf16(f * 2048 + c0, n)
            xk_fm = lambda f, c0, n: b_xkfm.bf16(f * 2048 + c0, n)
            mix_done = []
            for f in range(ND):
                s_mx1 = P.add("dve", lambda v, f=f: [
                    v.tensor_sub(out=b_mt.bf16(0, 2047), in0=xnf(f, 1, 2047), in1=xnf(f, 0, 2047)),
                    v.scalar_tensor_tensor(out=xr_fm(f, 1, 2047), in0=b_mt.bf16(0, 2047),
                                           scalar=mur(f)[:, 0:1], in1=xnf(f, 0, 2047),
                                           op0=ALU.mult, op1=ALU.add),
                    v.tensor_scalar(out=xr_fm(f, 0, 1), in0=xnf(f, 0, 1), scalar1=mur(f)[:, 0:1],
                                    scalar2=None, op0=ALU.mult),
                    v.scalar_tensor_tensor(out=xk_fm(f, 1, 2047), in0=b_mt.bf16(0, 2047),
                                           scalar=muk(f)[:, 0:1], in1=xnf(f, 0, 2047),
                                           op0=ALU.mult, op1=ALU.add),
                    v.tensor_scalar(out=xk_fm(f, 0, 1), in0=xnf(f, 0, 1), scalar1=muk(f)[:, 0:1],
                                    scalar2=None, op0=ALU.mult)],
                    waits=(s_fm2, s_mul) + ((mix_done[-1],) if mix_done else ()))
                mix_done.append(s_mx1)

            mark(f"ffn{which}:weights")
            barrier()
            A.release(b_xnfm2, b_mt)
            b_wf = A.alloc(ND * 512)
            b_wfs = A.alloc(1024)
            wf = lambda k: b_wf.bf16(k * 1024, 1024)

            def load_w(idx, src_h, waits):
                prev = tuple(waits)
                for k in range(ND):
                    if which == 0:
                        sc = load_cast(b_wf, k * 1024, src_h[k * 128:(k + 1) * 128, :], b_wfs, 1024, prev)
                        st = P.dma("gq", [(stash_ffnw[idx, k * 128:(k + 1) * 128, :], wf(k))], waits=(sc,))
                        prev = (sc, st)
                        last = st
                    else:
                        sl = P.dma("sp", [(wf(k), stash_ffnw[idx, k * 128:(k + 1) * 128, :])], waits=prev)
                        prev = (sl,)
                        last = sl
                return last

            r_tok = lambda tt, c0, n: b_xn.bf16(tt * 1024 + c0, n)
            lw = load_w(0, WrT_h, (mix_done[-1],))
            r_steps = []
            for tt in range(NT):
                pso = psA[:, (tt % 2) * 1024:(tt % 2) * 1024 + 1024]
                w = [lw, mix_done[-1]]
                if tt >= 2:
                    w.append(r_steps[tt - 2])
                s_rm = P.add("pe", lambda t, tt=tt, pso=pso: [
                    t.matmul(out=pso[:, hf * 512:(hf + 1) * 512], lhsT=xr_fm(k, tt * 128, 128),
                             rhs=wf(k)[:, hf * 512:(hf + 1) * 512], start=(k == 0), stop=(k == ND - 1))
                    for k in range(ND) for hf in range(2)], waits=tuple(w))
                s_rs = P.add("act", lambda s, tt=tt, pso=pso: s.activation(
                    out=r_tok(tt, 0, 1024), in_=pso[:, 0:1024], func=AF.Sigmoid), waits=(s_rm,))
                r_steps.append(s_rs)

            mark(f"ffn{which}:k")
            barrier()
            k2_fm = lambda f, c0, n: b_xrfm.bf16(f * 2048 + c0, n)
            tap(f"ffn{which}_r0", b_xn.f32(0, 512), r_steps[-1])
            tap(f"ffn{which}_xk0", b_xkfm.f32(0, 1024), r_steps[-1])
            tap(f"ffn{which}_xr0", b_xrfm.f32(0, 1024), mix_done[-1])
            b_krel = A.alloc(2048)
            lw2 = load_w(1, WkT_h, (r_steps[-1],))
            k_steps = []
            for m in range(ND):
                for toff in (0, 1024):
                    pso = psA[:, (len(k_steps) % 2) * 1024:(len(k_steps) % 2) * 1024 + 1024]
                    w = [lw2, r_steps[-1]]
                    if len(k_steps) >= 2:
                        w.append(k_steps[-2])
                    s_km = P.add("pe", lambda t, m=m, toff=toff, pso=pso: [
                        t.matmul(out=pso[:, q4 * 512:(q4 + 1) * 512], lhsT=wf(k)[:, m * 128:(m + 1) * 128],
                                 rhs=xk_fm(k, toff + q4 * 512, 512), start=(k == 0), stop=(k == ND - 1))
                        for k in range(ND) for q4 in range(2)], waits=tuple(w))
                    kp = len(k_steps) % 2
                    s_kr = P.add("dve", lambda v, pso=pso, kp=kp: v.tensor_scalar(
                        out=b_krel.f32(kp * 1024, 1024), in0=pso[:, 0:1024], scalar1=0.0, scalar2=None,
                        op0=ALU.max), waits=(s_km,))
                    s_sq = P.add("pool", lambda g, m=m, toff=toff, kp=kp: g.tensor_mul(
                        out=k2_fm(m, toff, 1024), in0=b_krel.f32(kp * 1024, 1024),
                        in1=b_krel.f32(kp * 1024, 1024)), waits=(s_kr,))
                    k_steps.append(s_sq)

            mark(f"ffn{which}:v")
            barrier()
            tap(f"ffn{which}_k20", b_xrfm.f32(0, 1024), k_steps[-1])
            b_g = A.alloc(1024)
            lw3 = load_w(2, WvT_h, (k_steps[-1],))
            res = []
            for tt in range(NT):
                pso = psA[:, (tt % 2) * 1024:(tt % 2) * 1024 + 1024]
                w = [lw3, k_steps[-1]]
                if tt >= 2:
                    w.append(res[tt - 2])
                s_vm = P.add("pe", lambda t, tt=tt, pso=pso: [
                    t.matmul(out=pso[:, hf * 512:(hf + 1) * 512], lhsT=k2_fm(k, tt * 128, 128),
                             rhs=wf(k)[:, hf * 512:(hf + 1) * 512], start=(k == 0), stop=(k == ND - 1))
                    for k in range(ND) for hf in range(2)], waits=tuple(w))
                s_g = P.add("dve", lambda v, tt=tt, pso=pso: [
                    v.tensor_tensor(out=b_g.f32(0, 1024), in0=pso[:, 0:1024], in1=r_tok(tt, 0, 1024),
                                    op=ALU.mult),
                    v.tensor_tensor(out=xres(tt), in0=b_g.f32(0, 1024), in1=xres(tt), op=ALU.add)],
                    waits=(s_vm,))
                res.append(s_g)
            A.release(b_mu, b_xrfm, b_xkfm, b_wf, b_wfs, b_krel, b_g)
            return res

        def emit_attn_b(ln_steps):
            b_xnfm = A.alloc(ND * 1024)
            xn_fm = lambda f, c0, n: b_xnfm.bf16(f * 2048 + c0, n)
            s_fm = tok2fm(xn_t, b_xnfm, tuple(ln_steps))
            s_spx = P.dma("gq", [(spill_x[tt, :, :], xres(tt)) for tt in range(NT)], waits=tuple(ln_steps))

            b_cs = A.alloc(NDI * 40)
            b_bs = A.alloc(64)
            cw = lambda dt, wi: b_cs.f32(dt * 40 + wi, 1)
            cb = lambda dt: b_cs.f32(dt * 40 + 4, 1)
            dtb = lambda dt, br: b_cs.f32(dt * 40 + 5 + br, 1)
            dD = lambda dt: b_cs.f32(dt * 40 + 7, 1)
            Aneg = lambda dt, br, n: b_cs.f32(dt * 40 + 8 + br * 16 + n, 1)
            prev = None
            for dt in range(NDI):
                pairs = [(b_cs.f32(dt * 40, 4), convw_h[dt, :, :]),
                         (b_cs.f32(dt * 40 + 4, 1), convb_h[dt, :, :]),
                         (b_bs.f32(0, 16), Alog1_h[dt, :, :]),
                         (b_bs.f32(16, 16), Alog2_h[dt, :, :]),
                         (b_cs.f32(dt * 40 + 5, 1), dtb1_h[dt, :, :]),
                         (b_cs.f32(dt * 40 + 6, 1), dtb2_h[dt, :, :]),
                         (b_bs.f32(32, 1), D1_h[dt, :, :]),
                         (b_bs.f32(33, 1), D2_h[dt, :, :])]
                sl = P.dma("sp", pairs, waits=(prev,) if prev else (ln_steps[-1],))
                s_e = P.add("act", lambda s, dt=dt: s.activation(
                    out=b_cs.f32(dt * 40 + 8, 32), in_=b_bs.f32(0, 32), func=AF.Exp), waits=(sl,))
                s_m = P.add("dve", lambda v, dt=dt: [
                    v.tensor_scalar(out=b_cs.f32(dt * 40 + 8, 32), in0=b_cs.f32(dt * 40 + 8, 32),
                                    scalar1=-1.0, scalar2=None, op0=ALU.mult),
                    v.tensor_sub(out=b_cs.f32(dt * 40 + 7, 1), in0=b_bs.f32(32, 1), in1=b_bs.f32(33, 1))],
                    waits=(s_e,))
                prev = s_m
            s_cs = prev

            b_selbf = A.alloc(2048)
            b_sel32 = A.alloc(4096)
            s_sl = P.dma("sp", [(b_sel32.f32(0, 4096)[0:64, :], sel_h[:, :])], waits=(ln_steps[-1],))
            s_sc = P.add("dve", lambda v: v.tensor_copy(out=b_selbf.bf16(0, 4096)[0:64, :],
                                                        in_=b_sel32.f32(0, 4096)[0:64, :]), waits=(s_sl,))
            A.release(b_sel32)
            selB = lambda n: b_selbf.bf16(0, 4096)[0:64, n * 128:(n + 1) * 128]
            selC = lambda n: b_selbf.bf16(0, 4096)[0:64, 2048 + n * 128:2048 + (n + 1) * 128]

            b_ipstage = Buf(big, b_xres.off, 1024)
            b_ipm = Buf(big, b_xres.off + 1024, 1024)
            b_xraw = Buf(big, b_xres.off + 2048, 4114)
            b_ctmp = Buf(big, b_xres.off + 6162, 4096)
            b_uf = Buf(big, b_xres.off + 10258, 2048)
            b_cfin = Buf(big, b_xn.off, 4096)
            b_ulo = Buf(big, b_xn.off + 4096, 2048)
            b_szt = Buf(big, b_xn.off + 6144, 2048)
            b_uhis = A.alloc(2048)
            ipm = lambda mp, k: b_ipm.bf16(mp * 1024 + k * 128, 128)

            xz_sp = []
            u_done = []
            prev_si = None
            prev_cast = None
            prev_mm = {}
            for m in range(2 * NDI):
                pb = m % 2
                pso = psA[:, pb * 1024:pb * 1024 + 1024]
                w = [s_fm, s_spx]
                if prev_cast is not None:
                    w.append(prev_cast)
                if (pb in prev_mm):
                    w.append(prev_mm[pb])
                s_ld = P.dma("sp", [(b_ipstage.f32(k * 128, 128),
                                     inprojT_h[k * 128:(k + 1) * 128, m * 128:(m + 1) * 128])
                                    for k in range(ND)], waits=tuple(w))
                s_ct = P.add("dve", lambda v, pb=pb: v.tensor_copy(
                    out=b_ipm.bf16(pb * 1024, 1024), in_=b_ipstage.f32(0, 1024)), waits=(s_ld,))
                prev_cast = s_ct
                halves = []
                for toff in (0, 1024):
                    w = [s_ct]
                    if len(xz_sp) >= 2:
                        w.append(xz_sp[-2])
                    if halves:
                        w.append(halves[-1])
                    s_mm = P.add("pe", lambda t, m=m, toff=toff, pso=pso, pb=pb: [
                        t.matmul(out=pso[:, q4 * 512:(q4 + 1) * 512], lhsT=ipm(pb, k),
                                 rhs=xn_fm(k, toff + q4 * 512, 512), start=(k == 0), stop=(k == ND - 1))
                        for k in range(ND) for q4 in range(2)], waits=tuple(w))
                    prev_mm[pb] = s_mm
                    if m < NDI:
                        s_ev = P.add("dve", lambda v, pb=pb, toff=toff, pso=pso: v.tensor_copy(
                            out=b_xraw.f32(pb * 2056 + 4 + toff, 1024), in_=pso[:, 0:1024]), waits=(s_mm,))
                    else:
                        s_ev = P.add("act", lambda s, pb=pb, toff=toff, pso=pso: s.activation(
                            out=b_szt.bf16(pb * 2048 + toff, 1024), in_=pso[:, 0:1024], func=AF.Silu),
                            waits=(s_mm,))
                    halves.append(s_ev)
                if m >= NDI:
                    dt_ = m - NDI
                    s_sp = P.dma("gq", [(spill_sz[dt_, :, :], b_szt.bf16(pb * 2048, 2048))], waits=(halves[-1],))
                    xz_sp.append(s_sp)
                    continue
                dt_ = m
                wcv = halves + [s_cs]
                if prev_si is not None:
                    wcv.append(prev_si)
                s_cv = P.add("dve", lambda v, dt_=dt_, pb=pb: [
                    v.memset(b_xraw.f32(pb * 2056, 4), 0.0),
                    v.tensor_scalar(out=b_ctmp.f32(0, 2048), in0=b_xraw.f32(pb * 2056 + 1, 2048),
                                    scalar1=cw(dt_, 0)[:, 0:1], scalar2=None, op0=ALU.mult),
                    v.scalar_tensor_tensor(out=b_ctmp.f32(2048, 2048), in0=b_xraw.f32(pb * 2056 + 2, 2048),
                                           scalar=cw(dt_, 1)[:, 0:1], in1=b_ctmp.f32(0, 2048),
                                           op0=ALU.mult, op1=ALU.add),
                    v.scalar_tensor_tensor(out=b_ctmp.f32(0, 2048), in0=b_xraw.f32(pb * 2056 + 3, 2048),
                                           scalar=cw(dt_, 2)[:, 0:1], in1=b_ctmp.f32(2048, 2048),
                                           op0=ALU.mult, op1=ALU.add),
                    v.scalar_tensor_tensor(out=b_cfin.f32(pb * 2048, 2048), in0=b_xraw.f32(pb * 2056 + 4, 2048),
                                           scalar=cw(dt_, 3)[:, 0:1], in1=b_ctmp.f32(0, 2048),
                                           op0=ALU.mult, op1=ALU.add)], waits=tuple(wcv))
                s_si = P.add("act", lambda s, dt_=dt_, pb=pb: s.activation(
                    out=b_uf.f32(0, 2048), in_=b_cfin.f32(pb * 2048, 2048), func=AF.Silu,
                    bias=cb(dt_)[:, 0:1]), waits=(s_cv,))
                prev_si = s_si
                w_us = [s_si]
                if len(u_done) >= 2:
                    w_us.append(xz_sp[-2])
                s_us = P.add("dve", lambda v, dt_=dt_, pb=pb: [
                    v.tensor_copy(out=b_uhis.bf16(pb * 2048, 2048), in_=b_uf.f32(0, 2048)),
                    v.tensor_sub(out=b_ulo.bf16(pb * 2048, 2048), in0=b_uf.f32(0, 2048),
                                 in1=b_uhis.bf16(pb * 2048, 2048))],
                    waits=tuple(w_us))
                s_ul = P.dma("gq", [(spill_ulo[dt_, :, :], b_ulo.bf16(pb * 2048, 2048)),
                                    (spill_uhi[dt_, :, :], b_uhis.bf16(pb * 2048, 2048))], waits=(s_us,))
                xz_sp.append(s_ul)
                u_done.append(s_us)
            A.release(b_xnfm)
            s_b1 = xz_sp[-1]

            barrier()
            b_xpw = Buf(big, b_xn.off, 768)
            b_xps = Buf(big, b_xn.off + 768, 96)
            b_bclo = Buf(big, b_xn.off + 1024, 2048)
            b_dtr = A.alloc(2 * 1024)
            b_bcs = A.alloc(2 * 1024)
            dtr = lambda br: b_dtr.bf16(br * 2048, 2048)[0:64, :]
            bcs = lambda br: b_bcs.bf16(br * 2048, 2048)[0:64, :]
            b_upx = A.alloc(2048)
            xp_done = []
            for br in range(2):
                xdt_h = xdtT1_h if br == 0 else xdtT2_h
                xbc_h = xbcT1_h if br == 0 else xbcT2_h
                prev = (u_done[-1], xz_sp[-1]) if br == 0 else (xp_done[-1],)
                for k in range(NDI):
                    sl = P.dma("sp", [(b_xps.f32(0, 64), xdt_h[k * 128:(k + 1) * 128, :]),
                                      (b_xps.f32(64, 32), xbc_h[k * 128:(k + 1) * 128, :])], waits=prev)
                    sc = P.add("dve", lambda v, k=k: v.tensor_copy(out=b_xpw.bf16(k * 96, 96),
                                                                   in_=b_xps.f32(0, 96)), waits=(sl,))
                    prev = (sc,)
                xpw = lambda k: b_xpw.bf16(k * 96, 96)
                mm_steps = []
                for k in range(NDI):
                    w = [prev[0], xz_sp[-1]]
                    if len(mm_steps) >= 2:
                        w.append(mm_steps[-2][0])
                    s_uk = P.dma("aq", [(b_upx.bf16((k % 2) * 2048, 2048), spill_uhi[k, :, :])],
                                 waits=tuple(w))
                    s_mm = P.add("pe", lambda t, k=k: [
                        t.matmul(out=psA[0:64, q4 * 512:(q4 + 1) * 512], lhsT=xpw(k)[:, 0:64],
                                 rhs=b_upx.bf16((k % 2) * 2048 + q4 * 512, 512),
                                 start=(k == 0), stop=(k == NDI - 1))
                        for q4 in range(4)] + [
                        t.matmul(out=psB[0:32, q4 * 512:(q4 + 1) * 512], lhsT=xpw(k)[:, 64:96],
                                 rhs=b_upx.bf16((k % 2) * 2048 + q4 * 512, 512),
                                 start=(k == 0), stop=(k == NDI - 1))
                        for q4 in range(4)], waits=(s_uk,) + ((mm_steps[-1][1],) if mm_steps else ()))
                    mm_steps.append((s_mm, s_mm))
                s_dt = mm_steps[-1][0]
                s_de = P.add("dve", lambda v, br=br: v.tensor_copy(out=dtr(br), in_=psA[0:64, 0:2048]),
                             waits=(s_dt,))
                s_bc = s_dt
                s_be = P.add("dve", lambda v, br=br: [
                    v.tensor_copy(out=bcs(br)[0:32, :], in_=psB[0:32, 0:2048]),
                    v.tensor_sub(out=b_bclo.bf16(0, 2048)[0:32, :], in0=psB[0:32, 0:2048],
                                 in1=bcs(br)[0:32, :])], waits=(s_bc,))
                s_bl0 = P.dma("gq", [(bxfer[:, :], b_bclo.bf16(0, 2048)[0:32, :])], waits=(s_be,))
                s_bl = P.dma("gq", [(b_bcs.bf16(br * 2048, 2048)[32:64, :], bxfer[:, :])], waits=(s_bl0,))
                xp_done.append(s_bl)


            b_dtw = A.alloc(2 * 1024)
            b_dws = Buf(big, b_xn.off + 3072, 2048)
            dtw = lambda br: b_dtw.bf16(br * 2048, 2048)[0:64, :]
            s_d1 = P.dma("sp", [(b_dws.f32(0, 2048)[0:64, :], dtT1_h[:, :])], waits=(xp_done[-1],))
            s_c1 = P.add("dve", lambda v: v.tensor_copy(out=dtw(0), in_=b_dws.f32(0, 2048)[0:64, :]),
                         waits=(s_d1,))
            s_d2 = P.dma("sp", [(b_dws.f32(0, 2048)[0:64, :], dtT2_h[:, :])], waits=(s_c1,))
            s_c2 = P.add("dve", lambda v: v.tensor_copy(out=dtw(1), in_=b_dws.f32(0, 2048)[0:64, :]),
                         waits=(s_d2,))
            s_dtw = s_c2

            barrier()
            b_dl = A.alloc(2048)
            b_du = A.alloc(2048)
            b_spt = A.alloc(2048)
            b_dA = Buf(big, b_xres.off, 4096)
            b_dBu = Buf(big, b_xres.off + 4096, 4096)
            b_h = Buf(big, b_xres.off + 8192, 4096)
            b_ypp = Buf(big, b_xres.off + 12288, 4096)
            b_hC = A.alloc(2 * 2048)
            b_up2 = A.alloc(2048)
            b_szt2 = Buf(big, b_xn.off, 2048)
            b_yft = Buf(big, b_xn.off + 2048, 2048)

            yfin_sp = []
            prev_tail = None
            scan_hist = {}
            dbu_hist = {}
            hc_hist = {}
            ya_hist = []
            for dt_ in range(NDI):
                pbt = dt_ % 2
                w = [s_b1, s_dtw]
                if prev_tail:
                    w.append(prev_tail)
                s_ulo = P.dma("sp", [(b_up2.bf16(0, 2048), spill_uhi[dt_, :, :]),
                                     (b_up2.bf16(2048, 2048), spill_ulo[dt_, :, :])], waits=tuple(w))
                s_szl = P.dma("aq", [(b_szt2.bf16(pbt * 2048, 2048), spill_sz[dt_, :, :])], waits=tuple(w))
                s_yi = P.add("dve", lambda v, dt_=dt_: [
                    v.tensor_scalar(out=b_ypp.f32(0, 2048), in0=b_up2.bf16(0, 2048), scalar1=dD(dt_)[:, 0:1],
                                    scalar2=None, op0=ALU.mult),
                    v.scalar_tensor_tensor(out=b_ypp.f32(0, 2048), in0=b_up2.bf16(2048, 2048),
                                           scalar=dD(dt_)[:, 0:1], in1=b_ypp.f32(0, 2048),
                                           op0=ALU.mult, op1=ALU.add)], waits=(s_ulo, s_cs))
                ycur = 0
                last_ya = s_yi
                for br in range(2):
                    w = [s_dtw, s_yi]
                    if dbu_hist:
                        w.append(max(dbu_hist.values()))
                    if prev_tail:
                        w.append(prev_tail)
                    s_dp = P.add("pe", lambda t, dt_=dt_, br=br: [
                        t.matmul(out=psA[:, q4 * 512:(q4 + 1) * 512],
                                 lhsT=dtw(br)[:, dt_ * 128:(dt_ + 1) * 128],
                                 rhs=dtr(br)[:, q4 * 512:(q4 + 1) * 512], start=True, stop=True)
                        for q4 in range(4)], waits=tuple(w))
                    s_sp2 = P.add("act", lambda s, dt_=dt_, br=br: [
                        s.activation(out=b_spt.f32(0, 2048), in_=psA[:, 0:2048], func=AF.Abs,
                                     bias=dtb(dt_, br)[:, 0:1]),
                        s.activation(out=b_spt.f32(0, 2048), in_=b_spt.f32(0, 2048), func=AF.Exp, scale=-1.0),
                        s.activation(out=b_spt.f32(0, 2048), in_=b_spt.f32(0, 2048), func=AF.Ln, bias=b_col.f32(31, 1)[:, 0:1]),
                        s.activation(out=b_dl.f32(0, 2048), in_=psA[:, 0:2048], func=AF.Relu,
                                     bias=dtb(dt_, br)[:, 0:1])], waits=(s_dp,))
                    s_df = P.add("dve", lambda v, dt_=dt_: [
                        v.tensor_add(out=b_dl.f32(0, 2048), in0=b_dl.f32(0, 2048), in1=b_spt.f32(0, 2048)),
                        v.scalar_tensor_tensor(out=b_du.f32(0, 2048), in0=b_up2.bf16(0, 2048), scalar=1.0,
                                               in1=b_dl.f32(0, 2048), op0=ALU.mult, op1=ALU.mult),
                        v.scalar_tensor_tensor(out=b_spt.f32(0, 2048), in0=b_up2.bf16(2048, 2048), scalar=1.0,
                                               in1=b_dl.f32(0, 2048), op0=ALU.mult, op1=ALU.mult),
                        v.tensor_add(out=b_du.f32(0, 2048), in0=b_du.f32(0, 2048), in1=b_spt.f32(0, 2048))],
                        waits=(s_sp2,))
                    for n in range(DS):
                        nb = n % 2
                        sgn = 1.0 if br == 0 else -1.0
                        prev_merged = None
                        for lh in range(2):
                            p2 = (2 * n + lh) % 2
                            hoff = nb * 2048 + lh * 1024
                            w = [s_df]
                            if (nb, lh) in scan_hist:
                                w.append(scan_hist[(nb, lh)])
                            s_dA = P.add("act", lambda s, dt_=dt_, br=br, n=n, hoff=hoff, lh=lh: s.activation(
                                out=b_dA.f32(hoff, 1024), in_=b_dl.f32(lh * 1024, 1024), func=AF.Exp,
                                scale=Aneg(dt_, br, n)[:, 0:1]), waits=tuple(w))
                            wb = [s_sc, s_df]
                            if (p2, "mg") in dbu_hist:
                                wb.append(dbu_hist[(p2, "mg")])
                            s_bb = P.add("pe", lambda t, br=br, n=n, lh=lh, p2=p2: [
                                t.matmul(out=psA[:, p2 * 1024 + q4 * 512:p2 * 1024 + (q4 + 1) * 512],
                                         lhsT=selB(n),
                                         rhs=bcs(br)[:, lh * 1024 + q4 * 512:lh * 1024 + (q4 + 1) * 512],
                                         start=True, stop=True)
                                for q4 in range(2)] + [
                                t.matmul(out=psB[:, p2 * 1024 + q4 * 512:p2 * 1024 + (q4 + 1) * 512],
                                         lhsT=selC(n),
                                         rhs=bcs(br)[:, lh * 1024 + q4 * 512:lh * 1024 + (q4 + 1) * 512],
                                         start=True, stop=True)
                                for q4 in range(2)], waits=tuple(wb))
                            w = [s_bb, s_dA]
                            if (nb, lh) in scan_hist:
                                w.append(scan_hist[(nb, lh)])
                            if len(ya_hist) >= 4:
                                w.append(ya_hist[-4])
                            hard = ()
                            if lh == 1 and prev_merged is not None:
                                hard = (prev_merged,)
                            init = 0.0 if lh == 0 else b_h.f32(nb * 2048 + 1023, 1)[:, 0:1]
                            s_hc = P.add("dve", lambda v, nb=nb, sgn=sgn, hoff=hoff, p2=p2, init=init: [
                                v.tensor_tensor(out=b_dBu.f32(hoff, 1024), in0=b_du.f32((hoff % 2048), 1024),
                                                in1=psA[:, p2 * 1024:p2 * 1024 + 1024], op=ALU.mult),
                                v.tensor_tensor_scan(out=b_h.f32(hoff, 1024),
                                                     data0=b_dA.f32(hoff, 1024),
                                                     data1=b_dBu.f32(hoff, 1024), initial=init,
                                                     op0=ALU.mult, op1=ALU.add),
                                v.scalar_tensor_tensor(out=b_hC.f32(hoff, 1024),
                                                       in0=b_h.f32(hoff, 1024), scalar=sgn,
                                                       in1=psB[:, p2 * 1024:p2 * 1024 + 1024],
                                                       op0=ALU.mult, op1=ALU.mult)],
                                waits=tuple(w), hard=hard)
                            prev_merged = s_hc
                            scan_hist[(nb, lh)] = s_hc
                            dbu_hist[(p2, "mg")] = s_hc
                            s_ya = P.add("pool", lambda g, hoff=hoff, yc=ycur, lh=lh: g.tensor_add(
                                out=b_ypp.f32((1 - yc) * 2048 + lh * 1024, 1024),
                                in0=b_ypp.f32(yc * 2048 + lh * 1024, 1024),
                                in1=b_hC.f32(hoff, 1024)), waits=(s_hc, last_ya))
                            last_ya = s_ya
                            ya_hist.append(s_ya)
                        ycur = 1 - ycur
                s_gate = P.add("dve", lambda v, yc=ycur, pbt=pbt: v.tensor_tensor(
                    out=b_yft.bf16(pbt * 2048, 2048), in0=b_ypp.f32(yc * 2048, 2048),
                    in1=b_szt2.bf16(pbt * 2048, 2048), op=ALU.mult), waits=(last_ya, s_szl))
                s_ysp = P.dma("gq", [(spill_yfin[dt_, :, :], b_yft.bf16(pbt * 2048, 2048))], waits=(s_gate,))
                prev_tail = s_ysp
                yfin_sp.append(s_ysp)

            A.release(b_dl, b_du, b_spt, b_hC, b_up2, b_upx, b_uhis,
                      b_dtr, b_bcs, b_dtw, b_selbf, b_cs, b_bs)

            barrier()
            b_opw = A.alloc(NDI * 512)
            b_ops = A.alloc(1024)
            b_ysb = A.alloc(2 * 1024)
            b_xrl = A.alloc(2 * 1024)
            opw = lambda k: b_opw.bf16(k * 1024, 1024)
            prev = (yfin_sp[-1],)
            for k in range(NDI):
                prev = (load_cast(b_opw, k * 1024, outprojT_h[k * 128:(k + 1) * 128, :], b_ops, 1024, prev),)
            s_op = prev[0]
            res = []
            for tt in range(NT):
                pb = tt % 2
                w = [s_op]
                if tt >= 2:
                    w.append(res[tt - 2])
                s_yl = P.dma("sp", [(b_ysb.bf16(pb * 2048 + k * 128, 128), spill_yfin[k, :, tt * 128:(tt + 1) * 128])
                                    for k in range(NDI)], waits=tuple(w))
                s_xl = P.dma("aq", [(b_xrl.f32(pb * 1024, 1024), spill_x[tt, :, :])], waits=tuple(w))
                pso = psA[:, pb * 1024:pb * 1024 + 1024]
                s_om = P.add("pe", lambda t, pb=pb, pso=pso: [
                    t.matmul(out=pso[:, hf * 512:(hf + 1) * 512], lhsT=b_ysb.bf16(pb * 2048 + k * 128, 128),
                             rhs=opw(k)[:, hf * 512:(hf + 1) * 512], start=(k == 0), stop=(k == NDI - 1))
                    for k in range(NDI) for hf in range(2)], waits=(s_yl,))
                s_rs = P.add("dve", lambda v, tt=tt, pb=pb, pso=pso: v.tensor_tensor(
                    out=xres(tt), in0=pso[:, 0:1024], in1=b_xrl.f32(pb * 1024, 1024), op=ALU.add),
                    waits=(s_om, s_xl))
                res.append(s_rs)
            A.release(b_opw, b_ops, b_ysb, b_xrl)
            return res

        import sys as _sys
        def mark(nm):
            print(f"[mark] {nm}: step {len(P.steps)}", file=_sys.stderr)

        last = emit_attn_a(emit_ln((s_base,)))
        if N_SUB >= 2:
            barrier()
            last = emit_ffn(emit_ln((last[-1],)), which=0)
        if N_SUB >= 3:
            barrier()
            last = emit_attn_b(emit_ln((last[-1],)))
        if N_SUB >= 4:
            barrier()
            last = emit_ffn(emit_ln((last[-1],)), which=1)

        mark("store")
        s_st = P.dma("gq", [(out_h[tt * 128:(tt + 1) * 128, :], xres(tt)) for tt in range(NT)],
                     waits=(last[-1],))
        P.add("pool", lambda g: g.engine_nop(), waits=(s_st,))

        P.resolve()

        @block.sync
        def _(e):
            P.emit_for(e, {"sp"}, sems)

        @block.tensor
        def _(e):
            P.emit_for(e, {"pe"}, sems)

        @block.vector
        def _(e):
            P.emit_for(e, {"dve"}, sems)

        @block.scalar
        def _(e):
            P.emit_for(e, {"act", "aq"}, sems)

        @block.gpsimd
        def _(e):
            P.emit_for(e, {"pool", "gq"}, sems)

    return nc


def _prep_inputs(inputs):
    g = lambda k: np.asarray(inputs[k], dtype=np.float32)
    b_attn = g("L0_attn_a_b_attn")
    common = {
        "W_attn": np.ascontiguousarray(g("L0_attn_a_W_attn")),
        "battn_qk": np.ascontiguousarray(np.concatenate(
            [b_attn[0:D].reshape(ND, 128).T, b_attn[D:2 * D].reshape(ND, 128).T], axis=1)),
        "bv_row": np.ascontiguousarray(b_attn[2 * D:3 * D].reshape(1, D)),
        "W_proj": np.ascontiguousarray(g("L0_attn_a_W_proj")),
        "bproj_row": np.ascontiguousarray(g("L0_attn_a_b_proj").reshape(1, D)),
        "negmask": np.where(np.arange(128)[None, :] <= np.arange(128)[:, None], 0.0, -1e9).astype(np.float32),
        "trimask": (np.arange(128)[:, None] <= np.arange(128)[None, :]).astype(np.float32),
        "eye": np.eye(128, dtype=np.float32),
        "WrT": np.ascontiguousarray(g("L0_ffn_W_r").T),
        "WkT": np.ascontiguousarray(g("L0_ffn_W_k").T),
        "WvT": np.ascontiguousarray(g("L0_ffn_W_v").T),
        "mur_col": np.ascontiguousarray(g("L0_ffn_mu_r").reshape(ND, 128).T),
        "muk_col": np.ascontiguousarray(g("L0_ffn_mu_k").reshape(ND, 128).T),
        "inprojT": np.ascontiguousarray(g("L0_attn_b_in_proj_w").T),
        "convw": np.ascontiguousarray(g("L0_attn_b_conv_w").reshape(NDI, 128, DCONV)),
        "convb": np.ascontiguousarray(g("L0_attn_b_conv_b").reshape(NDI, 128, 1)),
        "Alog1": np.ascontiguousarray(g("L0_attn_b_A1_log").reshape(NDI, 128, DS)),
        "Alog2": np.ascontiguousarray(g("L0_attn_b_A2_log").reshape(NDI, 128, DS)),
        "D1": np.ascontiguousarray(g("L0_attn_b_D1").reshape(NDI, 128, 1)),
        "D2": np.ascontiguousarray(g("L0_attn_b_D2").reshape(NDI, 128, 1)),
        "xdtT1": np.ascontiguousarray(g("L0_attn_b_x1_proj_w")[0:DTR].T),
        "xbcT1": np.ascontiguousarray(g("L0_attn_b_x1_proj_w")[DTR:].T),
        "xdtT2": np.ascontiguousarray(g("L0_attn_b_x2_proj_w")[0:DTR].T),
        "xbcT2": np.ascontiguousarray(g("L0_attn_b_x2_proj_w")[DTR:].T),
        "dtT1": np.ascontiguousarray(g("L0_attn_b_dt1_proj_w").T),
        "dtT2": np.ascontiguousarray(g("L0_attn_b_dt2_proj_w").T),
        "dtb1": np.ascontiguousarray(g("L0_attn_b_dt1_proj_b").reshape(NDI, 128, 1)),
        "dtb2": np.ascontiguousarray(g("L0_attn_b_dt2_proj_b").reshape(NDI, 128, 1)),
        "outprojT": np.ascontiguousarray(g("L0_attn_b_out_proj_w").T),
    }
    sel = np.zeros((64, 2 * DS * 128), np.float32)
    for n in range(DS):
        sel[n, n * 128:(n + 1) * 128] = 1.0
        sel[32 + n, n * 128:(n + 1) * 128] = 1.0
        sel[16 + n, 2048 + n * 128:2048 + (n + 1) * 128] = 1.0
        sel[48 + n, 2048 + n * 128:2048 + (n + 1) * 128] = 1.0
    common["sel"] = sel
    x_full = g("L0_attn_a_x")
    return [dict(common, x=np.ascontiguousarray(x_full[c])) for c in range(B)]


def kernel(**inputs):
    maps = _prep_inputs(inputs)
    nc = build_program()
    res = run_bass_kernel_spmd(nc, maps, core_ids=list(range(B)))
    try:
        kernel._last_dbg = [np.asarray(r["dbg"]) for r in res.results]
    except Exception:
        kernel._last_dbg = None
    return np.stack([np.asarray(r["out"]) for r in res.results], axis=0).astype(np.float32)
